# version 07 (final)
# baseline (speedup 1.0000x reference)
import numpy as np
import concourse.bass as bass
import concourse.mybir as mybir
from concourse.bass_utils import run_bass_kernel_spmd

F32 = mybir.dt.float32
BF16 = mybir.dt.bfloat16
U32 = mybir.dt.uint32
AF = mybir.ActivationFunctionType
ALU = mybir.AluOpType
AX = mybir.AxisListType

N_CORES = 8
T = 2048
D = 1024
NT = T // 128
EPS_LN = 1e-5

ENG_SEM = {"tensor": "t", "vector": "v", "scalar": "s", "gpsimd": "g", "sync": "y"}
DMA_QUEUES = {"gpsimd": "gd", "sync": "sd", "scalar": "cd"}
DMA_ROT = 4
DMA_SEMS = [q + str(i) for q in ("gd", "sd", "cd") for i in range(DMA_ROT)]
ALL_SEMS = ["t", "v", "s", "g", "y"] + DMA_SEMS


class PB:

    def __init__(self, nc, dbg_n=0):
        self.nc = nc
        self.dbg = None
        self.dbg_off = 0
        self.dbg_n = dbg_n
        if dbg_n:
            self.dbg = nc.dram_tensor("dbg", [128, dbg_n], F32,
                                      kind="ExternalOutput").ap()
        self.thunks = {e: [] for e in ENG_SEM}
        self.cnt = {s: 0 for s in ALL_SEMS}
        self.dma_n = {q: 0 for q in DMA_QUEUES.values()}
        self.writes = {}
        self.readers = {}
        self.waited = {e: {s: 0 for s in ALL_SEMS} for e in ENG_SEM}
        self.semh = {}

    def _wait(self, eng, sem, val):
        if self.waited[eng][sem] >= val:
            return
        self.waited[eng][sem] = val
        self.thunks[eng].append(
            lambda E, sem=sem, val=val: E.wait_ge(self.semh[sem], val))

    def emit(self, eng, fn, reads=(), writes=(), dma=False, inc=True):
        deps = {}

        def add(tok):
            if tok is None:
                return
            s_, v_ = tok
            if deps.get(s_, 0) < v_:
                deps[s_] = v_
        for k in reads:
            add(self.writes.get(k))
        for k in writes:
            add(self.writes.get(k))
            for t_ in self.readers.get(k, ()):
                add(t_)
        for s_, v_ in deps.items():
            self._wait(eng, s_, v_)
        if dma:
            q = DMA_QUEUES[eng]
            sem = q + str(self.dma_n[q] % DMA_ROT)
            self.dma_n[q] += 1
        else:
            sem = ENG_SEM[eng]
        tok = None
        if inc:
            amt = 16 if dma else 1
            self.cnt[sem] += amt
            val = self.cnt[sem]
            self.thunks[eng].append(
                lambda E, fn=fn, sem=sem, val=val, amt=amt:
                    fn(E).then_inc(self.semh[sem], amt))
            tok = (sem, val)
            for k in writes:
                self.writes[k] = tok
                self.readers[k] = []
            for k in reads:
                self.readers.setdefault(k, []).append(tok)
        else:
            self.thunks[eng].append(lambda E, fn=fn: fn(E))
        return tok

    def dump(self, ap, keys, n, cast=False):
        if self.dbg is None:
            return
        off = self.dbg_off
        self.dbg_off += n
        assert self.dbg_off <= self.dbg_n
        if cast:
            self.emit("gpsimd", lambda E: E.dma_start(self.dbg[:, off:off + n], ap),
                      reads=keys, dma=True)
        else:
            self.emit("sync", lambda E: E.dma_start(self.dbg[:, off:off + n], ap),
                      reads=keys, dma=True)

    def barrier(self):
        snapshot = dict(self.cnt)
        for eng in ENG_SEM:
            for s_, v_ in snapshot.items():
                if v_ > 0:
                    self._wait(eng, s_, v_)
        self.writes = {}
        self.readers = {}

    def mm_group(self, out_ap, pieces, reads, bankkey):
        n = len(pieces)
        tok = None
        for i, (lh, rh) in enumerate(pieces):
            first, last = i == 0, i == n - 1
            tok = self.emit(
                "tensor",
                lambda E, lh=lh, rh=rh, first=first, last=last:
                    E.matmul(out_ap, lh, rh, start=first, stop=last),
                reads=reads if (first or last) else (),
                writes=(bankkey,) if (first or last) else (),
                inc=last)
        return tok

    def build(self):
        import contextlib
        es = contextlib.ExitStack()
        for s_ in ALL_SEMS:
            self.semh[s_] = es.enter_context(self.nc.semaphore("sem_" + s_))
        block = es.enter_context(self.nc.Block())
        for eng in ("sync", "gpsimd", "tensor", "vector", "scalar"):
            ths = self.thunks[eng]
            getattr(block, eng)(lambda E, ths=ths: [t_(E) for t_ in ths])
        es.close()



def _host_consts():
    return {"c_ident": np.eye(128, dtype=np.float32)}


def host_prep_d(inp):
    m = {}
    m["d_w1T"] = np.ascontiguousarray(inp["L1_ffn_w1_w"].T).astype(np.float32)
    m["d_w2T"] = np.ascontiguousarray(inp["L1_ffn_w2_w"].T).astype(np.float32)
    m["d_h1T"] = np.ascontiguousarray(inp["L1_ffn_h1_w"].T).astype(np.float32)
    m["d_h2T"] = np.ascontiguousarray(inp["L1_ffn_h2_w"].T).astype(np.float32)
    m["d_w3T"] = np.ascontiguousarray(inp["L1_ffn_w3_w"].T).astype(np.float32)
    m["d_s1"] = np.ascontiguousarray(np.asarray(inp["L1_ffn_w1_s"]).reshape(32, 128).T).astype(np.float32)
    m["d_s2"] = np.ascontiguousarray(np.asarray(inp["L1_ffn_w2_s"]).reshape(32, 128).T).astype(np.float32)
    m["d_sh1"] = np.ascontiguousarray(np.asarray(inp["L1_ffn_h1_s"]).reshape(32, 128).T).astype(np.float32)
    m["d_sh2"] = np.ascontiguousarray(np.asarray(inp["L1_ffn_h2_s"]).reshape(32, 128).T).astype(np.float32)
    m["d_sw3"] = np.ascontiguousarray(np.asarray(inp["L1_ffn_w3_s"]).reshape(8, 128).T).astype(np.float32)
    return m


def declare_dram_d(nc):
    a = {}
    a["d_w1T"] = nc.dram_tensor("d_w1T", [1024, 4096], F32, kind="ExternalInput").ap()
    a["d_w2T"] = nc.dram_tensor("d_w2T", [1024, 4096], F32, kind="ExternalInput").ap()
    a["d_h1T"] = nc.dram_tensor("d_h1T", [4096, 4096], F32, kind="ExternalInput").ap()
    a["d_h2T"] = nc.dram_tensor("d_h2T", [4096, 4096], F32, kind="ExternalInput").ap()
    a["d_w3T"] = nc.dram_tensor("d_w3T", [4096, 1024], F32, kind="ExternalInput").ap()
    for k in ("d_s1", "d_s2", "d_sh1", "d_sh2"):
        a[k] = nc.dram_tensor(k, [128, 32], F32, kind="ExternalInput").ap()
    a["d_sw3"] = nc.dram_tensor("d_sw3", [128, 8], F32, kind="ExternalInput").ap()
    return a



ARB_N = 80 * 1024
ARF_N = 7680


class Chip:
    def __init__(self, nc):
        self.nc = nc
        self.psf = [nc.alloc_psum_tensor(f"psf{i}", [128, 512], F32).ap()
                    for i in range(6)]
        self.psb = [nc.alloc_psum_tensor(f"psb{i}", [128, 1024], BF16).ap()
                    for i in range(2)]
        self.arb = nc.alloc_sbuf_tensor("arb", [128, ARB_N], BF16).ap()
        self.arf = nc.alloc_sbuf_tensor("arf", [128, ARF_N], F32).ap()
        self.outT = nc.alloc_sbuf_tensor("outT", [128, 4096], F32).ap()
        self.identf = nc.alloc_sbuf_tensor("identf", [128, 128], F32).ap()
        self.identb = nc.alloc_sbuf_tensor("identb", [128, 128], BF16).ap()
        self.onesf = nc.alloc_sbuf_tensor("onesf", [1, 128], F32).ap()
        self.rbrow = nc.alloc_sbuf_tensor("rbrow", [1, 8], F32).ap()
        self.idxi = nc.alloc_sbuf_tensor("idxi", [128, 16], mybir.dt.int32).ap()
        self.idxg = [nc.alloc_sbuf_tensor(f"idxg{i}", [128, 1], mybir.dt.int32).ap()
                     for i in range(4)]

    def eps_ap(self):
        return self.ff(EPS_O, 1)

    def load_consts(self, P, aps):
        P.emit("vector", lambda E: E.memset(self.ff(EPS_O, 1), EPS_LN),
               writes=("c_eps",))
        P.emit("vector", lambda E: E.memset(self.onesf[:], 1.0),
               writes=("onesf",))
        P.emit("gpsimd", lambda E: E.dma_start(self.identf, aps["c_ident"]),
               writes=("identf",), dma=True)
        P.emit("gpsimd", lambda E: E.dma_start(self.identb, aps["c_ident"]),
               writes=("identb",), dma=True)

    def bf(self, off, n):
        assert off >= 0 and off + n <= ARB_N, (off, n)
        return self.arb[:, off:off + n]

    def ff(self, off, n):
        assert off >= 0 and off + n <= ARF_N, (off, n)
        return self.arf[:, off:off + n]



def ln_stats(P, xtile_ap, xkeys, mean_ap, rstd_ap, skey, sc, sck, eps_ap=None):
    P.emit("scalar",
           lambda E: E.activation(sc[:, 0:1024], xtile_ap, AF.Square,
                                  accum_out=sc[:, 1024:1025]),
           reads=xkeys, writes=(sck,))
    P.emit("vector",
           lambda E: E.reduce_sum(sc[:, 1025:1026], xtile_ap, axis=AX.X),
           reads=xkeys, writes=(sck + "_sum",))
    P.emit("vector",
           lambda E: E.tensor_scalar(mean_ap, sc[:, 1025:1026], 1.0 / D, None,
                                     op0=ALU.mult),
           reads=(sck + "_sum",), writes=(skey + "_m",))
    P.emit("vector",
           lambda E: E.tensor_mul(sc[:, 1026:1027], mean_ap, mean_ap),
           reads=(skey + "_m",), writes=(sck + "_m2",))
    P.emit("vector",
           lambda E: E.scalar_tensor_tensor(sc[:, 1027:1028], sc[:, 1024:1025],
                                            1.0 / D, sc[:, 1026:1027],
                                            op0=ALU.mult, op1=ALU.subtract),
           reads=(sck, sck + "_m2"), writes=(sck + "_var",))
    P.emit("scalar",
           lambda E: E.activation(sc[:, 1028:1029], sc[:, 1027:1028], AF.Ln,
                                  bias=eps_ap),
           reads=(sck + "_var", "c_eps"), writes=(sck + "_lnv",))
    P.emit("scalar",
           lambda E: E.activation(rstd_ap, sc[:, 1028:1029], AF.Exp, scale=-0.5),
           reads=(sck + "_lnv",), writes=(skey + "_r",))


XT_O, AT_O, BT_O, CT_O = 0, 8192, 40960, 8192
W1S_O, XB_O = 73728, 77824
SC_O, XN_O, SIL_O, LNO_O = 0, 2080, 4128, 5152
S1_O, S2_O, SH1_O, SH2_O, SW3_O, MEAN_O = 7200, 7232, 7264, 7296, 7328, 7336
EPS_O = 7679


def ln_x_to_xT(P, ch, x_in_t, tt0, ntiles, tag, xt_off=XT_O):
    keys = []
    for j in range(ntiles):
        tt = tt0 + j
        xbuf = ch.ff(XN_O + (j % 2) * 1024, 1024)
        xk = f"{tag}x{j % 2}"
        P.emit("sync", lambda E, tt=tt, xbuf=xbuf: E.dma_start(xbuf, x_in_t[tt]),
               writes=(xk,), dma=True)
        sc = ch.ff(SC_O + (j % 2) * 1040, 1040)
        sck = f"{tag}sc{j % 2}"
        mean = ch.ff(MEAN_O + (j % 2), 1)
        rstd = ch.ff(MEAN_O + 2 + (j % 2), 1)
        skey = f"{tag}st{j % 2}"
        ln_stats(P, xbuf, (xk,), mean, rstd, skey, sc, sck, ch.eps_ap())
        xhb = ch.bf(XB_O + (j % 2) * 1024, 1024)
        xhbk = f"{tag}xhb{j % 2}"
        P.emit("vector",
               lambda E, xhb=xhb, xbuf=xbuf, mean=mean, rstd=rstd:
                   E.tensor_scalar(xhb, xbuf, mean, rstd,
                                   op0=ALU.subtract, op1=ALU.mult),
               reads=(xk, skey + "_m", skey + "_r"), writes=(xhbk,))
        for g in range(2):
            pb = ch.psb[g]
            pbk = f"psb{g}"
            for c4 in range(4):
                c = g * 4 + c4
                P.emit("tensor",
                       lambda E, c=c, c4=c4, pb=pb, j=j:
                           E.transpose(pb[:, c4 * 128:(c4 + 1) * 128],
                                       ch.bf(XB_O + (j % 2) * 1024 + c * 128, 128),
                                       ch.identb),
                       reads=(xhbk, "identb") if c4 in (0, 3) else (),
                       writes=(pbk,) if c4 in (0, 3) else (),
                       inc=(c4 == 3))
            xtv = ch.bf(xt_off, 8192).rearrange("p (c t) -> p c t", t=1024)
            pbv = pb.rearrange("p (c t) -> p c t", t=128)
            kk = f"{tag}xT_{j}_{g}"
            P.emit("vector",
                   lambda E, g=g, j=j, xtv=xtv, pbv=pbv:
                       E.tensor_copy(xtv[:, g * 4:(g + 1) * 4,
                                         j * 128:(j + 1) * 128], pbv[:, 0:4, :]),
                   reads=(pbk,), writes=(kk,))
            keys.append(kk)
    return tuple(keys)


def build_stage_d(P, ch, a, x_in, x_out, tag="D", skip_first=False):
    if not skip_first:
        for off, name in ((S1_O, "d_s1"), (S2_O, "d_s2"), (SH1_O, "d_sh1"),
                          (SH2_O, "d_sh2")):
            P.emit("gpsimd",
                   lambda E, off=off, name=name:
                       E.dma_start(ch.ff(off, 32), a[name]),
                   writes=(tag + name,), dma=True)
        P.emit("gpsimd", lambda E: E.dma_start(ch.ff(SW3_O, 8), a["d_sw3"]),
               writes=(tag + "d_sw3",), dma=True)

    x_in_t = x_in.rearrange("(n p) d -> n p d", p=128)
    x_out_t = x_out.rearrange("(n p) d -> n p d", p=128)
    w1v = a["d_w1T"].rearrange("(c p) o -> p c o", p=128)
    w2v = a["d_w2T"].rearrange("(c p) o -> p c o", p=128)
    h1v = a["d_h1T"].rearrange("(c p) o -> p c o", p=128)
    h2v = a["d_h2T"].rearrange("(c p) o -> p c o", p=128)
    w3v = a["d_w3T"].rearrange("(c p) o -> p c o", p=128)

    for th in range(2):
        tt0 = th * 8
        xT_keys = ln_x_to_xT(P, ch, x_in_t, tt0, 8, tag)

        for ot in range(32):
            wb = ch.bf(W1S_O + (ot % 2) * 2048, 1024)
            wb2 = ch.bf(W1S_O + (ot % 2) * 2048 + 1024, 1024)
            wk, wk2 = f"{tag}w1b{ot % 2}", f"{tag}w2b{ot % 2}"
            P.emit("gpsimd",
                   lambda E, wb=wb, ot=ot:
                       E.dma_start(wb, w1v[:, :, ot * 128:(ot + 1) * 128]),
                   writes=(wk,), dma=True)
            P.emit("gpsimd",
                   lambda E, wb2=wb2, ot=ot:
                       E.dma_start(wb2, w2v[:, :, ot * 128:(ot + 1) * 128]),
                   writes=(wk2,), dma=True)
            for half in range(2):
                b1, b2 = ch.psf[2 * half], ch.psf[2 * half + 1]
                k1, k2 = f"psf{2 * half}", f"psf{2 * half + 1}"
                pieces1 = [(wb[:, c * 128:(c + 1) * 128],
                            ch.bf(XT_O + c * 1024 + half * 512, 512))
                           for c in range(8)]
                pieces2 = [(wb2[:, c * 128:(c + 1) * 128],
                            ch.bf(XT_O + c * 1024 + half * 512, 512))
                           for c in range(8)]
                P.mm_group(b1[:, 0:512], pieces1, (wk,) + xT_keys, k1)
                P.mm_group(b2[:, 0:512], pieces2, (wk2,) + xT_keys, k2)
                sil = ch.ff(SIL_O + half * 512, 512)
                silk = f"{tag}sil{half}"
                P.emit("scalar",
                       lambda E, b1=b1, sil=sil, ot=ot:
                           E.activation(sil, b1[:, 0:512], AF.Silu,
                                        scale=ch.ff(S1_O + ot, 1)),
                       reads=(k1, tag + "d_s1"), writes=(silk,))
                atv = ch.bf(AT_O + ot * 1024 + half * 512, 512)
                P.emit("vector",
                       lambda E, b2=b2, atv=atv, sil=sil, ot=ot:
                           E.scalar_tensor_tensor(atv, b2[:, 0:512],
                                                  ch.ff(S2_O + ot, 1), sil,
                                                  op0=ALU.mult, op1=ALU.mult),
                       reads=(k2, tag + "d_s2", silk),
                       writes=(f"{tag}aT{ot}_{half}",))
        aT_keys = tuple(f"{tag}aT{o}_{h}" for o in range(32) for h in range(2))
        bT_keys = tuple(f"{tag}bT{o}_{h}" for o in range(32) for h in range(2))
        cT_keys = tuple(f"{tag}cT{o}_{h}" for o in range(32) for h in range(2))

        for ph in range(2):
            wv = h1v if ph == 0 else h2v
            sOFF = SH1_O if ph == 0 else SH2_O
            sKEY = tag + ("d_sh1" if ph == 0 else "d_sh2")
            src_off = AT_O if ph == 0 else BT_O
            src_keys = aT_keys if ph == 0 else bT_keys
            dst_off = BT_O if ph == 0 else CT_O
            dkey = f"{tag}bT" if ph == 0 else f"{tag}cT"
            for ot in range(32):
                hb = ch.bf(XT_O + (ot % 2) * 4096, 4096)
                hk = f"{tag}hb{ot % 2}"
                P.emit("gpsimd",
                       lambda E, hb=hb, ot=ot, wv=wv:
                           E.dma_start(hb, wv[:, :, ot * 128:(ot + 1) * 128]),
                       writes=(hk,) + (xT_keys if (ph == 0 and ot < 2) else ()),
                       dma=True)
                for half in range(2):
                    bank = ch.psf[2 * (ot % 2) + half]
                    bk = f"psf{2 * (ot % 2) + half}"
                    pieces = [(hb[:, c * 128:(c + 1) * 128],
                               ch.bf(src_off + c * 1024 + half * 512, 512))
                              for c in range(32)]
                    P.mm_group(bank[:, 0:512], pieces, (hk,) + src_keys, bk)
                    dst = ch.bf(dst_off + ot * 1024 + half * 512, 512)
                    P.emit("scalar",
                           lambda E, bank=bank, dst=dst, ot=ot, sOFF=sOFF:
                               E.activation(dst, bank[:, 0:512], AF.Gelu,
                                            scale=ch.ff(sOFF + ot, 1)),
                           reads=(bk, sKEY), writes=(f"{dkey}{ot}_{half}",))

        for tq in range(2):
            oT_keys = []
            for dt in range(8):
                w3b = ch.bf(XT_O + (dt % 2) * 4096, 4096)
                w3k = f"{tag}hb{dt % 2}"
                P.emit("gpsimd",
                       lambda E, w3b=w3b, dt=dt:
                           E.dma_start(w3b, w3v[:, :, dt * 128:(dt + 1) * 128]),
                       writes=(w3k,), dma=True)
                bank = ch.psf[dt % 4]
                bk = f"psf{dt % 4}"
                pieces = [(w3b[:, c * 128:(c + 1) * 128],
                           ch.bf(CT_O + c * 1024 + tq * 512, 512))
                          for c in range(32)]
                P.mm_group(bank[:, 0:512], pieces, (w3k,) + cT_keys, bk)
                kk = f"{tag}oT{dt}"
                P.emit("vector",
                       lambda E, bank=bank, dt=dt:
                           E.tensor_scalar(ch.outT[:, dt * 512:(dt + 1) * 512],
                                           bank[:, 0:512],
                                           ch.ff(SW3_O + dt, 1), None,
                                           op0=ALU.mult),
                       reads=(bk, tag + "d_sw3"), writes=(kk,))
                oT_keys.append(kk)
            oT_keys = tuple(oT_keys)
            for j in range(4):
                tt = tt0 + tq * 4 + j
                onat = ch.ff(XN_O + (j % 2) * 1024, 1024)
                onk = f"{tag}on{j % 2}"
                for g in range(2):
                    bank = ch.psf[4 + g]
                    bk = f"psf{4 + g}"
                    for c4 in range(4):
                        d_ = g * 4 + c4
                        P.emit("tensor",
                               lambda E, d_=d_, c4=c4, bank=bank, j=j:
                                   E.transpose(bank[:, c4 * 128:(c4 + 1) * 128],
                                               ch.outT[:, d_ * 512 + j * 128:
                                                       d_ * 512 + j * 128 + 128],
                                               ch.identf),
                               reads=(oT_keys + ("identf",)) if c4 in (0, 3) else (),
                               writes=(bk,) if c4 in (0, 3) else (),
                               inc=(c4 == 3))
                    P.emit("vector",
                           lambda E, bank=bank, onat=onat, g=g:
                               E.tensor_copy(onat[:, g * 512:(g + 1) * 512],
                                             bank[:, 0:512]),
                           reads=(bk,), writes=(onk,))
                sc = ch.ff(SC_O + (j % 2) * 1040, 1040)
                sck = f"{tag}osc{j % 2}"
                mean = ch.ff(MEAN_O + 4 + (j % 2), 1)
                rstd = ch.ff(MEAN_O + 6 + (j % 2), 1)
                skey = f"{tag}ost{j % 2}"
                ln_stats(P, onat, (onk,), mean, rstd, skey, sc, sck, ch.eps_ap())
                lnout = ch.ff(LNO_O + (j % 2) * 1024, 1024)
                lnk = f"{tag}lno{j % 2}"
                P.emit("vector",
                       lambda E, onat=onat, mean=mean, rstd=rstd, lnout=lnout:
                           E.tensor_scalar(lnout, onat, mean, rstd,
                                           op0=ALU.subtract, op1=ALU.mult),
                       reads=(onk, skey + "_m", skey + "_r"), writes=(lnk,))
                xbuf = ch.ff(SIL_O, 1024)
                xk = f"{tag}xr"
                P.emit("sync",
                       lambda E, tt=tt, xbuf=xbuf: E.dma_start(xbuf, x_in_t[tt]),
                       reads=(f"xres{tt}",),
                       writes=(xk, f"{tag}sil0", f"{tag}sil1"), dma=True)
                P.emit("vector",
                       lambda E, xbuf=xbuf, lnout=lnout:
                           E.tensor_add(xbuf, xbuf, lnout),
                       reads=(xk, lnk), writes=(xk,))
                P.emit("sync",
                       lambda E, tt=tt, xbuf=xbuf:
                           E.dma_start(x_out_t[tt], xbuf),
                       reads=(xk,), writes=(f"xres{tt}",), dma=True)
        P.barrier()



def build_program_d():
    nc = bass.Bass("TRN2", target_bir_lowering=False, debug=False)
    x_in = nc.dram_tensor("x_in", [T, D], F32, kind="ExternalInput").ap()
    x_out = nc.dram_tensor("x_out", [T, D], F32, kind="ExternalOutput").ap()
    a = declare_dram_d(nc)
    a["c_ident"] = nc.dram_tensor("c_ident", [128, 128], F32,
                                  kind="ExternalInput").ap()
    P = PB(nc)
    ch = Chip(nc)
    ch.load_consts(P, a)
    build_stage_d(P, ch, a, x_in, x_out)
    P.barrier()
    P.build()
    return nc


def run_stage_d(x, inputs, cores=1):
    nc = build_program_d()
    shared = host_prep_d(inputs)
    shared.update(_host_consts())
    in_maps = []
    for c in range(cores):
        m = dict(shared)
        m["x_in"] = np.ascontiguousarray(x[c])
        in_maps.append(m)
    res = run_bass_kernel_spmd(nc, in_maps, core_ids=list(range(cores)))
    return np.stack([r["x_out"] for r in res.results])


EPS_QK = 1e-6
AQB_O, AKT_O, AVS_O = 0, 16384, 34304
AOT_O, AWS_O = 51984, 68368
AXTT_O, AXB_O, AKB_O, AVB_O = 76560, 77584, 79632, 80656
ASC_O, AXN_O, AQF_O, AKF_O, ASQ_O = 0, 2080, 4128, 5152, 6176
ANRM_O, AE_O, ARZ_O, AMN_O = 7200, 7264, 7520, 7524
ATN_O_A = AWS_O
AOA_O = AWS_O + 1024
AMSK_O = AWS_O + 3072
AQT2_O = AWS_O + 3840


def host_prep_a(inp):
    m = {}
    wq, wk, wv = inp["L0_attn_Wq"], inp["L0_attn_Wk"], inp["L0_attn_Wv"]
    m["a_wqkvT"] = np.ascontiguousarray(
        np.concatenate([wq.T, wk.T, wv.T], axis=1)).astype(np.float32)
    m["a_woT"] = np.ascontiguousarray(inp["L0_attn_Wo"].T).astype(np.float32)
    bq = np.concatenate([np.asarray(inp["L0_attn_bq"]),
                         np.asarray(inp["L0_attn_bk"]),
                         np.asarray(inp["L0_attn_bv"])]).astype(np.float32)
    m["a_bqkv"] = np.ascontiguousarray(np.broadcast_to(bq[None, :], (128, 3072)))
    m["a_bo"] = np.ascontiguousarray(np.broadcast_to(
        np.asarray(inp["L0_attn_bo"], np.float32)[None, :], (128, 1024)))
    msk = np.zeros((128, 768), dtype=np.float32)
    for cls in range(3):
        for jj in range(192):
            for ii in range(128):
                diff = jj - 32 - ii
                ok = abs(diff) <= 32 and diff % 4 == 0
                if cls == 0:
                    ok = ok and jj >= 32
                if cls == 2:
                    ok = ok and jj < 160
                if not ok:
                    continue
                if jj < 128:
                    msk[jj, cls * 256 + ii] = 1.0
                else:
                    msk[jj - 128, cls * 256 + 128 + ii] = 1.0
    m["a_masks"] = msk
    return m


def declare_dram_a(nc):
    a = {}
    a["a_wqkvT"] = nc.dram_tensor("a_wqkvT", [1024, 3072], F32, kind="ExternalInput").ap()
    a["a_woT"] = nc.dram_tensor("a_woT", [1024, 1024], F32, kind="ExternalInput").ap()
    a["a_bqkv"] = nc.dram_tensor("a_bqkv", [128, 3072], F32, kind="ExternalInput").ap()
    a["a_bo"] = nc.dram_tensor("a_bo", [128, 1024], F32, kind="ExternalInput").ap()
    a["a_masks"] = nc.dram_tensor("a_masks", [128, 768], F32, kind="ExternalInput").ap()
    return a


def build_stage_a(P, ch, a, x_in, x_out, tag="A"):
    x_in_t = x_in.rearrange("(n p) d -> n p d", p=128)
    x_out_t = x_out.rearrange("(n p) d -> n p d", p=128)
    wqv = a["a_wqkvT"].rearrange("(c p) o -> p c o", p=128)
    eps = ch.eps_ap()
    biasF = ch.outT[:, 0:3072]
    boF = ch.outT[:, 3072:4096]

    P.emit("sync", lambda E: E.dma_start(biasF, a["a_bqkv"]),
           writes=(tag + "biasF",), dma=True)
    P.emit("sync", lambda E: E.dma_start(boF, a["a_bo"]),
           writes=(tag + "boF",), dma=True)
    for c in range(8):
        P.emit("gpsimd",
               lambda E, c=c: E.dma_start(ch.bf(AOT_O + c * 3072, 3072),
                                          wqv[:, c, :]),
               writes=(f"{tag}wqkv{c}",), dma=True)
    wq_keys = tuple(f"{tag}wqkv{c}" for c in range(8))
    P.emit("vector", lambda E: E.memset(ch.bf(AKT_O, 17920), 0.0),
           writes=(tag + "ktz",))
    P.emit("vector", lambda E: E.memset(ch.bf(AVS_O, 17680), 1.0),
           writes=(tag + "vsz",))

    ktv = ch.bf(AKT_O, 17920).rearrange("p (j w) -> p j w", w=2240)
    vsv = ch.bf(AVS_O, 17680).rearrange("p (j h c) -> p j h c", h=16, c=65)

    for tt in range(16):
        par = tt % 2
        xbuf = ch.ff(AXN_O + par * 1024, 1024)
        xk = f"{tag}x{par}"
        P.emit("sync", lambda E, tt=tt, xbuf=xbuf: E.dma_start(xbuf, x_in_t[tt]),
               reads=(f"xres{tt}",), writes=(xk,), dma=True)
        sc = ch.ff(ASC_O + par * 1040, 1040)
        sck = f"{tag}sc{par}"
        mean = ch.ff(AMN_O + par, 1)
        rstd = ch.ff(AMN_O + 2 + par, 1)
        skey = f"{tag}st{par}"
        ln_stats(P, xbuf, (xk,), mean, rstd, skey, sc, sck, eps)
        xhb = ch.bf(AXB_O + par * 1024, 1024)
        xhbk = f"{tag}xhb{par}"
        P.emit("vector",
               lambda E, xhb=xhb, xbuf=xbuf, mean=mean, rstd=rstd:
                   E.tensor_scalar(xhb, xbuf, mean, rstd,
                                   op0=ALU.subtract, op1=ALU.mult),
               reads=(xk, skey + "_m", skey + "_r"), writes=(xhbk,))
        for g in range(2):
            pb, pbk = ch.psb[g], f"psb{g}"
            for c4 in range(4):
                c = g * 4 + c4
                P.emit("tensor",
                       lambda E, c=c, c4=c4, pb=pb, par=par:
                           E.transpose(pb[:, c4 * 128:(c4 + 1) * 128],
                                       ch.bf(AXB_O + par * 1024 + c * 128, 128),
                                       ch.identb),
                       reads=(xhbk, "identb") if c4 in (0, 3) else (),
                       writes=(pbk,) if c4 in (0, 3) else (), inc=(c4 == 3))
            P.emit("vector",
                   lambda E, g=g, pb=pb:
                       E.tensor_copy(ch.bf(AXTT_O + g * 512, 512), pb[:, 0:512]),
                   reads=(pbk,), writes=(f"{tag}xtt{g}",))
        xtt_keys = (f"{tag}xtt0", f"{tag}xtt1")
        for tau in range(3):
            for eta in range(2):
                bank = ch.psf[tau * 2 + eta]
                bk = f"psf{tau * 2 + eta}"
                pieces = [(ch.bf(AXTT_O + c * 128, 128),
                           ch.bf(AOT_O + c * 3072 + tau * 1024 + eta * 512, 512))
                          for c in range(8)]
                P.mm_group(bank[:, 0:512], pieces, xtt_keys + wq_keys, bk)
        for tau, (foff, fkey) in enumerate(((AQF_O, f"{tag}qf"),
                                            (AKF_O, f"{tag}kf"))):
            for eta in range(2):
                bk = f"psf{tau * 2 + eta}"
                P.emit("vector",
                       lambda E, tau=tau, eta=eta, foff=foff:
                           E.tensor_tensor(ch.ff(foff + eta * 512, 512),
                                           ch.psf[tau * 2 + eta][:, 0:512],
                                           biasF[:, tau * 1024 + eta * 512:
                                                 tau * 1024 + eta * 512 + 512],
                                           op=ALU.add),
                       reads=(bk, tag + "biasF"), writes=(fkey + str(eta),))
        for eta in range(2):
            bk = f"psf{4 + eta}"
            P.emit("vector",
                   lambda E, eta=eta, par=par:
                       E.tensor_tensor(ch.bf(AVB_O + eta * 512, 512),
                                       ch.psf[4 + eta][:, 0:512],
                                       biasF[:, 2048 + eta * 512:
                                             2048 + eta * 512 + 512],
                                       op=ALU.add),
                   reads=(bk, tag + "biasF"), writes=(f"{tag}vb{eta}",))
        for fo, fkey, nrm_o, rkey in ((AQF_O, f"{tag}qf", ANRM_O, f"{tag}rq"),
                                      (AKF_O, f"{tag}kf", ANRM_O + 32, f"{tag}rk")):
            fkeys = (fkey + "0", fkey + "1")
            P.emit("scalar",
                   lambda E, fo=fo: E.activation(ch.ff(ASQ_O, 1024),
                                                 ch.ff(fo, 1024), AF.Square),
                   reads=fkeys, writes=(tag + "sq",))
            P.emit("vector",
                   lambda E, nrm_o=nrm_o:
                       E.reduce_sum(ch.ff(nrm_o, 16),
                                    ch.ff(ASQ_O, 1024).rearrange(
                                        "p (h c) -> p h c", c=64), axis=AX.X),
                   reads=(tag + "sq",), writes=(tag + "ss" + fkey,))
            P.emit("scalar",
                   lambda E, nrm_o=nrm_o:
                       E.activation(ch.ff(nrm_o, 16), ch.ff(nrm_o, 16), AF.Ln),
                   reads=(tag + "ss" + fkey,), writes=(tag + "ln" + fkey,))
            P.emit("scalar",
                   lambda E, nrm_o=nrm_o:
                       E.activation(ch.ff(nrm_o + 16, 16), ch.ff(nrm_o, 16),
                                    AF.Exp, scale=0.5),
                   reads=(tag + "ln" + fkey,), writes=(tag + "nm" + fkey,))
            P.emit("vector",
                   lambda E, nrm_o=nrm_o:
                       E.tensor_scalar(ch.ff(nrm_o + 16, 16),
                                       ch.ff(nrm_o + 16, 16), EPS_QK, None,
                                       op0=ALU.add),
                   reads=(tag + "nm" + fkey,), writes=(tag + "ne" + fkey,))
            P.emit("vector",
                   lambda E, nrm_o=nrm_o:
                       E.reciprocal(ch.ff(nrm_o + 16, 16), ch.ff(nrm_o + 16, 16)),
                   reads=(tag + "ne" + fkey,), writes=(rkey,))
        for h in range(16):
            eng = "vector" if h % 2 == 0 else "scalar"
            if eng == "vector":
                P.emit("vector",
                       lambda E, h=h, tt=tt:
                           E.tensor_scalar(
                               ch.bf(AQB_O + tt * 1024 + h * 64, 64),
                               ch.ff(AQF_O + h * 64, 64),
                               ch.ff(ANRM_O + 16 + h, 1), None, op0=ALU.mult),
                       reads=(f"{tag}qf0", f"{tag}qf1", f"{tag}rq"),
                       writes=(f"{tag}qb{tt}_{h}",))
            else:
                P.emit("scalar",
                       lambda E, h=h, tt=tt:
                           E.activation(ch.bf(AQB_O + tt * 1024 + h * 64, 64),
                                        ch.ff(AQF_O + h * 64, 64), AF.Copy,
                                        scale=ch.ff(ANRM_O + 16 + h, 1)),
                       reads=(f"{tag}qf0", f"{tag}qf1", f"{tag}rq"),
                       writes=(f"{tag}qb{tt}_{h}",))
        for h in range(16):
            eng = "vector" if h % 2 == 1 else "scalar"
            if eng == "vector":
                P.emit("vector",
                       lambda E, h=h:
                           E.tensor_scalar(ch.bf(AKB_O + h * 64, 64),
                                           ch.ff(AKF_O + h * 64, 64),
                                           ch.ff(ANRM_O + 48 + h, 1), None,
                                           op0=ALU.mult),
                       reads=(f"{tag}kf0", f"{tag}kf1", f"{tag}rk"),
                       writes=(f"{tag}kb{h}",))
            else:
                P.emit("scalar",
                       lambda E, h=h:
                           E.activation(ch.bf(AKB_O + h * 64, 64),
                                        ch.ff(AKF_O + h * 64, 64), AF.Copy,
                                        scale=ch.ff(ANRM_O + 48 + h, 1)),
                       reads=(f"{tag}kf0", f"{tag}kf1", f"{tag}rk"),
                       writes=(f"{tag}kb{h}",))
        for g in range(2):
            pb, pbk = ch.psb[g], f"psb{g}"
            for c4 in range(4):
                c = g * 4 + c4
                P.emit("tensor",
                       lambda E, c=c, c4=c4, pb=pb:
                           E.transpose(pb[:, c4 * 128:(c4 + 1) * 128],
                                       ch.bf(AKB_O + c * 128, 128), ch.identb),
                       reads=((tuple(f"{tag}kb{h}" for h in range(16))
                               + ("identb",)) if c4 in (0, 3) else ()),
                       writes=(pbk,) if c4 in (0, 3) else (), inc=(c4 == 3))
            P.emit("vector",
                   lambda E, g=g, pb=pb, tt=tt, ktv=ktv:
                       E.tensor_copy(
                           ktv[:, g * 4:(g + 1) * 4,
                               32 + tt * 128:32 + (tt + 1) * 128],
                           pb.rearrange("p (c t) -> p c t", t=128)[:, 0:4, :]),
                   reads=(pbk, tag + "ktz"), writes=(f"{tag}kT{tt}_{g}",))
        vbv = ch.bf(AVB_O, 1024).rearrange("p (h c) -> p h c", c=64)
        P.emit("gpsimd",
               lambda E, tt=tt, vbv=vbv, vsv=vsv:
                   E.dma_start(vsv[32:128, tt, :, 0:64], vbv[0:96, :, :]),
               reads=(f"{tag}vb0", f"{tag}vb1", tag + "vsz"),
               writes=(f"{tag}vs{tt}a",), dma=True)
        P.emit("gpsimd",
               lambda E, tt=tt, vbv=vbv, vsv=vsv:
                   E.dma_start(vsv[0:32, tt + 1, :, 0:64], vbv[96:128, :, :]),
               reads=(f"{tag}vb0", f"{tag}vb1", tag + "vsz"),
               writes=(f"{tag}vs{tt}b",), dma=True)
    P.barrier()

    maskb = ch.bf(AMSK_O, 768)
    P.emit("gpsimd", lambda E: E.dma_start(maskb, a["a_masks"]),
           writes=(tag + "maskb",), dma=True)
    for qt in range(16):
        par = qt % 2
        cls = 0 if qt == 0 else (2 if qt == 15 else 1)
        qttk = f"{tag}qtt{par}"
        for g in range(2):
            pb, pbk = ch.psb[g], f"psb{g}"
            for c4 in range(4):
                c = g * 4 + c4
                P.emit("tensor",
                       lambda E, c=c, c4=c4, pb=pb, qt=qt:
                           E.transpose(pb[:, c4 * 128:(c4 + 1) * 128],
                                       ch.bf(AQB_O + qt * 1024 + c * 128, 128),
                                       ch.identb),
                       reads=(tuple(f"{tag}qb{qt}_{h}" for h in range(16))
                              + ("identb",)) if c4 in (0, 3) else (),
                       writes=(pbk,) if c4 in (0, 3) else (), inc=(c4 == 3))
            P.emit("vector",
                   lambda E, g=g, pb=pb, par=par:
                       E.tensor_copy(ch.bf(AQT2_O + par * 1024 + g * 512, 512),
                                     pb[:, 0:512]),
                   reads=(pbk,), writes=(qttk + str(g),))
        oattn_o = AOA_O + par * 1024
        for h in range(16):
            sb = ch.psf[h % 4]
            sbk = f"psf{h % 4}"
            kcol = (h // 2) * 2240 + qt * 128
            lh1 = ch.bf(AKT_O + (h // 2) * 2240, 2240)[64 * (h % 2):64 * (h % 2) + 64,
                                                       qt * 128:qt * 128 + 128]
            lh2 = ch.bf(AKT_O + (h // 2) * 2240, 2240)[64 * (h % 2):64 * (h % 2) + 64,
                                                       qt * 128 + 128:qt * 128 + 192]
            rh = ch.bf(AQT2_O + par * 1024 + (h // 2) * 128, 128)[64 * (h % 2):64 * (h % 2) + 64, :]
            kt_keys = tuple(f"{tag}kT{t_}_{g_}" for t_ in range(16) for g_ in range(2))
            P.emit("tensor",
                   lambda E, sb=sb, lh1=lh1, rh=rh:
                       E.matmul(sb[:, 0:128], lh1, rh, start=True, stop=True),
                   reads=(qttk + "0", qttk + "1") + kt_keys,
                   writes=(sbk,), inc=False)
            P.emit("tensor",
                   lambda E, sb=sb, lh2=lh2, rh=rh:
                       E.matmul(sb[0:64, 128:256], lh2, rh, start=True, stop=True),
                   reads=(qttk + "0", qttk + "1") + kt_keys, writes=(sbk,))
            ab = ch.bf(ATN_O_A + (h % 4) * 256, 256)
            abk = f"{tag}ab{h % 4}"
            P.emit("scalar",
                   lambda E, sb=sb, ab=ab:
                       E.activation(ab[:, 0:128], sb[:, 0:128], AF.Exp),
                   reads=(sbk,), writes=(abk,), inc=False)
            P.emit("scalar",
                   lambda E, sb=sb, ab=ab:
                       E.activation(ab[0:64, 128:256], sb[0:64, 128:256], AF.Exp),
                   reads=(sbk,), writes=(abk,))
            P.emit("vector",
                   lambda E, ab=ab, cls=cls:
                       E.tensor_tensor(ab[:, 0:128], ab[:, 0:128],
                                       maskb[:, cls * 256:cls * 256 + 128],
                                       op=ALU.mult),
                   reads=(abk, tag + "maskb"), writes=(abk,), inc=False)
            P.emit("vector",
                   lambda E, ab=ab, cls=cls:
                       E.tensor_tensor(ab[0:64, 128:256], ab[0:64, 128:256],
                                       maskb[0:64, cls * 256 + 128:cls * 256 + 256],
                                       op=ALU.mult),
                   reads=(abk, tag + "maskb"), writes=(abk,))
            vb2 = ch.psf[4 + h % 2]
            vbk = f"psf{4 + h % 2}"
            P.emit("tensor",
                   lambda E, vb2=vb2, ab=ab, qt=qt, h=h, vsv=vsv:
                       E.matmul(vb2[:, 0:65], ab[:, 0:128], vsv[:, qt, h, :],
                                start=True, stop=False),
                   reads=(abk,) + tuple(f"{tag}vs{t_}{s_}" for t_ in range(16)
                                        for s_ in "ab"),
                   writes=(vbk,), inc=False)
            P.emit("tensor",
                   lambda E, vb2=vb2, ab=ab, qt=qt, h=h, vsv=vsv:
                       E.matmul(vb2[:, 0:65], ab[0:64, 128:256],
                                vsv[0:64, qt + 1, h, :], start=False, stop=True),
                   reads=(abk,), writes=(vbk,))
            P.emit("vector",
                   lambda E, vb2=vb2: E.reciprocal(ch.ff(ARZ_O, 1),
                                                   vb2[:, 64:65]),
                   reads=(vbk,), writes=(tag + "rz",))
            P.emit("vector",
                   lambda E, vb2=vb2, h=h, oattn_o=oattn_o:
                       E.tensor_scalar(ch.bf(oattn_o + h * 64, 64),
                                       vb2[:, 0:64], ch.ff(ARZ_O, 1), None,
                                       op0=ALU.mult),
                   reads=(vbk, tag + "rz"), writes=(f"{tag}oa{par}_{h}",))
        oa_keys = tuple(f"{tag}oa{par}_{h}" for h in range(16))
        for g in range(2):
            pb, pbk = ch.psb[g], f"psb{g}"
            for c4 in range(4):
                c = g * 4 + c4
                P.emit("tensor",
                       lambda E, c=c, c4=c4, pb=pb, oattn_o=oattn_o:
                           E.transpose(pb[:, c4 * 128:(c4 + 1) * 128],
                                       ch.bf(oattn_o + c * 128, 128), ch.identb),
                       reads=(oa_keys + ("identb",)) if c4 in (0, 3) else (),
                       writes=(pbk,) if c4 in (0, 3) else (), inc=(c4 == 3))
            otv = ch.bf(AOT_O, 16384).rearrange("p (c t) -> p c t", t=2048)
            P.emit("vector",
                   lambda E, g=g, pb=pb, qt=qt, otv=otv:
                       E.tensor_copy(otv[:, g * 4:(g + 1) * 4,
                                         qt * 128:(qt + 1) * 128],
                                     pb.rearrange("p (c t) -> p c t", t=128)[:, 0:4, :]),
                   reads=(pbk,) + (wq_keys if qt == 0 else ()),
                   writes=(f"{tag}oT{qt}_{g}",))
    wov = a["a_woT"].rearrange("(c p) m -> p c m", p=128)
    wo = ch.bf(AQB_O, 8192)
    wok = f"{tag}wo"
    qb_keys_all = tuple(f"{tag}qb{t_}_{h_}" for t_ in range(16) for h_ in range(16))
    P.emit("gpsimd", lambda E, wo=wo: E.dma_start(wo, wov[:, :, :]),
           writes=(wok,) + qb_keys_all, dma=True)
    P.barrier()

    oT_keys = tuple(f"{tag}oT{q_}_{g_}" for q_ in range(16) for g_ in range(2))
    for grp in range(4):
        for j in range(4):
            qt = grp * 4 + j
            par = qt % 2
            for eta in range(2):
                bank = ch.psf[par * 2 + eta]
                bk = f"psf{par * 2 + eta}"
                pieces = [(ch.bf(AOT_O + c * 2048 + qt * 128, 128),
                           wo[:, c * 1024 + eta * 512:c * 1024 + eta * 512 + 512])
                          for c in range(8)]
                P.mm_group(bank[:, 0:512], pieces, (wok,) + oT_keys, bk)
                P.emit("vector",
                       lambda E, bank=bank, eta=eta, par=par:
                           E.tensor_tensor(ch.ff(AXN_O + par * 1024 + eta * 512, 512),
                                           bank[:, 0:512],
                                           boF[:, eta * 512:eta * 512 + 512],
                                           op=ALU.add),
                       reads=(bk, tag + "boF"), writes=(f"{tag}on{par}",))
            onat = ch.ff(AXN_O + par * 1024, 1024)
            onk = f"{tag}on{par}"
            sc = ch.ff(ASC_O + par * 1040, 1040)
            sck = f"{tag}csc{par}"
            mean = ch.ff(AMN_O + 4 + par, 1)
            rstd = ch.ff(AMN_O + 6 + par, 1)
            skey = f"{tag}cst{par}"
            ln_stats(P, onat, (onk,), mean, rstd, skey, sc, sck, eps)
            lnout = ch.ff(AKF_O, 1024)
            lnk = f"{tag}lno"
            P.emit("vector",
                   lambda E, onat=onat, mean=mean, rstd=rstd, lnout=lnout:
                       E.tensor_scalar(lnout, onat, mean, rstd,
                                       op0=ALU.subtract, op1=ALU.mult),
                   reads=(onk, skey + "_m", skey + "_r"), writes=(lnk,))
            xbuf = ch.ff(AQF_O, 1024)
            xk = f"{tag}cxr"
            P.emit("sync",
                   lambda E, qt=qt, xbuf=xbuf: E.dma_start(xbuf, x_in_t[qt]),
                   reads=(f"xres{qt}",), writes=(xk, f"{tag}qf0", f"{tag}qf1"),
                   dma=True)
            P.emit("vector",
                   lambda E, xbuf=xbuf, lnout=lnout:
                       E.tensor_add(xbuf, xbuf, lnout),
                   reads=(xk, lnk), writes=(xk,))
            P.emit("sync",
                   lambda E, qt=qt, xbuf=xbuf: E.dma_start(x_out_t[qt], xbuf),
                   reads=(xk,), writes=(f"xres{qt}",), dma=True)
    P.barrier()


def build_program_a():
    nc = bass.Bass("TRN2", target_bir_lowering=False, debug=False)
    x_in = nc.dram_tensor("x_in", [T, D], F32, kind="ExternalInput").ap()
    x_out = nc.dram_tensor("x_out", [T, D], F32, kind="ExternalOutput").ap()
    a = declare_dram_a(nc)
    a["c_ident"] = nc.dram_tensor("c_ident", [128, 128], F32,
                                  kind="ExternalInput").ap()
    P = PB(nc)
    ch = Chip(nc)
    ch.load_consts(P, a)
    build_stage_a(P, ch, a, x_in, x_out)
    P.barrier()
    P.build()
    return nc


def run_stage_a(x, inputs, cores=1):
    nc = build_program_a()
    shared = host_prep_a(inputs)
    shared.update(_host_consts())
    in_maps = []
    for c in range(cores):
        m = dict(shared)
        m["x_in"] = np.ascontiguousarray(x[c])
        in_maps.append(m)
    res = run_bass_kernel_spmd(nc, in_maps, core_ids=list(range(cores)))
    return np.stack([r["x_out"] for r in res.results])


BXT_O, BH_O, BW_O, BXB_O = 0, 8192, 40960, 73728
BSC_O, BXN_O, BXF_O, BRW_O = 0, 2080, 4128, 5152
BLG_O, BCB_O, BNR_O, BST_O = 5216, 5280, 5408, 5440
BMN_O, BIO_O, BKMI_O = 7490, 7500, 7508


def host_prep_b(inp):
    m = {}
    m["b_rwT"] = np.ascontiguousarray(np.asarray(inp["L0_ffn_router_w"]).T).astype(np.float32)
    m["b_rbrow"] = np.asarray(inp["L0_ffn_router_b"], np.float32).reshape(1, 8).copy()
    wup = np.asarray(inp["L0_ffn_w_up"])
    wdn = np.asarray(inp["L0_ffn_w_down"])
    swup = np.asarray(inp["L0_ffn_shared_w_up"])
    swdn = np.asarray(inp["L0_ffn_shared_w_down"])
    m["b_wupT"] = np.ascontiguousarray(
        np.concatenate([wup, swup], 0).transpose(0, 2, 1)).astype(np.float32)
    m["b_wdnT"] = np.ascontiguousarray(
        np.concatenate([wdn, swdn], 0).transpose(0, 2, 1)).astype(np.float32)
    m["c_iota8"] = np.broadcast_to(np.arange(8, dtype=np.float32), (128, 8)).copy()
    m["c_kmi8"] = np.broadcast_to(1e4 - np.arange(8, dtype=np.float32), (128, 8)).copy()
    pfx = np.zeros((128, 256), dtype=np.float32)
    pfx[:, 0:128] = 1.0
    pfx[:, 128:256] = np.tril(np.ones((128, 128), dtype=np.float32), -1)
    m["c_pfx"] = pfx
    m["c_iotacol"] = np.arange(128, dtype=np.float32)[:, None].copy()
    m["c_huge"] = np.full((4096, 2), 1e9, dtype=np.float32)
    return m


def declare_dram_b(nc):
    a = {}
    a["b_rwT"] = nc.dram_tensor("b_rwT", [1024, 8], F32, kind="ExternalInput").ap()
    a["b_rbrow"] = nc.dram_tensor("b_rbrow", [1, 8], F32, kind="ExternalInput").ap()
    a["b_wupT"] = nc.dram_tensor("b_wupT", [9, 1024, 2048], F32, kind="ExternalInput").ap()
    a["b_wdnT"] = nc.dram_tensor("b_wdnT", [9, 2048, 1024], F32, kind="ExternalInput").ap()
    a["c_iota8"] = nc.dram_tensor("c_iota8", [128, 8], F32, kind="ExternalInput").ap()
    a["c_kmi8"] = nc.dram_tensor("c_kmi8", [128, 8], F32, kind="ExternalInput").ap()
    a["c_pfx"] = nc.dram_tensor("c_pfx", [128, 256], F32, kind="ExternalInput").ap()
    a["c_iotacol"] = nc.dram_tensor("c_iotacol", [128, 1], F32, kind="ExternalInput").ap()
    a["c_huge"] = nc.dram_tensor("c_huge", [4096, 2], F32, kind="ExternalInput").ap()
    return a


def build_stage_b(P, ch, a, x_in, x_out, moe_scratch, tag="B",
                  tail_preload=None):
    nc = P.nc
    x_in_t = x_in.rearrange("(n p) d -> n p d", p=128)
    x_out_t = x_out.rearrange("(n p) d -> n p d", p=128)
    moe_t = moe_scratch.rearrange("(n p) d -> n p d", p=128)
    eps = ch.eps_ap()
    rwv = a["b_rwT"].rearrange("(c p) e -> p c e", p=128)
    wupv = a["b_wupT"].rearrange("e (c p) f -> p e c f", p=128)
    wdnv = a["b_wdnT"].rearrange("e (c p) f -> p e c f", p=128)

    P.emit("sync", lambda E: E.dma_start(
               ch.ff(BRW_O, 64).rearrange("p (c e) -> p c e", e=8), rwv),
           writes=(tag + "rw",), dma=True)
    P.emit("sync", lambda E: E.dma_start(ch.ff(BIO_O, 8), a["c_iota8"]),
           writes=(tag + "iota",), dma=True)
    P.emit("sync", lambda E: E.dma_start(ch.ff(BKMI_O, 8), a["c_kmi8"]),
           writes=(tag + "kmi",), dma=True)
    P.emit("sync", lambda E: E.dma_start(ch.rbrow, a["b_rbrow"]),
           writes=(tag + "rb",), dma=True)

    wup = ch.bf(BW_O, 16384)
    wdn = ch.bf(BW_O + 16384, 16384)
    wupk, wdnk = f"{tag}wup", f"{tag}wdn"

    def emit_wup_b(e):
        P.emit("gpsimd",
               lambda E, e=e:
                   E.dma_start(wup.rearrange("p (c f) -> p c f", f=2048),
                               wupv[:, e, :, :]),
               writes=(wupk,), dma=True)

    def emit_wdn_b(e):
        P.emit("gpsimd",
               lambda E, e=e:
                   E.dma_start(wdn.rearrange("p (c f) -> p c f", f=1024),
                               wdnv[:, e, :, :]),
               writes=(wdnk,), dma=True)

    for th in range(2):
        emit_wup_b(0)
        emit_wdn_b(0)
        for j in range(8):
            tt = th * 8 + j
            par = j % 2
            xbuf = ch.ff(BXN_O + par * 1024, 1024)
            xk = f"{tag}x{par}"
            P.emit("sync", lambda E, tt=tt, xbuf=xbuf: E.dma_start(xbuf, x_in_t[tt]),
                   reads=(f"xres{tt}",), writes=(xk,), dma=True)
            sc = ch.ff(BSC_O + par * 1040, 1040)
            sck = f"{tag}sc{par}"
            mean = ch.ff(BMN_O + par, 1)
            rstd = ch.ff(BMN_O + 2 + par, 1)
            skey = f"{tag}st{par}"
            ln_stats(P, xbuf, (xk,), mean, rstd, skey, sc, sck, eps)
            xhb = ch.bf(BXB_O + par * 1024, 1024)
            xhbk = f"{tag}xhb{par}"
            P.emit("vector",
                   lambda E, xhb=xhb, xbuf=xbuf, mean=mean, rstd=rstd:
                       E.tensor_scalar(xhb, xbuf, mean, rstd,
                                       op0=ALU.subtract, op1=ALU.mult),
                   reads=(xk, skey + "_m", skey + "_r"), writes=(xhbk,))
            xhf = ch.ff(BXF_O, 1024)
            xhfk = f"{tag}xhf"
            P.emit("vector",
                   lambda E, xhf=xhf, xbuf=xbuf, mean=mean, rstd=rstd:
                       E.tensor_scalar(xhf, xbuf, mean, rstd,
                                       op0=ALU.subtract, op1=ALU.mult),
                   reads=(xk, skey + "_m", skey + "_r"), writes=(xhfk,))
            for g in range(2):
                pb, pbk = ch.psb[g], f"psb{g}"
                for c4 in range(4):
                    c = g * 4 + c4
                    P.emit("tensor",
                           lambda E, c=c, c4=c4, pb=pb, par=par:
                               E.transpose(pb[:, c4 * 128:(c4 + 1) * 128],
                                           ch.bf(BXB_O + par * 1024 + c * 128, 128),
                                           ch.identb),
                           reads=(xhbk, "identb") if c4 in (0, 3) else (),
                           writes=(pbk,) if c4 in (0, 3) else (), inc=(c4 == 3))
                xtv = ch.bf(BXT_O, 8192).rearrange("p (c t) -> p c t", t=1024)
                P.emit("vector",
                       lambda E, g=g, j=j, pb=pb, xtv=xtv:
                           E.tensor_copy(xtv[:, g * 4:(g + 1) * 4,
                                             j * 128:(j + 1) * 128],
                                         pb.rearrange("p (c t) -> p c t", t=128)[:, 0:4, :]),
                       reads=(pbk,), writes=(f"{tag}xT_{j}_{g}",))
            xtf = ch.outT[:, 0:1024]
            for g in range(2):
                bank, bk = ch.psf[4 + g], f"psf{4 + g}"
                for c4 in range(4):
                    c = g * 4 + c4
                    P.emit("tensor",
                           lambda E, c=c, c4=c4, bank=bank, xhf=xhf:
                               E.transpose(bank[:, c4 * 128:(c4 + 1) * 128],
                                           xhf[:, c * 128:(c + 1) * 128],
                                           ch.identf),
                           reads=(xhfk, "identf") if c4 in (0, 3) else (),
                           writes=(bk,) if c4 in (0, 3) else (), inc=(c4 == 3))
                P.emit("vector",
                       lambda E, g=g, bank=bank, xtf=xtf:
                           E.tensor_copy(xtf[:, g * 512:(g + 1) * 512],
                                         bank[:, 0:512]),
                       reads=(bk,), writes=(f"{tag}xtf{g}",))
            rb_ = ch.psf[0]
            rbk = "psf0"
            pieces = [(xtf[:, c * 128:(c + 1) * 128], ch.ff(BRW_O + c * 8, 8))
                      for c in range(8)]
            n_p = len(pieces)
            for i, (lh, rh) in enumerate(pieces):
                P.emit("tensor",
                       lambda E, lh=lh, rh=rh, i=i:
                           E.matmul(rb_[:, 0:8], lh, rh, start=(i == 0), stop=False),
                       reads=((f"{tag}xtf0", f"{tag}xtf1", tag + "rw", tag + "rb")
                              if i == 0 else ()),
                       writes=(rbk,) if i == 0 else (), inc=False)
            P.emit("tensor",
                   lambda E: E.matmul(rb_[:, 0:8], ch.onesf[0:1, :],
                                      ch.rbrow[0:1, :], start=False, stop=True),
                   reads=(f"{tag}xtf0", f"{tag}xtf1", tag + "rw", tag + "rb"),
                   writes=(rbk,))
            eb = ch.ff(BLG_O, 8)
            P.emit("scalar", lambda E, eb=eb, rb_=rb_: E.activation(eb, rb_[:, 0:8], AF.Exp),
                   reads=(rbk,), writes=(tag + "eb",))
            c_m1 = ch.ff(BNR_O, 1); c_i1 = ch.ff(BNR_O + 1, 1)
            c_m2 = ch.ff(BNR_O + 2, 1); c_rd = ch.ff(BNR_O + 3, 1)
            w8a = ch.ff(BLG_O + 8, 8); w8b = ch.ff(BLG_O + 16, 8)
            cmb = ch.ff(BCB_O + tt * 8, 8)
            def ve(fn, rd, wr):
                P.emit("vector", fn, reads=rd, writes=wr)
            ve(lambda E, eb=eb, c_m1=c_m1: E.reduce_max(c_m1, eb, axis=AX.X),
               (tag + "eb",), (tag + "m1",))
            ve(lambda E, w8a=w8a, eb=eb, c_m1=c_m1:
               E.tensor_scalar(w8a, eb, c_m1, None, op0=ALU.is_equal),
               (tag + "eb", tag + "m1"), (tag + "w8a",))
            ve(lambda E, w8a=w8a: E.tensor_mul(w8a, w8a, ch.ff(BKMI_O, 8)),
               (tag + "w8a", tag + "kmi"), (tag + "w8a",))
            ve(lambda E, w8a=w8a: E.tensor_scalar(w8a, w8a, -1e4, None, op0=ALU.add),
               (tag + "w8a",), (tag + "w8a",))
            ve(lambda E, c_i1=c_i1, w8a=w8a: E.reduce_max(c_i1, w8a, axis=AX.X),
               (tag + "w8a",), (tag + "i1",))
            ve(lambda E, c_i1=c_i1: E.tensor_scalar(c_i1, c_i1, -1.0, None, op0=ALU.mult),
               (tag + "i1",), (tag + "i1",))
            ve(lambda E, w8a=w8a, c_i1=c_i1:
               E.tensor_scalar(w8a, ch.ff(BIO_O, 8), c_i1, None, op0=ALU.is_equal),
               (tag + "iota", tag + "i1"), (tag + "w8a",))
            ve(lambda E, w8b=w8b, eb=eb: E.tensor_scalar(w8b, eb, 1e4, None, op0=ALU.add),
               (tag + "eb",), (tag + "w8b",))
            ve(lambda E, w8b=w8b, w8a=w8a: E.tensor_mul(w8b, w8a, w8b),
               (tag + "w8a", tag + "w8b"), (tag + "w8b",))
            ve(lambda E, w8b=w8b, eb=eb: E.tensor_sub(w8b, eb, w8b),
               (tag + "eb", tag + "w8b"), (tag + "w8b",))
            ve(lambda E, c_m2=c_m2, w8b=w8b: E.reduce_max(c_m2, w8b, axis=AX.X),
               (tag + "w8b",), (tag + "m2",))
            ve(lambda E, c_rd=c_rd, c_m1=c_m1, c_m2=c_m2: E.tensor_add(c_rd, c_m1, c_m2),
               (tag + "m1", tag + "m2"), (tag + "rd",))
            ve(lambda E, c_rd=c_rd: E.reciprocal(c_rd, c_rd), (tag + "rd",), (tag + "rd",))
            ve(lambda E, w8b=w8b, c_m2=c_m2:
               E.tensor_scalar(w8b, w8b, c_m2, None, op0=ALU.is_equal),
               (tag + "w8b", tag + "m2"), (tag + "w8b",))
            ve(lambda E, w8a=w8a, c_m1=c_m1: E.tensor_scalar(w8a, w8a, c_m1, None, op0=ALU.mult),
               (tag + "w8a", tag + "m1"), (tag + "w8a",))
            ve(lambda E, w8b=w8b, c_m2=c_m2: E.tensor_scalar(w8b, w8b, c_m2, None, op0=ALU.mult),
               (tag + "w8b", tag + "m2"), (tag + "w8b",))
            ve(lambda E, w8a=w8a, w8b=w8b: E.tensor_add(w8a, w8a, w8b),
               (tag + "w8a", tag + "w8b"), (tag + "w8a",))
            ve(lambda E, cmb=cmb, w8a=w8a, c_rd=c_rd:
               E.tensor_scalar(cmb, w8a, c_rd, None, op0=ALU.mult),
               (tag + "w8a", tag + "rd"), (f"{tag}cb{tt}",))
        xT_keys = tuple(f"{tag}xT_{j}_{g}" for j in range(8) for g in range(2))

        for e in range(9):
            hb_o = BH_O + (e % 2) * 16384
            hpfx = f"{tag}h{e % 2}_"
            if e > 0:
                emit_wdn_b(e)
            for ft in range(16):
                for eta in range(2):
                    bank = ch.psf[(ft % 2) * 2 + eta]
                    bk = f"psf{(ft % 2) * 2 + eta}"
                    pieces = [(wup[:, c * 2048 + ft * 128:c * 2048 + ft * 128 + 128],
                               ch.bf(BXT_O + c * 1024 + eta * 512, 512))
                              for c in range(8)]
                    P.mm_group(bank[:, 0:512], pieces, (wupk,) + xT_keys, bk)
                    P.emit("scalar",
                           lambda E, bank=bank, ft=ft, eta=eta, hb_o=hb_o:
                               E.activation(ch.bf(hb_o + ft * 1024 + eta * 512, 512),
                                            bank[:, 0:512], AF.Silu),
                           reads=(bk,), writes=(hpfx + f"{ft}_{eta}",))
            h_keys = tuple(hpfx + f"{f_}_{e_}" for f_ in range(16) for e_ in range(2))
            if e < 8:
                emit_wup_b(e + 1)
            for tt in range(8):
                gtt = th * 8 + tt
                stg = ch.ff(BST_O + (tt % 2) * 1024, 1024)
                stgk = f"{tag}stg{tt % 2}"
                for eta in range(2):
                    bank = ch.psf[4 + eta]
                    bk = f"psf{4 + eta}"
                    pieces = [(ch.bf(hb_o + c * 1024 + tt * 128, 128),
                               wdn[:, c * 1024 + eta * 512:c * 1024 + eta * 512 + 512])
                              for c in range(16)]
                    P.mm_group(bank[:, 0:512], pieces, (wdnk,) + h_keys, bk)
                    if e < 8:
                        P.emit("vector",
                               lambda E, bank=bank, eta=eta, stg=stg, gtt=gtt, e=e:
                                   E.tensor_scalar(stg[:, eta * 512:eta * 512 + 512],
                                                   bank[:, 0:512],
                                                   ch.ff(BCB_O + gtt * 8 + e, 1),
                                                   None, op0=ALU.mult),
                               reads=(bk, f"{tag}cb{gtt}"), writes=(stgk,))
                    else:
                        P.emit("vector",
                               lambda E, bank=bank, eta=eta, stg=stg:
                                   E.tensor_copy(stg[:, eta * 512:eta * 512 + 512],
                                                 bank[:, 0:512]),
                               reads=(bk,), writes=(stgk,))
                P.emit("gpsimd" if e > 0 else "gpsimd",
                       lambda E, stg=stg, gtt=gtt, e=e:
                           E.dma_start(moe_t[gtt], stg,
                                       accum_op=(ALU.bypass if e == 0 else ALU.add)),
                       reads=(stgk,), writes=(f"{tag}moe{gtt}",), dma=True)
        for j in range(8):
            tt = th * 8 + j
            par = j % 2
            obuf = ch.ff(BXN_O + par * 1024, 1024)
            obk = f"{tag}ob{par}"
            P.emit("sync",
                   lambda E, tt=tt, obuf=obuf: E.dma_start(obuf, moe_t[tt]),
                   reads=(f"{tag}moe{tt}",), writes=(obk,), dma=True)
            sc = ch.ff(BSC_O + par * 1040, 1040)
            sck = f"{tag}fsc{par}"
            mean = ch.ff(BMN_O + 4 + par, 1)
            rstd = ch.ff(BMN_O + 6 + par, 1)
            skey = f"{tag}fst{par}"
            ln_stats(P, obuf, (obk,), mean, rstd, skey, sc, sck, eps)
            lnout = ch.ff(BXF_O, 1024)
            lnk = f"{tag}flno"
            P.emit("vector",
                   lambda E, obuf=obuf, mean=mean, rstd=rstd, lnout=lnout:
                       E.tensor_scalar(lnout, obuf, mean, rstd,
                                       op0=ALU.subtract, op1=ALU.mult),
                   reads=(obk, skey + "_m", skey + "_r"),
                   writes=(lnk, f"{tag}xhf"),)
            xbuf = ch.ff(BST_O, 1024)
            xk2 = f"{tag}fxr"
            P.emit("sync",
                   lambda E, tt=tt, xbuf=xbuf: E.dma_start(xbuf, x_in_t[tt]),
                   reads=(f"xres{tt}",), writes=(xk2, f"{tag}stg0"), dma=True)
            P.emit("vector",
                   lambda E, xbuf=xbuf, lnout=lnout: E.tensor_add(xbuf, xbuf, lnout),
                   reads=(xk2, lnk), writes=(xk2,))
            P.emit("sync",
                   lambda E, tt=tt, xbuf=xbuf: E.dma_start(x_out_t[tt], xbuf),
                   reads=(xk2,), writes=(f"xres{tt}",), dma=True)
        if th == 1 and tail_preload is not None:
            tail_preload()
        P.barrier()


def build_program_b():
    nc = bass.Bass("TRN2", target_bir_lowering=False, debug=False)
    x_in = nc.dram_tensor("x_in", [T, D], F32, kind="ExternalInput").ap()
    x_out = nc.dram_tensor("x_out", [T, D], F32, kind="ExternalOutput").ap()
    moe = nc.dram_tensor("b_moe", [T, D], F32, kind="Internal").ap()
    a = declare_dram_b(nc)
    a["c_ident"] = nc.dram_tensor("c_ident", [128, 128], F32,
                                  kind="ExternalInput").ap()
    P = PB(nc)
    ch = Chip(nc)
    ch.load_consts(P, a)
    build_stage_b(P, ch, a, x_in, x_out, moe)
    P.barrier()
    P.build()
    return nc


def run_stage_b(x, inputs, cores=1):
    nc = build_program_b()
    shared = host_prep_b(inputs)
    shared.update(_host_consts())
    in_maps = []
    for c in range(cores):
        m = dict(shared)
        m["x_in"] = np.ascontiguousarray(x[c])
        in_maps.append(m)
    res = run_bass_kernel_spmd(nc, in_maps, core_ids=list(range(cores)))
    return np.stack([r["x_out"] for r in res.results])


CU_O, CV_O, CVF_O, CDG_O, CG_O, CG1_O, CW_O = \
    0, 16384, 32768, 67584, 71936, 72976, 77072
CSC_O, CXN_O, CS1_O, CS2_O, CMN_O, CB12_O = 0, 2080, 4128, 5152, 6176, 6200
NF = 2176
NFT = NF // 128


def host_prep_c(inp):
    m = {}
    wu = np.asarray(inp["L1_attn_Wu"], np.float32)
    wv = np.asarray(inp["L1_attn_Wv"], np.float32)
    m["c_uvw"] = np.ascontiguousarray(np.concatenate([wu, wv], axis=1))
    m["c_wo2"] = np.ascontiguousarray(np.asarray(inp["L1_attn_Wo"], np.float32))
    bu = np.asarray(inp["L1_attn_bu"], np.float32)
    bv = np.asarray(inp["L1_attn_bv"], np.float32)
    bo = np.asarray(inp["L1_attn_bo"], np.float32)
    m["c_buv"] = np.ascontiguousarray(np.broadcast_to(
        np.concatenate([bu, bv])[None, :], (128, 2048)))
    m["c_bo2"] = np.ascontiguousarray(np.broadcast_to(bo[None, :], (128, 1024)))
    L, FO = 2048, 64
    pos = np.arange(L, dtype=np.float64)[:, None]
    div = np.exp(np.arange(0, FO, 2, dtype=np.float64) * -(np.log(10000.0) / FO))
    pe = np.zeros((L, FO))
    pe[:, 0::2] = np.sin(pos * div)
    pe[:, 1::2] = np.cos(pos * div)
    m["c_peT"] = np.ascontiguousarray(pe.T).astype(np.float32)
    m["c_w12"] = np.ascontiguousarray(np.concatenate(
        [np.asarray(inp["L1_attn_W1"], np.float32),
         np.asarray(inp["L1_attn_W2"], np.float32)], axis=1))
    m["c_b12"] = np.ascontiguousarray(np.stack(
        [np.asarray(inp["L1_attn_b1"], np.float32),
         np.asarray(inp["L1_attn_b2"], np.float32)], axis=1))
    m["c_w3p"] = np.ascontiguousarray(np.concatenate(
        [np.asarray(inp["L1_attn_W3"], np.float32),
         np.asarray(inp["L1_attn_b3"], np.float32)[None, :]], axis=0))
    n = 4096
    t = np.arange(L, dtype=np.float64)
    f = np.arange(2049, dtype=np.float64)
    ang = 2 * np.pi * np.outer(t, f) / n
    dftt = np.zeros((L, 2 * NF))
    dftt[:, 0:2049] = np.cos(ang)
    dftt[:, NF:NF + 2049] = -np.sin(ang)
    m["c_dftt"] = dftt.astype(np.float32)
    decay = np.exp(-0.01 * t)[:, None]
    m["c_ddt"] = (dftt * decay).astype(np.float32)
    wf = np.full(2049, 2.0); wf[0] = 1.0; wf[2048] = 1.0
    ict = np.zeros((2 * NF, L))
    ict[0:2049, :] = (wf[:, None] / n) * np.cos(ang.T)
    ict[NF:NF + 2049, :] = -(wf[:, None] / n) * np.sin(ang.T)
    m["c_ict"] = ict.astype(np.float32)
    return m


def declare_dram_c(nc):
    a = {}
    a["c_uvw"] = nc.dram_tensor("c_uvw", [1024, 2048], F32, kind="ExternalInput").ap()
    a["c_wo2"] = nc.dram_tensor("c_wo2", [1024, 1024], F32, kind="ExternalInput").ap()
    a["c_buv"] = nc.dram_tensor("c_buv", [128, 2048], F32, kind="ExternalInput").ap()
    a["c_bo2"] = nc.dram_tensor("c_bo2", [128, 1024], F32, kind="ExternalInput").ap()
    a["c_peT"] = nc.dram_tensor("c_peT", [64, 2048], F32, kind="ExternalInput").ap()
    a["c_w12"] = nc.dram_tensor("c_w12", [64, 128], F32, kind="ExternalInput").ap()
    a["c_b12"] = nc.dram_tensor("c_b12", [64, 2], F32, kind="ExternalInput").ap()
    a["c_w3p"] = nc.dram_tensor("c_w3p", [65, 1024], F32, kind="ExternalInput").ap()
    a["c_dftt"] = nc.dram_tensor("c_dftt", [2048, 2 * NF], F32, kind="ExternalInput").ap()
    a["c_ddt"] = nc.dram_tensor("c_ddt", [2048, 2 * NF], F32, kind="ExternalInput").ap()
    a["c_ict"] = nc.dram_tensor("c_ict", [2 * NF, 2048], F32, kind="ExternalInput").ap()
    return a


def build_stage_c(P, ch, a, x_in, x_out, tag="C", skip_uvw=False,
                  tail_preload=None):
    x_in_t = x_in.rearrange("(n p) d -> n p d", p=128)
    x_out_t = x_out.rearrange("(n p) d -> n p d", p=128)
    eps = ch.eps_ap()
    buvF = ch.outT[:, 0:2048]
    boF = ch.outT[:, 2048:3072]
    P.emit("sync", lambda E: E.dma_start(buvF, a["c_buv"]),
           writes=(tag + "buv",), dma=True)
    P.emit("sync", lambda E: E.dma_start(boF, a["c_bo2"]),
           writes=(tag + "bo",), dma=True)
    uvv = a["c_uvw"].rearrange("(c p) o -> p c o", p=128)
    if not skip_uvw:
        for c in range(8):
            P.emit("gpsimd",
                   lambda E, c=c: E.dma_start(ch.bf(CVF_O + c * 2048, 2048),
                                              uvv[:, c, :]),
                   writes=(f"{tag}uvw{c}",), dma=True)
    uvw_keys = tuple(f"{tag}uvw{c}" for c in range(8))

    for tt in range(16):
        par = tt % 2
        xbuf = ch.ff(CXN_O + par * 1024, 1024)
        xk = f"{tag}x{par}"
        P.emit("sync", lambda E, tt=tt, xbuf=xbuf: E.dma_start(xbuf, x_in_t[tt]),
               reads=(f"xres{tt}",), writes=(xk,), dma=True)
        sc = ch.ff(CSC_O + par * 1040, 1040)
        sck = f"{tag}sc{par}"
        mean = ch.ff(CMN_O + par, 1)
        rstd = ch.ff(CMN_O + 2 + par, 1)
        skey = f"{tag}st{par}"
        ln_stats(P, xbuf, (xk,), mean, rstd, skey, sc, sck, eps)
        xhb = ch.bf(CG1_O + par * 1024, 1024)
        xhbk = f"{tag}xhb{par}"
        P.emit("vector",
               lambda E, xhb=xhb, xbuf=xbuf, mean=mean, rstd=rstd:
                   E.tensor_scalar(xhb, xbuf, mean, rstd,
                                   op0=ALU.subtract, op1=ALU.mult),
               reads=(xk, skey + "_m", skey + "_r"), writes=(xhbk,))
        for g in range(2):
            pb, pbk = ch.psb[g], f"psb{g}"
            for c4 in range(4):
                c = g * 4 + c4
                P.emit("tensor",
                       lambda E, c=c, c4=c4, pb=pb, par=par:
                           E.transpose(pb[:, c4 * 128:(c4 + 1) * 128],
                                       ch.bf(CG1_O + par * 1024 + c * 128, 128),
                                       ch.identb),
                       reads=(xhbk, "identb") if c4 in (0, 3) else (),
                       writes=(pbk,) if c4 in (0, 3) else (), inc=(c4 == 3))
            P.emit("vector",
                   lambda E, g=g, pb=pb:
                       E.tensor_copy(ch.bf(CW_O + g * 512, 512), pb[:, 0:512]),
                   reads=(pbk,), writes=(f"{tag}xtt{g}",))
        xtt_keys = (f"{tag}xtt0", f"{tag}xtt1")
        for q in range(4):
            bank, bk = ch.psf[q], f"psf{q}"
            pieces = [(ch.bf(CW_O + c * 128, 128),
                       ch.bf(CVF_O + c * 2048 + q * 512, 512))
                      for c in range(8)]
            P.mm_group(bank[:, 0:512], pieces, xtt_keys + uvw_keys, bk)
            dst_o = (CU_O if q < 2 else CV_O) + tt * 1024 + (q % 2) * 512
            P.emit("vector",
                   lambda E, bank=bank, q=q, dst_o=dst_o:
                       E.tensor_tensor(ch.bf(dst_o, 512), bank[:, 0:512],
                                       buvF[:, q * 512:(q + 1) * 512], op=ALU.add),
                   reads=(bk, tag + "buv"), writes=(f"{tag}uv{tt}_{q}",))
    u_keys = tuple(f"{tag}uv{t_}_{q_}" for t_ in range(16) for q_ in (0, 1))
    v_keys = tuple(f"{tag}uv{t_}_{q_}" for t_ in range(16) for q_ in (2, 3))
    P.barrier()

    peT = ch.bf(CG1_O, 2048)
    P.emit("gpsimd", lambda E: E.dma_start(peT[0:64, :], a["c_peT"]),
           writes=(tag + "peT",), dma=True)
    w12 = ch.bf(CW_O + 4096, 128)
    P.emit("gpsimd", lambda E: E.dma_start(w12[0:64, :], a["c_w12"]),
           writes=(tag + "w12",), dma=True)
    b12 = ch.ff(CB12_O, 2)
    P.emit("sync", lambda E: E.dma_start(b12[0:64, :], a["c_b12"]),
           writes=(tag + "b12",), dma=True)
    g1T = ch.bf(CG1_O + 2048, 2048)
    for q in range(4):
        bank, bk = ch.psf[4 + q % 2], f"psf{4 + q % 2}"
        P.emit("tensor",
               lambda E, q=q, bank=bank:
                   E.matmul(bank[0:64, 0:512], w12[0:64, 0:64],
                            peT[0:64, q * 512:(q + 1) * 512],
                            start=True, stop=True),
               reads=(tag + "peT", tag + "w12"), writes=(bk,))
        P.emit("scalar",
               lambda E, q=q, bank=bank, g1T=g1T:
                   E.activation(g1T[0:64, q * 512:(q + 1) * 512],
                                bank[0:64, 0:512], AF.Gelu_apprx_tanh,
                                bias=b12[0:64, 0:1]),
               reads=(f"psf{4 + q % 2}", tag + "b12"), writes=(f"{tag}g1_{q}",))
    g2T = ch.bf(CG1_O, 2048)
    for q in range(4):
        bank, bk = ch.psf[4 + q % 2], f"psf{4 + q % 2}"
        P.emit("tensor",
               lambda E, q=q, bank=bank, g1T=g1T:
                   E.matmul(bank[0:64, 0:512], w12[0:64, 64:128],
                            g1T[0:64, q * 512:(q + 1) * 512],
                            start=True, stop=True),
               reads=tuple(f"{tag}g1_{qq}" for qq in range(4)) + (tag + "w12",)
                     + ((tag + "peT",) if q == 0 else ()),
               writes=(bk,))
        P.emit("scalar",
               lambda E, q=q, bank=bank, g2T=g2T:
                   E.activation(g2T[0:64, q * 512:(q + 1) * 512],
                                bank[0:64, 0:512], AF.Gelu_apprx_tanh,
                                bias=b12[0:64, 1:2]),
               reads=(f"psf{4 + q % 2}", tag + "b12"),
               writes=(f"{tag}g2_{q}",))
    gp = ch.bf(CG_O, 1040)
    gpv = gp.rearrange("p (t c) -> p t c", c=65)
    P.emit("vector", lambda E, gp=gp: E.memset(gp, 1.0), writes=(tag + "gp1",))
    g2_keys = tuple(f"{tag}g2_{q}" for q in range(4))
    for tt in range(16):
        pb, pbk = ch.psb[tt % 2], f"psb{tt % 2}"
        P.emit("tensor",
               lambda E, tt=tt, pb=pb, g2T=g2T:
                   E.transpose(pb[:, 0:64], g2T[0:64, tt * 128:(tt + 1) * 128],
                               ch.identb[0:64, 0:64]),
               reads=g2_keys + ("identb",), writes=(pbk,))
        P.emit("vector",
               lambda E, tt=tt, pb=pb, gpv=gpv:
                   E.tensor_copy(gpv[:, tt, 0:64], pb[:, 0:64]),
               reads=(pbk, tag + "gp1"), writes=(f"{tag}gp{tt}",))
    gp_keys = tuple(f"{tag}gp{t_}" for t_ in range(16)) + (tag + "gp1",)

    ddv = a["c_ddt"].rearrange("(c p) f -> p c f", p=128)
    dgt = ch.bf(CDG_O, 4352)
    for fs in range(34):
        dbuf = ch.bf(CW_O + (fs % 2) * 2048, 2048)
        dbk = f"{tag}db{fs % 2}"
        P.emit("gpsimd",
               lambda E, fs=fs, dbuf=dbuf:
                   E.dma_start(dbuf.rearrange("p (c f) -> p c f", f=128),
                               ddv[:, :, fs * 128:(fs + 1) * 128]),
               writes=(dbk,), dma=True)
        bank, bk = ch.psf[fs % 2], f"psf{fs % 2}"
        pieces = [(ch.bf(CG_O + c * 65, 65),
                   dbuf[:, c * 128:(c + 1) * 128]) for c in range(16)]
        P.mm_group(bank[0:65, 0:128], pieces, gp_keys + (dbk,), bk)
        P.emit("vector",
               lambda E, fs=fs, bank=bank, dgt=dgt:
                   E.tensor_copy(dgt[0:65, fs * 128:(fs + 1) * 128],
                                 bank[0:65, 0:128]),
               reads=(bk,), writes=(f"{tag}dgt{fs}",))
    dgt_keys = tuple(f"{tag}dgt{f_}" for f_ in range(34))
    P.barrier()

    w3b = ch.bf(CG1_O, 1024)
    P.emit("gpsimd", lambda E, w3b=w3b: E.dma_start(w3b[0:65, :], a["c_w3p"]),
           writes=(tag + "w3b",), dma=True)
    dfv = a["c_dftt"].rearrange("(c p) f -> p c f", p=128)
    dgt = ch.bf(CDG_O, 4352)
    for fp in range(17):
        for half, ft in enumerate((fp, 17 + fp)):
            dbuf = ch.bf(CW_O + (ft % 2) * 2048, 2048)
            dbk = f"{tag}db{ft % 2}"
            P.emit("gpsimd",
                   lambda E, ft=ft, dbuf=dbuf:
                       E.dma_start(dbuf.rearrange("p (c f) -> p c f", f=128),
                                   dfv[:, :, ft * 128:(ft + 1) * 128]),
                   writes=(dbk,) + (uvw_keys if fp == 0 else ()), dma=True)
            for eta in range(2):
                bank = ch.psf[2 * half + eta]
                bk = f"psf{2 * half + eta}"
                pieces = [(dbuf[:, c * 128:(c + 1) * 128],
                           ch.bf(CV_O + c * 1024 + eta * 512, 512))
                          for c in range(16)]
                P.mm_group(bank[:, 0:512], pieces, (dbk,) + v_keys, bk)
                P.emit("vector",
                       lambda E, ft=ft, eta=eta, bank=bank:
                           E.tensor_copy(ch.bf(CVF_O + ft * 1024 + eta * 512, 512),
                                         bank[:, 0:512]),
                       reads=(bk,), writes=(f"{tag}vf{ft}_{eta}",))
        for eta in range(2):
            hr, hrk = ch.psf[4], "psf4"
            hi, hik = ch.psf[5], "psf5"
            P.emit("tensor",
                   lambda E, hr=hr, fp=fp, eta=eta, w3b=w3b, dgt=dgt:
                       E.matmul(hr[:, 0:512], dgt[0:65, fp * 128:fp * 128 + 128],
                                w3b[0:65, eta * 512:(eta + 1) * 512],
                                start=True, stop=True),
                   reads=dgt_keys + (tag + "w3b",), writes=(hrk,))
            P.emit("tensor",
                   lambda E, hi=hi, fp=fp, eta=eta, w3b=w3b, dgt=dgt:
                       E.matmul(hi[:, 0:512],
                                dgt[0:65, NF + fp * 128:NF + fp * 128 + 128],
                                w3b[0:65, eta * 512:(eta + 1) * 512],
                                start=True, stop=True),
                   reads=dgt_keys + (tag + "w3b",), writes=(hik,))
            vrs = ch.bf(CVF_O + fp * 1024 + eta * 512, 512)
            vis = ch.bf(CVF_O + (17 + fp) * 1024 + eta * 512, 512)
            vrk = f"{tag}vf{fp}_{eta}"
            vik = f"{tag}vf{17 + fp}_{eta}"
            s1 = ch.ff(CS1_O + eta * 512, 512)
            s2 = ch.ff(CS2_O + eta * 512, 512)
            s1k, s2k = f"{tag}s1{eta}", f"{tag}s2{eta}"
            P.emit("vector", lambda E, s1=s1, vrs=vrs, hr=hr:
                   E.tensor_tensor(s1, vrs, hr[:, 0:512], op=ALU.mult),
                   reads=(vrk, hrk), writes=(s1k,))
            P.emit("vector", lambda E, s2=s2, vrs=vrs, hi=hi:
                   E.tensor_tensor(s2, vrs, hi[:, 0:512], op=ALU.mult),
                   reads=(vrk, hik), writes=(s2k,))
            P.emit("vector", lambda E, vrs=vrs, vis=vis, hi=hi:
                   E.tensor_tensor(vrs, vis, hi[:, 0:512], op=ALU.mult),
                   reads=(vik, hik, s1k, s2k), writes=(vrk,))
            P.emit("vector", lambda E, vrs=vrs, s1=s1:
                   E.tensor_sub(vrs, s1, vrs),
                   reads=(vrk, s1k), writes=(vrk,))
            P.emit("vector", lambda E, s1=s1, vis=vis, hr=hr:
                   E.tensor_tensor(s1, vis, hr[:, 0:512], op=ALU.mult),
                   reads=(vik, hrk, s1k), writes=(s1k,))
            P.emit("vector", lambda E, vis=vis, s1=s1, s2=s2:
                   E.tensor_add(vis, s2, s1),
                   reads=(s1k, s2k, vik), writes=(vik,))
    P.barrier()

    icv = a["c_ict"].rearrange("(c p) t -> p c t", p=128)
    ict1 = ch.bf(CW_O, 2176)
    ict2 = ch.bf(CW_O + 2176, 2176)
    vf_keys = tuple(f"{tag}vf{f_}_{e_}" for f_ in range(34) for e_ in range(2))
    for ts in range(16):
        for half, (buf, bufk, c0) in enumerate(((ict1, f"{tag}ic1", 0),
                                                (ict2, f"{tag}ic2", 17))):
            P.emit("gpsimd",
                   lambda E, ts=ts, buf=buf, c0=c0:
                       E.dma_start(buf.rearrange("p (c t) -> p c t", t=128),
                                   icv[:, c0:c0 + 17, ts * 128:(ts + 1) * 128]),
                   writes=(bufk,), dma=True)
        for eta in range(2):
            bank = ch.psf[4 + eta]
            bk = f"psf{4 + eta}"
            p1 = [(ict1[:, c * 128:(c + 1) * 128],
                   ch.bf(CVF_O + c * 1024 + eta * 512, 512)) for c in range(17)]
            p2 = [(ict2[:, c * 128:(c + 1) * 128],
                   ch.bf(CVF_O + (17 + c) * 1024 + eta * 512, 512))
                  for c in range(17)]
            for i, (lh, rh) in enumerate(p1):
                P.emit("tensor",
                       lambda E, lh=lh, rh=rh, i=i, bank=bank:
                           E.matmul(bank[:, 0:512], lh, rh, start=(i == 0),
                                    stop=False),
                       reads=((f"{tag}ic1",) + vf_keys) if i in (0, 16) else (),
                       writes=(bk,) if i == 0 else (), inc=(i == 16))
            for i, (lh, rh) in enumerate(p2):
                P.emit("tensor",
                       lambda E, lh=lh, rh=rh, i=i, bank=bank:
                           E.matmul(bank[:, 0:512], lh, rh, start=False,
                                    stop=(i == 16)),
                       reads=((f"{tag}ic2",) + vf_keys) if i in (0, 16) else (),
                       writes=(bk,) if i in (0, 16) else (), inc=(i == 16))
            P.emit("vector",
                   lambda E, bank=bank, ts=ts, eta=eta:
                       E.tensor_tensor(ch.bf(CG1_O + 2048 + (ts % 2) * 1024
                                             + eta * 512, 512),
                                       bank[:, 0:512],
                                       ch.bf(CU_O + ts * 1024 + eta * 512, 512),
                                       op=ALU.mult),
                   reads=(bk,) + u_keys, writes=(f"{tag}yb{ts % 2}_{eta}",))
        ybk = (f"{tag}yb{ts % 2}_0", f"{tag}yb{ts % 2}_1")
        for g in range(2):
            pb, pbk = ch.psb[g], f"psb{g}"
            for c4 in range(4):
                c = g * 4 + c4
                P.emit("tensor",
                       lambda E, c=c, c4=c4, pb=pb, ts=ts:
                           E.transpose(pb[:, c4 * 128:(c4 + 1) * 128],
                                       ch.bf(CG1_O + 2048 + (ts % 2) * 1024
                                             + c * 128, 128), ch.identb),
                       reads=(ybk + ("identb",)) if c4 in (0, 3) else (),
                       writes=(pbk,) if c4 in (0, 3) else (), inc=(c4 == 3))
            ytv = ch.bf(CV_O, 16384).rearrange("p (c t) -> p c t", t=2048)
            P.emit("vector",
                   lambda E, g=g, pb=pb, ts=ts, ytv=ytv:
                       E.tensor_copy(ytv[:, g * 4:(g + 1) * 4,
                                         ts * 128:(ts + 1) * 128],
                                     pb.rearrange("p (c t) -> p c t", t=128)[:, 0:4, :]),
                   reads=(pbk,) + (v_keys if ts == 0 else ()),
                   writes=(f"{tag}yT{ts}_{g}",))
    wov = a["c_wo2"].rearrange("(c p) m -> p c m", p=128)
    wo = ch.bf(CU_O, 8192)
    wok = f"{tag}wo"
    P.emit("gpsimd", lambda E, wo=wo: E.dma_start(wo, wov[:, :, :]),
           writes=(wok,) + u_keys, dma=True)
    P.barrier()

    yT_keys = tuple(f"{tag}yT{t_}_{g_}" for t_ in range(16) for g_ in range(2))
    for qt in range(16):
        par = qt % 2
        for eta in range(2):
            bank = ch.psf[par * 2 + eta]
            bk = f"psf{par * 2 + eta}"
            pieces = [(ch.bf(CV_O + c * 2048 + qt * 128, 128),
                       wo[:, c * 1024 + eta * 512:c * 1024 + eta * 512 + 512])
                      for c in range(8)]
            P.mm_group(bank[:, 0:512], pieces, (wok,) + yT_keys, bk)
            P.emit("vector",
                   lambda E, bank=bank, eta=eta, par=par:
                       E.tensor_tensor(ch.ff(CXN_O + par * 1024 + eta * 512, 512),
                                       bank[:, 0:512],
                                       boF[:, eta * 512:eta * 512 + 512],
                                       op=ALU.add),
                   reads=(bk, tag + "bo"), writes=(f"{tag}on{par}",))
        onat = ch.ff(CXN_O + par * 1024, 1024)
        onk = f"{tag}on{par}"
        sc = ch.ff(CSC_O + par * 1040, 1040)
        sck = f"{tag}fsc{par}"
        mean = ch.ff(CMN_O + 4 + par, 1)
        rstd = ch.ff(CMN_O + 6 + par, 1)
        skey = f"{tag}fst{par}"
        ln_stats(P, onat, (onk,), mean, rstd, skey, sc, sck, eps)
        lnout = ch.ff(CS1_O, 1024)
        lnk = f"{tag}flno"
        P.emit("vector",
               lambda E, onat=onat, mean=mean, rstd=rstd, lnout=lnout:
                   E.tensor_scalar(lnout, onat, mean, rstd,
                                   op0=ALU.subtract, op1=ALU.mult),
               reads=(onk, skey + "_m", skey + "_r"),
               writes=(lnk, f"{tag}s10", f"{tag}s11"))
        xbuf = ch.ff(CS2_O, 1024)
        xk = f"{tag}fxr"
        P.emit("sync",
               lambda E, qt=qt, xbuf=xbuf: E.dma_start(xbuf, x_in_t[qt]),
               reads=(f"xres{qt}",), writes=(xk, f"{tag}s20", f"{tag}s21"),
               dma=True)
        P.emit("vector",
               lambda E, xbuf=xbuf, lnout=lnout: E.tensor_add(xbuf, xbuf, lnout),
               reads=(xk, lnk), writes=(xk,))
        P.emit("sync",
               lambda E, qt=qt, xbuf=xbuf: E.dma_start(x_out_t[qt], xbuf),
               reads=(xk,), writes=(f"xres{qt}",), dma=True)
    if tail_preload is not None:
        tail_preload()
    P.barrier()


def build_program_c():
    nc = bass.Bass("TRN2", target_bir_lowering=False, debug=False)
    x_in = nc.dram_tensor("x_in", [T, D], F32, kind="ExternalInput").ap()
    x_out = nc.dram_tensor("x_out", [T, D], F32, kind="ExternalOutput").ap()
    a = declare_dram_c(nc)
    a["c_ident"] = nc.dram_tensor("c_ident", [128, 128], F32,
                                  kind="ExternalInput").ap()
    P = PB(nc)
    ch = Chip(nc)
    ch.load_consts(P, a)
    build_stage_c(P, ch, a, x_in, x_out)
    P.barrier()
    P.build()
    return nc


def run_stage_c(x, inputs, cores=1):
    nc = build_program_c()
    shared = host_prep_c(inputs)
    shared.update(_host_consts())
    in_maps = []
    for c in range(cores):
        m = dict(shared)
        m["x_in"] = np.ascontiguousarray(x[c])
        in_maps.append(m)
    res = run_bass_kernel_spmd(nc, in_maps, core_ids=list(range(cores)))
    return np.stack([r["x_out"] for r in res.results])


from concourse.bass_core_rust import IndirectOffsetOnAxis

BS_XT, BS_XET, BS_H, BS_W, BS_XB, BS_XE = 0, 8192, 12288, 28672, 61440, 63488
BS_CNT, BS_GS, BS_C12, BS_IOJ, BS_SLT, BS_PFV, BS_TOK, BS_IOC, BS_EQ1, BS_EQ2 = \
    7516, 7580, 7582, 7584, 7585, 7593, 7601, 7609, 7610, 7618
BS_TOKG = 7626
BS_GS2 = 7631
BS_GSA = 7634


def build_stage_b2(P, ch, a, x_in, x_out, moe_scratch, xh_scratch,
                   toke_scratch, eo_scratch, tag="B"):
    nc = P.nc
    x_in_t = x_in.rearrange("(n p) d -> n p d", p=128)
    x_out_t = x_out.rearrange("(n p) d -> n p d", p=128)
    moe_t = moe_scratch.rearrange("(n p) d -> n p d", p=128)
    eps = ch.eps_ap()
    rwv = a["b_rwT"].rearrange("(c p) e -> p c e", p=128)
    wupv = a["b_wupT"].rearrange("e (c p) f -> p e c f", p=128)
    wdnv = a["b_wdnT"].rearrange("e (c p) f -> p e c f", p=128)
    btv = toke_scratch.rearrange("(c p) o -> p c o", p=128)
    pfxc = ch.outT[:, 1024:1280]

    P.emit("sync", lambda E: E.dma_start(
               ch.ff(BRW_O, 64).rearrange("p (c e) -> p c e", e=8), rwv),
           writes=(tag + "rw",), dma=True)
    P.emit("sync", lambda E: E.dma_start(ch.ff(BIO_O, 8), a["c_iota8"]),
           writes=(tag + "iota",), dma=True)
    P.emit("sync", lambda E: E.dma_start(ch.ff(BKMI_O, 8), a["c_kmi8"]),
           writes=(tag + "kmi",), dma=True)
    P.emit("sync", lambda E: E.dma_start(ch.rbrow, a["b_rbrow"]),
           writes=(tag + "rb",), dma=True)
    P.emit("sync", lambda E: E.dma_start(pfxc, a["c_pfx"]),
           writes=(tag + "pfxc",), dma=True)
    P.emit("sync", lambda E: E.dma_start(ch.ff(BS_IOC, 1), a["c_iotacol"]),
           writes=(tag + "ioc",), dma=True)

    for th in range(2):
        P.emit("gpsimd", lambda E: E.dma_start(toke_scratch, a["c_huge"]),
               writes=(tag + "tokeinit",) + tuple(
                   f"{tag}tk{e_}_{j_}" for e_ in range(8) for j_ in range(8)),
               dma=True)
        for j in range(8):
            tt = th * 8 + j
            par = j % 2
            xbuf = ch.ff(BXN_O + par * 1024, 1024)
            xk = f"{tag}x{par}"
            P.emit("sync", lambda E, tt=tt, xbuf=xbuf: E.dma_start(xbuf, x_in_t[tt]),
                   reads=(f"xres{tt}",), writes=(xk,), dma=True)
            sc = ch.ff(BSC_O + par * 1040, 1040)
            sck = f"{tag}sc{par}"
            mean = ch.ff(BMN_O + par, 1)
            rstd = ch.ff(BMN_O + 2 + par, 1)
            skey = f"{tag}st{par}"
            ln_stats(P, xbuf, (xk,), mean, rstd, skey, sc, sck, eps)
            xhb = ch.bf(BS_XB + par * 1024, 1024)
            xhbk = f"{tag}xhb{par}"
            P.emit("vector",
                   lambda E, xhb=xhb, xbuf=xbuf, mean=mean, rstd=rstd:
                       E.tensor_scalar(xhb, xbuf, mean, rstd,
                                       op0=ALU.subtract, op1=ALU.mult),
                   reads=(xk, skey + "_m", skey + "_r"), writes=(xhbk,))
            xhf = ch.ff(BXF_O, 1024)
            xhfk = f"{tag}xhf"
            P.emit("vector",
                   lambda E, xhf=xhf, xbuf=xbuf, mean=mean, rstd=rstd:
                       E.tensor_scalar(xhf, xbuf, mean, rstd,
                                       op0=ALU.subtract, op1=ALU.mult),
                   reads=(xk, skey + "_m", skey + "_r"), writes=(xhfk,))
            xhv = xh_scratch.rearrange("(n p) d -> n p d", p=128)
            P.emit("sync",
                   lambda E, j=j, xhb=xhb, xhv=xhv:
                       E.dma_start(xhv[j], xhb.bitcast(F32)),
                   reads=(xhbk,), writes=(f"{tag}xh{j}",), dma=True)
            for g in range(2):
                pb, pbk = ch.psb[g], f"psb{g}"
                for c4 in range(4):
                    c = g * 4 + c4
                    P.emit("tensor",
                           lambda E, c=c, c4=c4, pb=pb, par=par:
                               E.transpose(pb[:, c4 * 128:(c4 + 1) * 128],
                                           ch.bf(BS_XB + par * 1024 + c * 128, 128),
                                           ch.identb),
                           reads=(xhbk, "identb") if c4 in (0, 3) else (),
                           writes=(pbk,) if c4 in (0, 3) else (), inc=(c4 == 3))
                xtv = ch.bf(BS_XT, 8192).rearrange("p (c t) -> p c t", t=1024)
                P.emit("vector",
                       lambda E, g=g, j=j, pb=pb, xtv=xtv:
                           E.tensor_copy(xtv[:, g * 4:(g + 1) * 4,
                                             j * 128:(j + 1) * 128],
                                         pb.rearrange("p (c t) -> p c t", t=128)[:, 0:4, :]),
                       reads=(pbk,), writes=(f"{tag}xT_{j}_{g}",))
            xtf = ch.outT[:, 0:1024]
            for g in range(2):
                bank, bk = ch.psf[4 + g], f"psf{4 + g}"
                for c4 in range(4):
                    c = g * 4 + c4
                    P.emit("tensor",
                           lambda E, c=c, c4=c4, bank=bank, xhf=xhf:
                               E.transpose(bank[:, c4 * 128:(c4 + 1) * 128],
                                           xhf[:, c * 128:(c + 1) * 128],
                                           ch.identf),
                           reads=(xhfk, "identf") if c4 in (0, 3) else (),
                           writes=(bk,) if c4 in (0, 3) else (), inc=(c4 == 3))
                P.emit("vector",
                       lambda E, g=g, bank=bank, xtf=xtf:
                           E.tensor_copy(xtf[:, g * 512:(g + 1) * 512],
                                         bank[:, 0:512]),
                       reads=(bk,), writes=(f"{tag}xtf{g}",))
            rb_ = ch.psf[0]
            rbk = "psf0"
            pieces = [(xtf[:, c * 128:(c + 1) * 128], ch.ff(BRW_O + c * 8, 8))
                      for c in range(8)]
            for i, (lh, rh) in enumerate(pieces):
                P.emit("tensor",
                       lambda E, lh=lh, rh=rh, i=i, rb_=rb_:
                           E.matmul(rb_[:, 0:8], lh, rh, start=(i == 0), stop=False),
                       reads=((f"{tag}xtf0", f"{tag}xtf1", tag + "rw", tag + "rb")
                              if i == 0 else ()),
                       writes=(rbk,) if i == 0 else (), inc=False)
            P.emit("tensor",
                   lambda E, rb_=rb_: E.matmul(rb_[:, 0:8], ch.onesf[0:1, :],
                                               ch.rbrow[0:1, :], start=False,
                                               stop=True),
                   reads=(f"{tag}xtf0", f"{tag}xtf1", tag + "rw", tag + "rb"),
                   writes=(rbk,))
            eb = ch.ff(BLG_O, 8)
            P.emit("scalar", lambda E, eb=eb, rb_=rb_:
                   E.activation(eb, rb_[:, 0:8], AF.Exp),
                   reads=(rbk,), writes=(tag + "eb",))
            c_m1 = ch.ff(BNR_O, 1); c_i1 = ch.ff(BNR_O + 1, 1)
            c_m2 = ch.ff(BNR_O + 2, 1); c_rd = ch.ff(BNR_O + 3, 1)
            w8a = ch.ff(BLG_O + 8, 8); w8b = ch.ff(BLG_O + 16, 8)
            eq1b = ch.ff(BS_EQ1, 8); eq2b = ch.ff(BS_EQ2, 8)
            cmb = ch.ff(BCB_O + tt * 8, 8)
            def ve(fn, rd, wr):
                P.emit("vector", fn, reads=rd, writes=wr)
            ve(lambda E, eb=eb, c_m1=c_m1: E.reduce_max(c_m1, eb, axis=AX.X),
               (tag + "eb",), (tag + "m1",))
            ve(lambda E, w8a=w8a, eb=eb, c_m1=c_m1:
               E.tensor_scalar(w8a, eb, c_m1, None, op0=ALU.is_equal),
               (tag + "eb", tag + "m1"), (tag + "w8a",))
            ve(lambda E, w8a=w8a: E.tensor_mul(w8a, w8a, ch.ff(BKMI_O, 8)),
               (tag + "w8a", tag + "kmi"), (tag + "w8a",))
            ve(lambda E, w8a=w8a: E.tensor_scalar(w8a, w8a, -1e4, None, op0=ALU.add),
               (tag + "w8a",), (tag + "w8a",))
            ve(lambda E, c_i1=c_i1, w8a=w8a: E.reduce_max(c_i1, w8a, axis=AX.X),
               (tag + "w8a",), (tag + "i1",))
            ve(lambda E, c_i1=c_i1: E.tensor_scalar(c_i1, c_i1, -1.0, None, op0=ALU.mult),
               (tag + "i1",), (tag + "i1",))
            ve(lambda E, eq1b=eq1b, c_i1=c_i1:
               E.tensor_scalar(eq1b, ch.ff(BIO_O, 8), c_i1, None, op0=ALU.is_equal),
               (tag + "iota", tag + "i1"), (tag + "eq1b",))
            ve(lambda E, w8b=w8b, eb=eb: E.tensor_scalar(w8b, eb, 1e4, None, op0=ALU.add),
               (tag + "eb",), (tag + "w8b",))
            ve(lambda E, w8b=w8b, eq1b=eq1b: E.tensor_mul(w8b, eq1b, w8b),
               (tag + "eq1b", tag + "w8b"), (tag + "w8b",))
            ve(lambda E, w8b=w8b, eb=eb: E.tensor_sub(w8b, eb, w8b),
               (tag + "eb", tag + "w8b"), (tag + "w8b",))
            ve(lambda E, c_m2=c_m2, w8b=w8b: E.reduce_max(c_m2, w8b, axis=AX.X),
               (tag + "w8b",), (tag + "m2",))
            ve(lambda E, c_rd=c_rd, c_m1=c_m1, c_m2=c_m2: E.tensor_add(c_rd, c_m1, c_m2),
               (tag + "m1", tag + "m2"), (tag + "rd",))
            ve(lambda E, c_rd=c_rd: E.reciprocal(c_rd, c_rd), (tag + "rd",), (tag + "rd",))
            ve(lambda E, eq2b=eq2b, w8b=w8b, c_m2=c_m2:
               E.tensor_scalar(eq2b, w8b, c_m2, None, op0=ALU.is_equal),
               (tag + "w8b", tag + "m2"), (tag + "eq2b",))
            ve(lambda E, w8a=w8a, eq1b=eq1b, c_m1=c_m1:
               E.tensor_scalar(w8a, eq1b, c_m1, None, op0=ALU.mult),
               (tag + "eq1b", tag + "m1"), (tag + "w8a",))
            ve(lambda E, w8b=w8b, eq2b=eq2b, c_m2=c_m2:
               E.tensor_scalar(w8b, eq2b, c_m2, None, op0=ALU.mult),
               (tag + "eq2b", tag + "m2"), (tag + "w8b",))
            ve(lambda E, w8a=w8a, w8b=w8b: E.tensor_add(w8a, w8a, w8b),
               (tag + "w8a", tag + "w8b"), (tag + "w8a",))
            ve(lambda E, cmb=cmb, w8a=w8a, c_rd=c_rd:
               E.tensor_scalar(cmb, w8a, c_rd, None, op0=ALU.mult),
               (tag + "w8a", tag + "rd"), (f"{tag}cb{tt}",))
            cntj = ch.ff(BS_CNT + j * 8, 8)
            ve(lambda E, cntj=cntj, eq1b=eq1b, eq2b=eq2b:
               E.tensor_add(cntj, eq1b, eq2b),
               (tag + "eq1b", tag + "eq2b"), (f"{tag}cnt{j}",))
            pbk2 = "psf1"
            pb2 = ch.psf[1]
            pieces2 = [(pfxc[:, 0:128], ch.ff(BS_CNT + c * 8, 8)) for c in range(j)] \
                + [(pfxc[:, 128:256], cntj)]
            n2 = len(pieces2)
            for i, (lh, rh) in enumerate(pieces2):
                P.emit("tensor",
                       lambda E, lh=lh, rh=rh, i=i, n2=n2, pb2=pb2:
                           E.matmul(pb2[:, 0:8], lh, rh, start=(i == 0),
                                    stop=(i == n2 - 1)),
                       reads=(tuple(f"{tag}cnt{c_}" for c_ in range(j + 1))
                              + (tag + "pfxc",)) if i in (0, n2 - 1) else (),
                       writes=(pbk2,) if i in (0, n2 - 1) else (),
                       inc=(i == n2 - 1))
            pfv = ch.ff(BS_PFV, 8)
            ve(lambda E, pfv=pfv, pb2=pb2: E.tensor_copy(pfv, pb2[:, 0:8]),
               (pbk2,), (tag + "pfv",))
            slt = ch.ff(BS_SLT, 8)
            ve(lambda E, slt=slt, cntj=cntj, pfv=pfv:
               E.scalar_tensor_tensor(slt, cntj, -1e9, pfv, op0=ALU.mult, op1=ALU.add),
               (f"{tag}cnt{j}", tag + "pfv"), (tag + "slt",))
            ve(lambda E, slt=slt: E.tensor_scalar(slt, slt, 1e9, None, op0=ALU.add),
               (tag + "slt",), (tag + "slt",))
            ve(lambda E, w8b=w8b, slt=slt:
               E.tensor_scalar(w8b, slt, 511.5, None, op0=ALU.is_gt),
               (tag + "slt",), (tag + "w8b",))
            ve(lambda E, slt=slt, w8b=w8b:
               E.scalar_tensor_tensor(slt, w8b, 1e9, slt, op0=ALU.mult, op1=ALU.add),
               (tag + "slt", tag + "w8b"), (tag + "slt",))
            ve(lambda E, slt=slt:
               E.scalar_tensor_tensor(slt, ch.ff(BIO_O, 8), 512.0, slt,
                                      op0=ALU.mult, op1=ALU.add),
               (tag + "slt", tag + "iota"), (tag + "slt",))
            gs1 = ch.ff(BS_GSA + 2 * j, 1); gs2 = ch.ff(BS_GSA + 2 * j + 1, 1)
            pair1 = ch.ff(BS_GS, 2); pair2 = ch.ff(BS_C12, 2)
            c1 = ch.ff(BS_GS + 1, 1); c2 = ch.ff(BS_C12 + 1, 1)
            ve(lambda E, w8a=w8a, slt=slt, eq1b=eq1b: E.tensor_mul(w8a, eq1b, slt),
               (tag + "eq1b", tag + "slt"), (tag + "w8a",))
            ve(lambda E, gs1=gs1, w8a=w8a: E.reduce_sum(gs1, w8a, axis=AX.X),
               (tag + "w8a",), (tag + "gs1",))
            ve(lambda E, w8a=w8a, slt=slt, eq2b=eq2b: E.tensor_mul(w8a, eq2b, slt),
               (tag + "eq2b", tag + "slt"), (tag + "w8a",))
            ve(lambda E, gs2=gs2, w8a=w8a: E.reduce_sum(gs2, w8a, axis=AX.X),
               (tag + "w8a",), (tag + "gs2",))
            ve(lambda E, w8a=w8a, cmb=cmb, eq1b=eq1b: E.tensor_mul(w8a, eq1b, cmb),
               (tag + "eq1b", f"{tag}cb{tt}"), (tag + "w8a",))
            ve(lambda E, c1=c1, w8a=w8a: E.reduce_sum(c1, w8a, axis=AX.X),
               (tag + "w8a",), (tag + "c1",))
            ve(lambda E, w8a=w8a, cmb=cmb, eq2b=eq2b: E.tensor_mul(w8a, eq2b, cmb),
               (tag + "eq2b", f"{tag}cb{tt}"), (tag + "w8a",))
            ve(lambda E, c2=c2, w8a=w8a: E.reduce_sum(c2, w8a, axis=AX.X),
               (tag + "w8a",), (tag + "c2",))
            ioj = ch.ff(BS_IOJ, 1)
            ve(lambda E, ioj=ioj, j=j:
               E.tensor_scalar(ioj, ch.ff(BS_IOC, 1), float(j * 128), None,
                               op0=ALU.add),
               (tag + "ioc",), (tag + "ioj",))
            ve(lambda E, ioj=ioj, pair1=pair1: E.tensor_copy(pair1[:, 0:1], ioj),
               (tag + "ioj",), (tag + "p1a",))
            ve(lambda E, ioj=ioj, pair2=pair2: E.tensor_copy(pair2[:, 0:1], ioj),
               (tag + "ioj",), (tag + "p2a",))
            ve(lambda E, gs1=gs1: E.tensor_copy(ch.idxi[:, 0:1], gs1),
               (tag + "gs1",), (tag + "ix1",))
            ve(lambda E, gs2=gs2: E.tensor_copy(ch.idxi[:, 1:2], gs2),
               (tag + "gs2",), (tag + "ix2",))
            tkkeys = tuple(f"{tag}tk{e_}_{j}" for e_ in range(8))
            P.emit("gpsimd",
                   lambda E, pair1=pair1:
                       E.indirect_dma_start(toke_scratch,
                                            IndirectOffsetOnAxis(ch.idxi[:, 0:1], 0),
                                            pair1, None, bounds_check=4095,
                                            oob_is_err=False),
                   reads=(tag + "ix1", tag + "p1a", tag + "c1", tag + "tokeinit"),
                   writes=tkkeys, dma=True)
            P.emit("gpsimd",
                   lambda E, pair2=pair2:
                       E.indirect_dma_start(toke_scratch,
                                            IndirectOffsetOnAxis(ch.idxi[:, 1:2], 0),
                                            pair2, None, bounds_check=4095,
                                            oob_is_err=False),
                   reads=(tag + "ix2", tag + "p2a", tag + "c2"),
                   writes=tkkeys, dma=True)
        xT_keys = tuple(f"{tag}xT_{j}_{g}" for j in range(8) for g in range(2))
        tk_all = tuple(f"{tag}tk{e_}_{j_}" for e_ in range(8) for j_ in range(8))
        ce_all = tuple(f"{tag}ce{j_}" for j_ in range(8))

        wup = ch.bf(BS_W, 16384)
        wdn = ch.bf(BS_W + 16384, 16384)
        wupk, wdnk = f"{tag}wslot0", f"{tag}wslot1"
        P.emit("gpsimd",
               lambda E, wup=wup:
                   E.dma_start(wup.rearrange("p (c f) -> p c f", f=2048),
                               wupv[:, 8, :, :]),
               writes=(wupk,), dma=True)
        P.emit("gpsimd",
               lambda E, wdn=wdn:
                   E.dma_start(wdn.rearrange("p (c f) -> p c f", f=1024),
                               wdnv[:, 8, :, :]),
               writes=(wdnk,), dma=True)
        hsh = ch.bf(BS_H, 16384)
        for ft in range(16):
            for eta in range(2):
                bank = ch.psf[(ft % 2) * 2 + eta]
                bk = f"psf{(ft % 2) * 2 + eta}"
                pieces = [(wup[:, c * 2048 + ft * 128:c * 2048 + ft * 128 + 128],
                           ch.bf(BS_XT + c * 1024 + eta * 512, 512))
                          for c in range(8)]
                P.mm_group(bank[:, 0:512], pieces, (wupk,) + xT_keys, bk)
                P.emit("scalar",
                       lambda E, bank=bank, ft=ft, eta=eta, hsh=hsh:
                           E.activation(hsh[:, ft * 1024 + eta * 512:
                                            ft * 1024 + eta * 512 + 512],
                                        bank[:, 0:512], AF.Silu),
                       reads=(bk,), writes=(f"{tag}hs{ft}_{eta}",))
        hs_keys = tuple(f"{tag}hs{f_}_{e_}" for f_ in range(16) for e_ in range(2))
        for tt in range(8):
            gtt = th * 8 + tt
            stg = ch.ff(BST_O + (tt % 2) * 1024, 1024)
            stgk = f"{tag}stg{tt % 2}"
            for eta in range(2):
                bank = ch.psf[4 + eta]
                bk = f"psf{4 + eta}"
                pieces = [(hsh[:, c * 1024 + tt * 128:c * 1024 + tt * 128 + 128],
                           wdn[:, c * 1024 + eta * 512:c * 1024 + eta * 512 + 512])
                          for c in range(16)]
                P.mm_group(bank[:, 0:512], pieces, (wdnk,) + hs_keys, bk)
                P.emit("vector",
                       lambda E, bank=bank, eta=eta, stg=stg:
                           E.tensor_copy(stg[:, eta * 512:eta * 512 + 512],
                                         bank[:, 0:512]),
                       reads=(bk,), writes=(stgk,))
            P.emit("gpsimd",
                   lambda E, stg=stg, gtt=gtt:
                       E.dma_start(moe_t[gtt], stg, accum_op=ALU.bypass),
                   reads=(stgk,), writes=(f"{tag}moe{gtt}",), dma=True)
        moe_keys_th = tuple(f"{tag}moe{th * 8 + t_}" for t_ in range(8))

        prev_scatter = ()
        for e in range(8):
            par_e = e % 2
            wup = ch.bf(BS_W + par_e * 16384, 16384)
            wdn = ch.bf(BS_W + (1 - par_e) * 16384, 16384)
            wupk = f"{tag}wslot{par_e}"
            wdnk = f"{tag}wslot{1 - par_e}"
            P.emit("gpsimd",
                   lambda E, e=e, wup=wup:
                       E.dma_start(wup.rearrange("p (c f) -> p c f", f=2048),
                                   wupv[:, e, :, :]),
                   writes=(wupk,), dma=True)
            P.emit("gpsimd",
                   lambda E, e=e, wdn=wdn:
                       E.dma_start(wdn.rearrange("p (c f) -> p c f", f=1024),
                                   wdnv[:, e, :, :]),
                   writes=(wdnk,), dma=True)
            tcf = ch.ff(BS_TOK, 8)
            tcv = tcf.rearrange("p (c o) -> p c o", o=2)
            P.emit("gpsimd",
                   lambda E, e=e, tcv=tcv:
                       E.dma_start(tcv, btv[:, 4 * e:4 * e + 4, :],
                                   allow_slow_non_contiguous=True),
                   reads=tk_all, writes=(f"{tag}tokf", f"{tag}cef"), dma=True)
            P.emit("vector",
                   lambda E, tcv=tcv: E.tensor_copy(
                       ch.idxi[:, 4:8].rearrange("p (c o) -> p c o", o=1),
                       tcv[:, :, 0:1]),
                   reads=(f"{tag}tokf",), writes=(f"{tag}toki",))

            for ct in range(4):
                xe = ch.bf(BS_XE + (ct % 2) * 1024, 1024)
                xek = f"{tag}xe{ct % 2}"
                P.emit("gpsimd",
                       lambda E, ct=ct, xe=xe:
                           E.indirect_dma_start(xe.bitcast(F32), None, xh_scratch,
                                                IndirectOffsetOnAxis(
                                                    ch.idxi[:, 4 + ct:5 + ct], 0),
                                                bounds_check=1023,
                                                oob_is_err=False),
                       reads=(f"{tag}toki",) + tuple(f"{tag}xh{j_}" for j_ in range(8)),
                       writes=(xek,), dma=True)
                for g in range(2):
                    pb, pbk = ch.psb[g], f"psb{g}"
                    for c4 in range(4):
                        c = g * 4 + c4
                        P.emit("tensor",
                               lambda E, c=c, c4=c4, pb=pb, xe=xe:
                                   E.transpose(pb[:, c4 * 128:(c4 + 1) * 128],
                                               xe[:, c * 128:(c + 1) * 128],
                                               ch.identb),
                               reads=(xek, "identb") if c4 in (0, 3) else (),
                               writes=(pbk,) if c4 in (0, 3) else (), inc=(c4 == 3))
                    xetv = ch.bf(BS_XET, 4096).rearrange("p (c t) -> p c t", t=512)
                    P.emit("vector",
                           lambda E, g=g, ct=ct, pb=pb, xetv=xetv:
                               E.tensor_copy(xetv[:, g * 4:(g + 1) * 4,
                                                 ct * 128:(ct + 1) * 128],
                                             pb.rearrange("p (c t) -> p c t",
                                                          t=128)[:, 0:4, :]),
                           reads=(pbk,), writes=(f"{tag}xet{ct}_{g}",))
            xet_keys = tuple(f"{tag}xet{c_}_{g_}" for c_ in range(4) for g_ in range(2))
            hsp = ch.bf(BS_H + par_e * 8192, 8192)
            for ft in range(16):
                bank = ch.psf[ft % 4]
                bk = f"psf{ft % 4}"
                pieces = [(wup[:, c * 2048 + ft * 128:c * 2048 + ft * 128 + 128],
                           ch.bf(BS_XET + c * 512, 512))
                          for c in range(8)]
                P.mm_group(bank[:, 0:512], pieces, (wupk,) + xet_keys, bk)
                hs_alias = (tuple(f"{tag}hs{8 * par_e + f2}_{e2}"
                                  for f2 in range(8) for e2 in range(2))
                            if e < 2 else ())
                P.emit("scalar",
                       lambda E, bank=bank, ft=ft, hsp=hsp:
                           E.activation(hsp[:, ft * 512:(ft + 1) * 512],
                                        bank[:, 0:512], AF.Silu),
                       reads=(bk,), writes=(f"{tag}he{par_e}_{ft}",) + hs_alias)
            he_keys = tuple(f"{tag}he{par_e}_{f_}" for f_ in range(16))
            scat_keys = []
            for ct in range(4):
                stg = ch.ff(BST_O + (ct % 2) * 1024, 1024)
                stgk = f"{tag}stg{ct % 2}"
                for eta in range(2):
                    bank = ch.psf[4 + eta]
                    bk = f"psf{4 + eta}"
                    pieces = [(hsp[:, c * 512 + ct * 128:c * 512 + ct * 128 + 128],
                               wdn[:, c * 1024 + eta * 512:c * 1024 + eta * 512 + 512])
                              for c in range(16)]
                    P.mm_group(bank[:, 0:512], pieces, (wdnk,) + he_keys, bk)
                    P.emit("vector",
                           lambda E, bank=bank, eta=eta, stg=stg, ct=ct, tcf=tcf:
                               E.tensor_scalar(stg[:, eta * 512:eta * 512 + 512],
                                               bank[:, 0:512],
                                               tcf[:, 2 * ct + 1:2 * ct + 2], None,
                                               op0=ALU.mult),
                           reads=(bk, f"{tag}cef"), writes=(stgk,))
                kk = f"{tag}eo{e}_{ct}"
                eov = eo_scratch.rearrange("(n p) d -> n p d", p=128)
                P.emit("gpsimd",
                       lambda E, stg=stg, ct=ct, e=e, eov=eov:
                           E.dma_start(eov[4 * e + ct], stg),
                       reads=(stgk,), writes=(kk,), dma=True)
                scat_keys.append(kk)
            prev_scatter = prev_scatter + tuple(scat_keys)

        eo_all = prev_scatter
        for j in range(8):
            tt = th * 8 + j
            par = j % 2
            obuf = ch.ff(BXN_O + par * 1024, 1024)
            obk = f"{tag}ob{par}"
            P.emit("sync",
                   lambda E, tt=tt, obuf=obuf: E.dma_start(obuf, moe_t[tt]),
                   reads=(f"{tag}moe{tt}",), writes=(obk,), dma=True)
            g1 = ch.outT[:, 1280:2304]
            g2 = ch.outT[:, 2304:3328]
            P.emit("vector", lambda E, g1=g1: E.memset(g1, 0.0),
                   writes=(tag + "g1",))
            P.emit("vector", lambda E, g2=g2: E.memset(g2, 0.0),
                   writes=(tag + "g2",))
            P.emit("vector",
                   lambda E, j=j: E.tensor_copy(ch.idxg[0], ch.ff(BS_GSA + 2 * j, 1)),
                   reads=(f"{tag}gs1",), writes=(tag + "gi1",))
            P.emit("vector",
                   lambda E, j=j: E.tensor_copy(ch.idxg[1], ch.ff(BS_GSA + 2 * j + 1, 1)),
                   reads=(f"{tag}gs2",), writes=(tag + "gi2",))
            P.emit("gpsimd",
                   lambda E, g1=g1:
                       E.indirect_dma_start(g1, None, eo_scratch,
                                            IndirectOffsetOnAxis(ch.idxg[0], 0),
                                            bounds_check=4095, oob_is_err=False),
                   reads=(tag + "gi1", tag + "g1") + eo_all,
                   writes=(tag + "g1",), dma=True)
            P.emit("gpsimd",
                   lambda E, g2=g2:
                       E.indirect_dma_start(g2, None, eo_scratch,
                                            IndirectOffsetOnAxis(ch.idxg[1], 0),
                                            bounds_check=4095, oob_is_err=False),
                   reads=(tag + "gi2", tag + "g2") + eo_all,
                   writes=(tag + "g2",), dma=True)
            P.emit("vector",
                   lambda E, obuf=obuf, g1=g1: E.tensor_add(obuf, obuf, g1),
                   reads=(obk, tag + "g1"), writes=(obk,))
            P.emit("vector",
                   lambda E, obuf=obuf, g2=g2: E.tensor_add(obuf, obuf, g2),
                   reads=(obk, tag + "g2"), writes=(obk,))
            sc = ch.ff(BSC_O + par * 1040, 1040)
            sck = f"{tag}fsc{par}"
            mean = ch.ff(BMN_O + 4 + par, 1)
            rstd = ch.ff(BMN_O + 6 + par, 1)
            skey = f"{tag}fst{par}"
            ln_stats(P, obuf, (obk,), mean, rstd, skey, sc, sck, eps)
            lnout = ch.ff(BXF_O, 1024)
            lnk = f"{tag}flno"
            P.emit("vector",
                   lambda E, obuf=obuf, mean=mean, rstd=rstd, lnout=lnout:
                       E.tensor_scalar(lnout, obuf, mean, rstd,
                                       op0=ALU.subtract, op1=ALU.mult),
                   reads=(obk, skey + "_m", skey + "_r"),
                   writes=(lnk, f"{tag}xhf"),)
            xbuf = ch.ff(BST_O, 1024)
            xk2 = f"{tag}fxr"
            P.emit("sync",
                   lambda E, tt=tt, xbuf=xbuf: E.dma_start(xbuf, x_in_t[tt]),
                   reads=(f"xres{tt}",), writes=(xk2, f"{tag}stg0"), dma=True)
            P.emit("vector",
                   lambda E, xbuf=xbuf, lnout=lnout: E.tensor_add(xbuf, xbuf, lnout),
                   reads=(xk2, lnk), writes=(xk2,))
            P.emit("sync",
                   lambda E, tt=tt, xbuf=xbuf: E.dma_start(x_out_t[tt], xbuf),
                   reads=(xk2,), writes=(f"xres{tt}",), dma=True)
        P.barrier()


def build_program_b2():
    nc = bass.Bass("TRN2", target_bir_lowering=False, debug=False)
    x_in = nc.dram_tensor("x_in", [T, D], F32, kind="ExternalInput").ap()
    x_out = nc.dram_tensor("x_out", [T, D], F32, kind="ExternalOutput").ap()
    moe = nc.dram_tensor("b_moe", [T, D], F32, kind="Internal").ap()
    xh = nc.dram_tensor("b_xh", [1024, 512], F32, kind="Internal").ap()
    toke = nc.dram_tensor("b_toke", [4096, 2], F32, kind="Internal").ap()
    eo = nc.dram_tensor("b_eo", [4096, 1024], F32, kind="Internal").ap()
    a = declare_dram_b(nc)
    a["c_ident"] = nc.dram_tensor("c_ident", [128, 128], F32,
                                  kind="ExternalInput").ap()
    P = PB(nc)
    ch = Chip(nc)
    ch.load_consts(P, a)
    build_stage_b2(P, ch, a, x_in, x_out, moe, xh, toke, eo)
    P.barrier()
    P.build()
    return nc


def run_stage_b2(x, inputs, cores=1):
    nc = build_program_b2()
    shared = host_prep_b(inputs)
    shared.update(_host_consts())
    in_maps = []
    for c in range(cores):
        m = dict(shared)
        m["x_in"] = np.ascontiguousarray(x[c])
        in_maps.append(m)
    res = run_bass_kernel_spmd(nc, in_maps, core_ids=list(range(cores)))
    return np.stack([r["x_out"] for r in res.results])


FUSED = True


def build_program_full():
    nc = bass.Bass("TRN2", target_bir_lowering=False, debug=False)
    x_in = nc.dram_tensor("x_in", [T, D], F32, kind="ExternalInput").ap()
    x_out = nc.dram_tensor("x_out", [T, D], F32, kind="ExternalOutput").ap()
    moe = nc.dram_tensor("b_moe", [T, D], F32, kind="Internal").ap()
    a = {}
    a.update(declare_dram_a(nc))
    a.update(declare_dram_b(nc))
    a.update(declare_dram_c(nc))
    a.update(declare_dram_d(nc))
    a["c_ident"] = nc.dram_tensor("c_ident", [128, 128], F32,
                                  kind="ExternalInput").ap()
    P = PB(nc)
    ch = Chip(nc)
    ch.load_consts(P, a)

    def preload_c():
        uvv = a["c_uvw"].rearrange("(c p) o -> p c o", p=128)
        bkeys = ("Bwup", "Bwdn") + tuple(f"Bh1_{f_}_{e_}" for f_ in range(16)
                                         for e_ in range(2))
        for c in range(8):
            P.emit("gpsimd",
                   lambda E, c=c, uvv=uvv:
                       E.dma_start(ch.bf(CVF_O + c * 2048, 2048), uvv[:, c, :]),
                   writes=(f"Cuvw{c}",) + (bkeys if c == 0 else ()), dma=True)

    def preload_d():
        for off, name in ((S1_O, "d_s1"), (S2_O, "d_s2"), (SH1_O, "d_sh1"),
                          (SH2_O, "d_sh2")):
            P.emit("gpsimd",
                   lambda E, off=off, name=name:
                       E.dma_start(ch.ff(off, 32), a[name]),
                   writes=("D" + name,), dma=True)
        P.emit("gpsimd", lambda E: E.dma_start(ch.ff(SW3_O, 8), a["d_sw3"]),
               writes=("Dd_sw3",), dma=True)

    build_stage_a(P, ch, a, x_in, x_out)
    build_stage_b(P, ch, a, x_out, x_out, moe, tail_preload=preload_c)
    build_stage_c(P, ch, a, x_out, x_out, skip_uvw=True, tail_preload=preload_d)
    build_stage_d(P, ch, a, x_out, x_out, skip_first=True)
    P.barrier()
    P.build()
    return nc


def kernel(**inputs):
    x = np.ascontiguousarray(np.asarray(inputs["L0_attn_x"], np.float32))
    consts = _host_consts()
    if FUSED:
        shared = {}
        for prep in (host_prep_a, host_prep_b, host_prep_c, host_prep_d):
            shared.update(prep(inputs))
        shared.update(consts)
        nc = build_program_full()
        in_maps = [dict(shared, x_in=np.ascontiguousarray(x[c]))
                   for c in range(N_CORES)]
        res = run_bass_kernel_spmd(nc, in_maps, core_ids=list(range(N_CORES)))
        return np.stack([r["x_out"] for r in res.results])
    cur = x
    for builder, prep in ((build_program_a, host_prep_a),
                          (build_program_b, host_prep_b),
                          (build_program_c, host_prep_c),
                          (build_program_d, host_prep_d)):
        nc = builder()
        shared = dict(prep(inputs))
        shared.update(consts)
        in_maps = [dict(shared, x_in=np.ascontiguousarray(cur[c]))
                   for c in range(N_CORES)]
        res = run_bass_kernel_spmd(nc, in_maps, core_ids=list(range(N_CORES)))
        cur = np.stack([r["x_out"] for r in res.results])
    return cur


# version 08
# speedup vs baseline: 1.0278x; 1.0278x over previous
import numpy as np
import concourse.bass as bass
import concourse.mybir as mybir
from concourse.bass_utils import run_bass_kernel_spmd

F32 = mybir.dt.float32
BF16 = mybir.dt.bfloat16
U32 = mybir.dt.uint32
AF = mybir.ActivationFunctionType
ALU = mybir.AluOpType
AX = mybir.AxisListType

N_CORES = 8
T = 2048
D = 1024
NT = T // 128
EPS_LN = 1e-5

ENG_SEM = {"tensor": "t", "vector": "v", "scalar": "s", "gpsimd": "g", "sync": "y"}
DMA_QUEUES = {"gpsimd": "gd", "sync": "sd", "scalar": "cd"}
DMA_ROT = 4
DMA_SEMS = [q + str(i) for q in ("gd", "sd", "cd") for i in range(DMA_ROT)]
ALL_SEMS = ["t", "v", "s", "g", "y"] + DMA_SEMS


class PB:

    def __init__(self, nc, dbg_n=0):
        self.nc = nc
        self.dbg = None
        self.dbg_off = 0
        self.dbg_n = dbg_n
        if dbg_n:
            self.dbg = nc.dram_tensor("dbg", [128, dbg_n], F32,
                                      kind="ExternalOutput").ap()
        self.thunks = {e: [] for e in ENG_SEM}
        self.cnt = {s: 0 for s in ALL_SEMS}
        self.dma_n = {q: 0 for q in DMA_QUEUES.values()}
        self.writes = {}
        self.readers = {}
        self.waited = {e: {s: 0 for s in ALL_SEMS} for e in ENG_SEM}
        self.semh = {}

    def _wait(self, eng, sem, val):
        if self.waited[eng][sem] >= val:
            return
        self.waited[eng][sem] = val
        self.thunks[eng].append(
            lambda E, sem=sem, val=val: E.wait_ge(self.semh[sem], val))

    def emit(self, eng, fn, reads=(), writes=(), dma=False, inc=True):
        deps = {}

        def add(tok):
            if tok is None:
                return
            s_, v_ = tok
            if deps.get(s_, 0) < v_:
                deps[s_] = v_
        for k in reads:
            add(self.writes.get(k))
        for k in writes:
            add(self.writes.get(k))
            for t_ in self.readers.get(k, ()):
                add(t_)
        for s_, v_ in deps.items():
            self._wait(eng, s_, v_)
        if dma:
            q = DMA_QUEUES[eng]
            sem = q + str(self.dma_n[q] % DMA_ROT)
            self.dma_n[q] += 1
        else:
            sem = ENG_SEM[eng]
        tok = None
        if inc:
            amt = 16 if dma else 1
            self.cnt[sem] += amt
            val = self.cnt[sem]
            self.thunks[eng].append(
                lambda E, fn=fn, sem=sem, val=val, amt=amt:
                    fn(E).then_inc(self.semh[sem], amt))
            tok = (sem, val)
            for k in writes:
                self.writes[k] = tok
                self.readers[k] = []
            for k in reads:
                self.readers.setdefault(k, []).append(tok)
        else:
            self.thunks[eng].append(lambda E, fn=fn: fn(E))
        return tok

    def dump(self, ap, keys, n, cast=False):
        if self.dbg is None:
            return
        off = self.dbg_off
        self.dbg_off += n
        assert self.dbg_off <= self.dbg_n
        if cast:
            self.emit("gpsimd", lambda E: E.dma_start(self.dbg[:, off:off + n], ap),
                      reads=keys, dma=True)
        else:
            self.emit("sync", lambda E: E.dma_start(self.dbg[:, off:off + n], ap),
                      reads=keys, dma=True)

    def barrier(self):
        snapshot = dict(self.cnt)
        for eng in ENG_SEM:
            for s_, v_ in snapshot.items():
                if v_ > 0:
                    self._wait(eng, s_, v_)
        self.writes = {}
        self.readers = {}

    def mm_group(self, out_ap, pieces, reads, bankkey):
        n = len(pieces)
        tok = None
        for i, (lh, rh) in enumerate(pieces):
            first, last = i == 0, i == n - 1
            tok = self.emit(
                "tensor",
                lambda E, lh=lh, rh=rh, first=first, last=last:
                    E.matmul(out_ap, lh, rh, start=first, stop=last),
                reads=reads if (first or last) else (),
                writes=(bankkey,) if (first or last) else (),
                inc=last)
        return tok

    def build(self):
        import contextlib
        es = contextlib.ExitStack()
        for s_ in ALL_SEMS:
            self.semh[s_] = es.enter_context(self.nc.semaphore("sem_" + s_))
        block = es.enter_context(self.nc.Block())
        for eng in ("sync", "gpsimd", "tensor", "vector", "scalar"):
            ths = self.thunks[eng]
            getattr(block, eng)(lambda E, ths=ths: [t_(E) for t_ in ths])
        es.close()



def _host_consts():
    return {"c_ident": np.eye(128, dtype=np.float32)}


def host_prep_d(inp):
    m = {}
    m["d_w1T"] = np.ascontiguousarray(inp["L1_ffn_w1_w"].T).astype(np.float32)
    m["d_w2T"] = np.ascontiguousarray(inp["L1_ffn_w2_w"].T).astype(np.float32)
    m["d_h1T"] = np.ascontiguousarray(inp["L1_ffn_h1_w"].T).astype(np.float32)
    m["d_h2T"] = np.ascontiguousarray(inp["L1_ffn_h2_w"].T).astype(np.float32)
    m["d_w3T"] = np.ascontiguousarray(inp["L1_ffn_w3_w"].T).astype(np.float32)
    m["d_s1"] = np.ascontiguousarray(np.asarray(inp["L1_ffn_w1_s"]).reshape(32, 128).T).astype(np.float32)
    m["d_s2"] = np.ascontiguousarray(np.asarray(inp["L1_ffn_w2_s"]).reshape(32, 128).T).astype(np.float32)
    m["d_sh1"] = np.ascontiguousarray(np.asarray(inp["L1_ffn_h1_s"]).reshape(32, 128).T).astype(np.float32)
    m["d_sh2"] = np.ascontiguousarray(np.asarray(inp["L1_ffn_h2_s"]).reshape(32, 128).T).astype(np.float32)
    m["d_sw3"] = np.ascontiguousarray(np.asarray(inp["L1_ffn_w3_s"]).reshape(8, 128).T).astype(np.float32)
    return m


def declare_dram_d(nc):
    a = {}
    a["d_w1T"] = nc.dram_tensor("d_w1T", [1024, 4096], F32, kind="ExternalInput").ap()
    a["d_w2T"] = nc.dram_tensor("d_w2T", [1024, 4096], F32, kind="ExternalInput").ap()
    a["d_h1T"] = nc.dram_tensor("d_h1T", [4096, 4096], F32, kind="ExternalInput").ap()
    a["d_h2T"] = nc.dram_tensor("d_h2T", [4096, 4096], F32, kind="ExternalInput").ap()
    a["d_w3T"] = nc.dram_tensor("d_w3T", [4096, 1024], F32, kind="ExternalInput").ap()
    for k in ("d_s1", "d_s2", "d_sh1", "d_sh2"):
        a[k] = nc.dram_tensor(k, [128, 32], F32, kind="ExternalInput").ap()
    a["d_sw3"] = nc.dram_tensor("d_sw3", [128, 8], F32, kind="ExternalInput").ap()
    return a



ARB_N = 80 * 1024
ARF_N = 7680


class Chip:
    def __init__(self, nc):
        self.nc = nc
        self.psf = [nc.alloc_psum_tensor(f"psf{i}", [128, 512], F32).ap()
                    for i in range(6)]
        self.psb = [nc.alloc_psum_tensor(f"psb{i}", [128, 1024], BF16).ap()
                    for i in range(2)]
        self.arb = nc.alloc_sbuf_tensor("arb", [128, ARB_N], BF16).ap()
        self.arf = nc.alloc_sbuf_tensor("arf", [128, ARF_N], F32).ap()
        self.outT = nc.alloc_sbuf_tensor("outT", [128, 4096], F32).ap()
        self.identf = nc.alloc_sbuf_tensor("identf", [128, 128], F32).ap()
        self.identb = nc.alloc_sbuf_tensor("identb", [128, 128], BF16).ap()
        self.onesf = nc.alloc_sbuf_tensor("onesf", [1, 128], F32).ap()
        self.rbrow = nc.alloc_sbuf_tensor("rbrow", [1, 8], F32).ap()
        self.idxi = nc.alloc_sbuf_tensor("idxi", [128, 16], mybir.dt.int32).ap()
        self.idxg = [nc.alloc_sbuf_tensor(f"idxg{i}", [128, 1], mybir.dt.int32).ap()
                     for i in range(4)]

    def eps_ap(self):
        return self.ff(EPS_O, 1)

    def load_consts(self, P, aps):
        P.emit("vector", lambda E: E.memset(self.ff(EPS_O, 1), EPS_LN),
               writes=("c_eps",))
        P.emit("vector", lambda E: E.memset(self.onesf[:], 1.0),
               writes=("onesf",))
        P.emit("gpsimd", lambda E: E.dma_start(self.identf, aps["c_ident"]),
               writes=("identf",), dma=True)
        P.emit("gpsimd", lambda E: E.dma_start(self.identb, aps["c_ident"]),
               writes=("identb",), dma=True)

    def bf(self, off, n):
        assert off >= 0 and off + n <= ARB_N, (off, n)
        return self.arb[:, off:off + n]

    def ff(self, off, n):
        assert off >= 0 and off + n <= ARF_N, (off, n)
        return self.arf[:, off:off + n]



def ln_stats(P, xtile_ap, xkeys, mean_ap, rstd_ap, skey, sc, sck, eps_ap=None):
    P.emit("scalar",
           lambda E: E.activation(sc[:, 0:1024], xtile_ap, AF.Square,
                                  accum_out=sc[:, 1024:1025]),
           reads=xkeys, writes=(sck,))
    P.emit("vector",
           lambda E: E.reduce_sum(sc[:, 1025:1026], xtile_ap, axis=AX.X),
           reads=xkeys, writes=(sck + "_sum",))
    P.emit("scalar",
           lambda E: E.mul(mean_ap, sc[:, 1025:1026], 1.0 / D),
           reads=(sck + "_sum",), writes=(skey + "_m",))
    P.emit("scalar",
           lambda E: E.activation(sc[:, 1026:1027], mean_ap, AF.Square),
           reads=(skey + "_m",), writes=(sck + "_m2",))
    P.emit("vector",
           lambda E: E.scalar_tensor_tensor(sc[:, 1027:1028], sc[:, 1024:1025],
                                            1.0 / D, sc[:, 1026:1027],
                                            op0=ALU.mult, op1=ALU.subtract),
           reads=(sck, sck + "_m2"), writes=(sck + "_var",))
    P.emit("scalar",
           lambda E: E.activation(sc[:, 1028:1029], sc[:, 1027:1028], AF.Ln,
                                  bias=eps_ap),
           reads=(sck + "_var", "c_eps"), writes=(sck + "_lnv",))
    P.emit("scalar",
           lambda E: E.activation(rstd_ap, sc[:, 1028:1029], AF.Exp, scale=-0.5),
           reads=(sck + "_lnv",), writes=(skey + "_r",))


XT_O, AT_O, BT_O, CT_O = 0, 8192, 40960, 8192
W1S_O, XB_O = 73728, 77824
SC_O, XN_O, SIL_O, LNO_O = 0, 2080, 4128, 5152
S1_O, S2_O, SH1_O, SH2_O, SW3_O, MEAN_O = 7200, 7232, 7264, 7296, 7328, 7336
EPS_O = 7679


def ln_x_to_xT(P, ch, x_in_t, tt0, ntiles, tag, xt_off=XT_O):
    keys = []
    for j in range(ntiles):
        tt = tt0 + j
        xbuf = ch.ff(XN_O + (j % 2) * 1024, 1024)
        xk = f"{tag}x{j % 2}"
        P.emit("sync", lambda E, tt=tt, xbuf=xbuf: E.dma_start(xbuf, x_in_t[tt]),
               writes=(xk,), dma=True)
        sc = ch.ff(SC_O + (j % 2) * 1040, 1040)
        sck = f"{tag}sc{j % 2}"
        mean = ch.ff(MEAN_O + (j % 2), 1)
        rstd = ch.ff(MEAN_O + 2 + (j % 2), 1)
        skey = f"{tag}st{j % 2}"
        ln_stats(P, xbuf, (xk,), mean, rstd, skey, sc, sck, ch.eps_ap())
        xhb = ch.bf(XB_O + (j % 2) * 1024, 1024)
        xhbk = f"{tag}xhb{j % 2}"
        P.emit("vector",
               lambda E, xhb=xhb, xbuf=xbuf, mean=mean, rstd=rstd:
                   E.tensor_scalar(xhb, xbuf, mean, rstd,
                                   op0=ALU.subtract, op1=ALU.mult),
               reads=(xk, skey + "_m", skey + "_r"), writes=(xhbk,))
        for g in range(2):
            pb = ch.psb[g]
            pbk = f"psb{g}"
            for c4 in range(4):
                c = g * 4 + c4
                P.emit("tensor",
                       lambda E, c=c, c4=c4, pb=pb, j=j:
                           E.transpose(pb[:, c4 * 128:(c4 + 1) * 128],
                                       ch.bf(XB_O + (j % 2) * 1024 + c * 128, 128),
                                       ch.identb),
                       reads=(xhbk, "identb") if c4 in (0, 3) else (),
                       writes=(pbk,) if c4 in (0, 3) else (),
                       inc=(c4 == 3))
            xtv = ch.bf(xt_off, 8192).rearrange("p (c t) -> p c t", t=1024)
            pbv = pb.rearrange("p (c t) -> p c t", t=128)
            kk = f"{tag}xT_{j}_{g}"
            P.emit("vector",
                   lambda E, g=g, j=j, xtv=xtv, pbv=pbv:
                       E.tensor_copy(xtv[:, g * 4:(g + 1) * 4,
                                         j * 128:(j + 1) * 128], pbv[:, 0:4, :]),
                   reads=(pbk,), writes=(kk,))
            keys.append(kk)
    return tuple(keys)


def build_stage_d(P, ch, a, x_in, x_out, tag="D", skip_first=False):
    if not skip_first:
        for off, name in ((S1_O, "d_s1"), (S2_O, "d_s2"), (SH1_O, "d_sh1"),
                          (SH2_O, "d_sh2")):
            P.emit("gpsimd",
                   lambda E, off=off, name=name:
                       E.dma_start(ch.ff(off, 32), a[name]),
                   writes=(tag + name,), dma=True)
        P.emit("gpsimd", lambda E: E.dma_start(ch.ff(SW3_O, 8), a["d_sw3"]),
               writes=(tag + "d_sw3",), dma=True)

    x_in_t = x_in.rearrange("(n p) d -> n p d", p=128)
    x_out_t = x_out.rearrange("(n p) d -> n p d", p=128)
    w1v = a["d_w1T"].rearrange("(c p) o -> p c o", p=128)
    w2v = a["d_w2T"].rearrange("(c p) o -> p c o", p=128)
    h1v = a["d_h1T"].rearrange("(c p) o -> p c o", p=128)
    h2v = a["d_h2T"].rearrange("(c p) o -> p c o", p=128)
    w3v = a["d_w3T"].rearrange("(c p) o -> p c o", p=128)

    for th in range(2):
        tt0 = th * 8
        xT_keys = ln_x_to_xT(P, ch, x_in_t, tt0, 8, tag)

        for ot in range(32):
            wb = ch.bf(W1S_O + (ot % 2) * 2048, 1024)
            wb2 = ch.bf(W1S_O + (ot % 2) * 2048 + 1024, 1024)
            wk, wk2 = f"{tag}w1b{ot % 2}", f"{tag}w2b{ot % 2}"
            P.emit("gpsimd",
                   lambda E, wb=wb, ot=ot:
                       E.dma_start(wb, w1v[:, :, ot * 128:(ot + 1) * 128]),
                   writes=(wk,), dma=True)
            P.emit("gpsimd",
                   lambda E, wb2=wb2, ot=ot:
                       E.dma_start(wb2, w2v[:, :, ot * 128:(ot + 1) * 128]),
                   writes=(wk2,), dma=True)
            for half in range(2):
                b1, b2 = ch.psf[2 * half], ch.psf[2 * half + 1]
                k1, k2 = f"psf{2 * half}", f"psf{2 * half + 1}"
                pieces1 = [(wb[:, c * 128:(c + 1) * 128],
                            ch.bf(XT_O + c * 1024 + half * 512, 512))
                           for c in range(8)]
                pieces2 = [(wb2[:, c * 128:(c + 1) * 128],
                            ch.bf(XT_O + c * 1024 + half * 512, 512))
                           for c in range(8)]
                P.mm_group(b1[:, 0:512], pieces1, (wk,) + xT_keys, k1)
                P.mm_group(b2[:, 0:512], pieces2, (wk2,) + xT_keys, k2)
                sil = ch.ff(SIL_O + half * 512, 512)
                silk = f"{tag}sil{half}"
                P.emit("scalar",
                       lambda E, b1=b1, sil=sil, ot=ot:
                           E.activation(sil, b1[:, 0:512], AF.Silu,
                                        scale=ch.ff(S1_O + ot, 1)),
                       reads=(k1, tag + "d_s1"), writes=(silk,))
                atv = ch.bf(AT_O + ot * 1024 + half * 512, 512)
                P.emit("vector",
                       lambda E, b2=b2, atv=atv, sil=sil, ot=ot:
                           E.scalar_tensor_tensor(atv, b2[:, 0:512],
                                                  ch.ff(S2_O + ot, 1), sil,
                                                  op0=ALU.mult, op1=ALU.mult),
                       reads=(k2, tag + "d_s2", silk),
                       writes=(f"{tag}aT{ot}_{half}",))
        aT_keys = tuple(f"{tag}aT{o}_{h}" for o in range(32) for h in range(2))
        bT_keys = tuple(f"{tag}bT{o}_{h}" for o in range(32) for h in range(2))
        cT_keys = tuple(f"{tag}cT{o}_{h}" for o in range(32) for h in range(2))

        for ph in range(2):
            wv = h1v if ph == 0 else h2v
            sOFF = SH1_O if ph == 0 else SH2_O
            sKEY = tag + ("d_sh1" if ph == 0 else "d_sh2")
            src_off = AT_O if ph == 0 else BT_O
            src_keys = aT_keys if ph == 0 else bT_keys
            dst_off = BT_O if ph == 0 else CT_O
            dkey = f"{tag}bT" if ph == 0 else f"{tag}cT"
            for ot in range(32):
                hb = ch.bf(XT_O + (ot % 2) * 4096, 4096)
                hk = f"{tag}hb{ot % 2}"
                P.emit("gpsimd",
                       lambda E, hb=hb, ot=ot, wv=wv:
                           E.dma_start(hb, wv[:, :, ot * 128:(ot + 1) * 128]),
                       writes=(hk,) + (xT_keys if (ph == 0 and ot < 2) else ()),
                       dma=True)
                for half in range(2):
                    bank = ch.psf[2 * (ot % 2) + half]
                    bk = f"psf{2 * (ot % 2) + half}"
                    pieces = [(hb[:, c * 128:(c + 1) * 128],
                               ch.bf(src_off + c * 1024 + half * 512, 512))
                              for c in range(32)]
                    P.mm_group(bank[:, 0:512], pieces, (hk,) + src_keys, bk)
                    dst = ch.bf(dst_off + ot * 1024 + half * 512, 512)
                    P.emit("scalar",
                           lambda E, bank=bank, dst=dst, ot=ot, sOFF=sOFF:
                               E.activation(dst, bank[:, 0:512], AF.Gelu,
                                            scale=ch.ff(sOFF + ot, 1)),
                           reads=(bk, sKEY), writes=(f"{dkey}{ot}_{half}",))

        for tq in range(2):
            oT_keys = []
            for dt in range(8):
                w3b = ch.bf(XT_O + (dt % 2) * 4096, 4096)
                w3k = f"{tag}hb{dt % 2}"
                P.emit("gpsimd",
                       lambda E, w3b=w3b, dt=dt:
                           E.dma_start(w3b, w3v[:, :, dt * 128:(dt + 1) * 128]),
                       writes=(w3k,), dma=True)
                bank = ch.psf[dt % 4]
                bk = f"psf{dt % 4}"
                pieces = [(w3b[:, c * 128:(c + 1) * 128],
                           ch.bf(CT_O + c * 1024 + tq * 512, 512))
                          for c in range(32)]
                P.mm_group(bank[:, 0:512], pieces, (w3k,) + cT_keys, bk)
                kk = f"{tag}oT{dt}"
                P.emit("vector",
                       lambda E, bank=bank, dt=dt:
                           E.tensor_scalar(ch.outT[:, dt * 512:(dt + 1) * 512],
                                           bank[:, 0:512],
                                           ch.ff(SW3_O + dt, 1), None,
                                           op0=ALU.mult),
                       reads=(bk, tag + "d_sw3"), writes=(kk,))
                oT_keys.append(kk)
            oT_keys = tuple(oT_keys)
            for j in range(4):
                tt = tt0 + tq * 4 + j
                onat = ch.ff(XN_O + (j % 2) * 1024, 1024)
                onk = f"{tag}on{j % 2}"
                for g in range(2):
                    bank = ch.psf[4 + g]
                    bk = f"psf{4 + g}"
                    for c4 in range(4):
                        d_ = g * 4 + c4
                        P.emit("tensor",
                               lambda E, d_=d_, c4=c4, bank=bank, j=j:
                                   E.transpose(bank[:, c4 * 128:(c4 + 1) * 128],
                                               ch.outT[:, d_ * 512 + j * 128:
                                                       d_ * 512 + j * 128 + 128],
                                               ch.identf),
                               reads=(oT_keys + ("identf",)) if c4 in (0, 3) else (),
                               writes=(bk,) if c4 in (0, 3) else (),
                               inc=(c4 == 3))
                    P.emit("vector",
                           lambda E, bank=bank, onat=onat, g=g:
                               E.tensor_copy(onat[:, g * 512:(g + 1) * 512],
                                             bank[:, 0:512]),
                           reads=(bk,), writes=(onk,))
                sc = ch.ff(SC_O + (j % 2) * 1040, 1040)
                sck = f"{tag}osc{j % 2}"
                mean = ch.ff(MEAN_O + 4 + (j % 2), 1)
                rstd = ch.ff(MEAN_O + 6 + (j % 2), 1)
                skey = f"{tag}ost{j % 2}"
                ln_stats(P, onat, (onk,), mean, rstd, skey, sc, sck, ch.eps_ap())
                lnout = ch.ff(LNO_O + (j % 2) * 1024, 1024)
                lnk = f"{tag}lno{j % 2}"
                P.emit("vector",
                       lambda E, onat=onat, mean=mean, rstd=rstd, lnout=lnout:
                           E.tensor_scalar(lnout, onat, mean, rstd,
                                           op0=ALU.subtract, op1=ALU.mult),
                       reads=(onk, skey + "_m", skey + "_r"), writes=(lnk,))
                xbuf = ch.ff(SIL_O, 1024)
                xk = f"{tag}xr"
                P.emit("sync",
                       lambda E, tt=tt, xbuf=xbuf: E.dma_start(xbuf, x_in_t[tt]),
                       reads=(f"xres{tt}",),
                       writes=(xk, f"{tag}sil0", f"{tag}sil1"), dma=True)
                P.emit("vector",
                       lambda E, xbuf=xbuf, lnout=lnout:
                           E.tensor_add(xbuf, xbuf, lnout),
                       reads=(xk, lnk), writes=(xk,))
                P.emit("sync",
                       lambda E, tt=tt, xbuf=xbuf:
                           E.dma_start(x_out_t[tt], xbuf),
                       reads=(xk,), writes=(f"xres{tt}",), dma=True)
        P.barrier()



def build_program_d():
    nc = bass.Bass("TRN2", target_bir_lowering=False, debug=False)
    x_in = nc.dram_tensor("x_in", [T, D], F32, kind="ExternalInput").ap()
    x_out = nc.dram_tensor("x_out", [T, D], F32, kind="ExternalOutput").ap()
    a = declare_dram_d(nc)
    a["c_ident"] = nc.dram_tensor("c_ident", [128, 128], F32,
                                  kind="ExternalInput").ap()
    P = PB(nc)
    ch = Chip(nc)
    ch.load_consts(P, a)
    build_stage_d(P, ch, a, x_in, x_out)
    P.barrier()
    P.build()
    return nc


def run_stage_d(x, inputs, cores=1):
    nc = build_program_d()
    shared = host_prep_d(inputs)
    shared.update(_host_consts())
    in_maps = []
    for c in range(cores):
        m = dict(shared)
        m["x_in"] = np.ascontiguousarray(x[c])
        in_maps.append(m)
    res = run_bass_kernel_spmd(nc, in_maps, core_ids=list(range(cores)))
    return np.stack([r["x_out"] for r in res.results])


EPS_QK = 1e-6
AQB_O, AKT_O, AVS_O = 0, 16384, 34304
AOT_O, AWS_O = 51984, 68368
AXTT_O, AXB_O, AKB_O, AVB_O = 76560, 77584, 79632, 80656
ASC_O, AXN_O, AQF_O, AKF_O, ASQ_O = 0, 2080, 4128, 5152, 6176
ANRM_O, AE_O, ARZ_O, AMN_O = 7200, 7264, 7520, 7524
ATN_O_A = AWS_O
AOA_O = AWS_O + 1024
AMSK_O = AWS_O + 3072
AQT2_O = AWS_O + 3840


def host_prep_a(inp):
    m = {}
    wq, wk, wv = inp["L0_attn_Wq"], inp["L0_attn_Wk"], inp["L0_attn_Wv"]
    m["a_wqkvT"] = np.ascontiguousarray(
        np.concatenate([wq.T, wk.T, wv.T], axis=1)).astype(np.float32)
    m["a_woT"] = np.ascontiguousarray(inp["L0_attn_Wo"].T).astype(np.float32)
    bq = np.concatenate([np.asarray(inp["L0_attn_bq"]),
                         np.asarray(inp["L0_attn_bk"]),
                         np.asarray(inp["L0_attn_bv"])]).astype(np.float32)
    m["a_bqkv"] = np.ascontiguousarray(np.broadcast_to(bq[None, :], (128, 3072)))
    m["a_bo"] = np.ascontiguousarray(np.broadcast_to(
        np.asarray(inp["L0_attn_bo"], np.float32)[None, :], (128, 1024)))
    msk = np.zeros((128, 768), dtype=np.float32)
    for cls in range(3):
        for jj in range(192):
            for ii in range(128):
                diff = jj - 32 - ii
                ok = abs(diff) <= 32 and diff % 4 == 0
                if cls == 0:
                    ok = ok and jj >= 32
                if cls == 2:
                    ok = ok and jj < 160
                if not ok:
                    continue
                if jj < 128:
                    msk[jj, cls * 256 + ii] = 1.0
                else:
                    msk[jj - 128, cls * 256 + 128 + ii] = 1.0
    m["a_masks"] = msk
    return m


def declare_dram_a(nc):
    a = {}
    a["a_wqkvT"] = nc.dram_tensor("a_wqkvT", [1024, 3072], F32, kind="ExternalInput").ap()
    a["a_woT"] = nc.dram_tensor("a_woT", [1024, 1024], F32, kind="ExternalInput").ap()
    a["a_bqkv"] = nc.dram_tensor("a_bqkv", [128, 3072], F32, kind="ExternalInput").ap()
    a["a_bo"] = nc.dram_tensor("a_bo", [128, 1024], F32, kind="ExternalInput").ap()
    a["a_masks"] = nc.dram_tensor("a_masks", [128, 768], F32, kind="ExternalInput").ap()
    return a


def build_stage_a(P, ch, a, x_in, x_out, tag="A"):
    x_in_t = x_in.rearrange("(n p) d -> n p d", p=128)
    x_out_t = x_out.rearrange("(n p) d -> n p d", p=128)
    wqv = a["a_wqkvT"].rearrange("(c p) o -> p c o", p=128)
    eps = ch.eps_ap()
    biasF = ch.outT[:, 0:3072]
    boF = ch.outT[:, 3072:4096]

    P.emit("sync", lambda E: E.dma_start(biasF, a["a_bqkv"]),
           writes=(tag + "biasF",), dma=True)
    P.emit("sync", lambda E: E.dma_start(boF, a["a_bo"]),
           writes=(tag + "boF",), dma=True)
    for c in range(8):
        P.emit("gpsimd",
               lambda E, c=c: E.dma_start(ch.bf(AOT_O + c * 3072, 3072),
                                          wqv[:, c, :]),
               writes=(f"{tag}wqkv{c}",), dma=True)
    wq_keys = tuple(f"{tag}wqkv{c}" for c in range(8))
    P.emit("vector", lambda E: E.memset(ch.bf(AKT_O, 17920), 0.0),
           writes=(tag + "ktz",))
    P.emit("vector", lambda E: E.memset(ch.bf(AVS_O, 17680), 1.0),
           writes=(tag + "vsz",))

    ktv = ch.bf(AKT_O, 17920).rearrange("p (j w) -> p j w", w=2240)
    vsv = ch.bf(AVS_O, 17680).rearrange("p (j h c) -> p j h c", h=16, c=65)

    for tt in range(16):
        par = tt % 2
        xbuf = ch.ff(AXN_O + par * 1024, 1024)
        xk = f"{tag}x{par}"
        P.emit("sync", lambda E, tt=tt, xbuf=xbuf: E.dma_start(xbuf, x_in_t[tt]),
               reads=(f"xres{tt}",), writes=(xk,), dma=True)
        sc = ch.ff(ASC_O + par * 1040, 1040)
        sck = f"{tag}sc{par}"
        mean = ch.ff(AMN_O + par, 1)
        rstd = ch.ff(AMN_O + 2 + par, 1)
        skey = f"{tag}st{par}"
        ln_stats(P, xbuf, (xk,), mean, rstd, skey, sc, sck, eps)
        xhb = ch.bf(AXB_O + par * 1024, 1024)
        xhbk = f"{tag}xhb{par}"
        P.emit("vector",
               lambda E, xhb=xhb, xbuf=xbuf, mean=mean, rstd=rstd:
                   E.tensor_scalar(xhb, xbuf, mean, rstd,
                                   op0=ALU.subtract, op1=ALU.mult),
               reads=(xk, skey + "_m", skey + "_r"), writes=(xhbk,))
        for g in range(2):
            pb, pbk = ch.psb[g], f"psb{g}"
            for c4 in range(4):
                c = g * 4 + c4
                P.emit("tensor",
                       lambda E, c=c, c4=c4, pb=pb, par=par:
                           E.transpose(pb[:, c4 * 128:(c4 + 1) * 128],
                                       ch.bf(AXB_O + par * 1024 + c * 128, 128),
                                       ch.identb),
                       reads=(xhbk, "identb") if c4 in (0, 3) else (),
                       writes=(pbk,) if c4 in (0, 3) else (), inc=(c4 == 3))
            P.emit("vector",
                   lambda E, g=g, pb=pb:
                       E.tensor_copy(ch.bf(AXTT_O + g * 512, 512), pb[:, 0:512]),
                   reads=(pbk,), writes=(f"{tag}xtt{g}",))
        xtt_keys = (f"{tag}xtt0", f"{tag}xtt1")
        for tau in range(3):
            for eta in range(2):
                bank = ch.psf[tau * 2 + eta]
                bk = f"psf{tau * 2 + eta}"
                pieces = [(ch.bf(AXTT_O + c * 128, 128),
                           ch.bf(AOT_O + c * 3072 + tau * 1024 + eta * 512, 512))
                          for c in range(8)]
                P.mm_group(bank[:, 0:512], pieces, xtt_keys + wq_keys, bk)
        for tau, (foff, fkey) in enumerate(((AQF_O, f"{tag}qf"),
                                            (AKF_O, f"{tag}kf"))):
            for eta in range(2):
                bk = f"psf{tau * 2 + eta}"
                P.emit("vector",
                       lambda E, tau=tau, eta=eta, foff=foff:
                           E.tensor_tensor(ch.ff(foff + eta * 512, 512),
                                           ch.psf[tau * 2 + eta][:, 0:512],
                                           biasF[:, tau * 1024 + eta * 512:
                                                 tau * 1024 + eta * 512 + 512],
                                           op=ALU.add),
                       reads=(bk, tag + "biasF"), writes=(fkey + str(eta),))
        for eta in range(2):
            bk = f"psf{4 + eta}"
            P.emit("vector",
                   lambda E, eta=eta, par=par:
                       E.tensor_tensor(ch.bf(AVB_O + eta * 512, 512),
                                       ch.psf[4 + eta][:, 0:512],
                                       biasF[:, 2048 + eta * 512:
                                             2048 + eta * 512 + 512],
                                       op=ALU.add),
                   reads=(bk, tag + "biasF"), writes=(f"{tag}vb{eta}",))
        for fo, fkey, nrm_o, rkey in ((AQF_O, f"{tag}qf", ANRM_O, f"{tag}rq"),
                                      (AKF_O, f"{tag}kf", ANRM_O + 32, f"{tag}rk")):
            fkeys = (fkey + "0", fkey + "1")
            P.emit("scalar",
                   lambda E, fo=fo: E.activation(ch.ff(ASQ_O, 1024),
                                                 ch.ff(fo, 1024), AF.Square),
                   reads=fkeys, writes=(tag + "sq",))
            P.emit("vector",
                   lambda E, nrm_o=nrm_o:
                       E.reduce_sum(ch.ff(nrm_o, 16),
                                    ch.ff(ASQ_O, 1024).rearrange(
                                        "p (h c) -> p h c", c=64), axis=AX.X),
                   reads=(tag + "sq",), writes=(tag + "ss" + fkey,))
            P.emit("scalar",
                   lambda E, nrm_o=nrm_o:
                       E.activation(ch.ff(nrm_o, 16), ch.ff(nrm_o, 16), AF.Ln),
                   reads=(tag + "ss" + fkey,), writes=(tag + "ln" + fkey,))
            P.emit("scalar",
                   lambda E, nrm_o=nrm_o:
                       E.activation(ch.ff(nrm_o + 16, 16), ch.ff(nrm_o, 16),
                                    AF.Exp, scale=0.5),
                   reads=(tag + "ln" + fkey,), writes=(tag + "nm" + fkey,))
            P.emit("vector",
                   lambda E, nrm_o=nrm_o:
                       E.tensor_scalar(ch.ff(nrm_o + 16, 16),
                                       ch.ff(nrm_o + 16, 16), EPS_QK, None,
                                       op0=ALU.add),
                   reads=(tag + "nm" + fkey,), writes=(tag + "ne" + fkey,))
            P.emit("vector",
                   lambda E, nrm_o=nrm_o:
                       E.reciprocal(ch.ff(nrm_o + 16, 16), ch.ff(nrm_o + 16, 16)),
                   reads=(tag + "ne" + fkey,), writes=(rkey,))
        for h in range(16):
            eng = "vector" if h % 2 == 0 else "scalar"
            if eng == "vector":
                P.emit("vector",
                       lambda E, h=h, tt=tt:
                           E.tensor_scalar(
                               ch.bf(AQB_O + tt * 1024 + h * 64, 64),
                               ch.ff(AQF_O + h * 64, 64),
                               ch.ff(ANRM_O + 16 + h, 1), None, op0=ALU.mult),
                       reads=(f"{tag}qf0", f"{tag}qf1", f"{tag}rq"),
                       writes=(f"{tag}qb{tt}_{h}",))
            else:
                P.emit("scalar",
                       lambda E, h=h, tt=tt:
                           E.activation(ch.bf(AQB_O + tt * 1024 + h * 64, 64),
                                        ch.ff(AQF_O + h * 64, 64), AF.Copy,
                                        scale=ch.ff(ANRM_O + 16 + h, 1)),
                       reads=(f"{tag}qf0", f"{tag}qf1", f"{tag}rq"),
                       writes=(f"{tag}qb{tt}_{h}",))
        for h in range(16):
            eng = "vector" if h % 2 == 1 else "scalar"
            if eng == "vector":
                P.emit("vector",
                       lambda E, h=h:
                           E.tensor_scalar(ch.bf(AKB_O + h * 64, 64),
                                           ch.ff(AKF_O + h * 64, 64),
                                           ch.ff(ANRM_O + 48 + h, 1), None,
                                           op0=ALU.mult),
                       reads=(f"{tag}kf0", f"{tag}kf1", f"{tag}rk"),
                       writes=(f"{tag}kb{h}",))
            else:
                P.emit("scalar",
                       lambda E, h=h:
                           E.activation(ch.bf(AKB_O + h * 64, 64),
                                        ch.ff(AKF_O + h * 64, 64), AF.Copy,
                                        scale=ch.ff(ANRM_O + 48 + h, 1)),
                       reads=(f"{tag}kf0", f"{tag}kf1", f"{tag}rk"),
                       writes=(f"{tag}kb{h}",))
        for g in range(2):
            pb, pbk = ch.psb[g], f"psb{g}"
            for c4 in range(4):
                c = g * 4 + c4
                P.emit("tensor",
                       lambda E, c=c, c4=c4, pb=pb:
                           E.transpose(pb[:, c4 * 128:(c4 + 1) * 128],
                                       ch.bf(AKB_O + c * 128, 128), ch.identb),
                       reads=((tuple(f"{tag}kb{h}" for h in range(16))
                               + ("identb",)) if c4 in (0, 3) else ()),
                       writes=(pbk,) if c4 in (0, 3) else (), inc=(c4 == 3))
            P.emit("vector",
                   lambda E, g=g, pb=pb, tt=tt, ktv=ktv:
                       E.tensor_copy(
                           ktv[:, g * 4:(g + 1) * 4,
                               32 + tt * 128:32 + (tt + 1) * 128],
                           pb.rearrange("p (c t) -> p c t", t=128)[:, 0:4, :]),
                   reads=(pbk, tag + "ktz"), writes=(f"{tag}kT{tt}_{g}",))
        vbv = ch.bf(AVB_O, 1024).rearrange("p (h c) -> p h c", c=64)
        P.emit("gpsimd",
               lambda E, tt=tt, vbv=vbv, vsv=vsv:
                   E.dma_start(vsv[32:128, tt, :, 0:64], vbv[0:96, :, :]),
               reads=(f"{tag}vb0", f"{tag}vb1", tag + "vsz"),
               writes=(f"{tag}vs{tt}a",), dma=True)
        P.emit("gpsimd",
               lambda E, tt=tt, vbv=vbv, vsv=vsv:
                   E.dma_start(vsv[0:32, tt + 1, :, 0:64], vbv[96:128, :, :]),
               reads=(f"{tag}vb0", f"{tag}vb1", tag + "vsz"),
               writes=(f"{tag}vs{tt}b",), dma=True)
    P.barrier()

    maskb = ch.bf(AMSK_O, 768)
    P.emit("gpsimd", lambda E: E.dma_start(maskb, a["a_masks"]),
           writes=(tag + "maskb",), dma=True)
    for qt in range(16):
        par = qt % 2
        cls = 0 if qt == 0 else (2 if qt == 15 else 1)
        qttk = f"{tag}qtt{par}"
        for g in range(2):
            pb, pbk = ch.psb[g], f"psb{g}"
            for c4 in range(4):
                c = g * 4 + c4
                P.emit("tensor",
                       lambda E, c=c, c4=c4, pb=pb, qt=qt:
                           E.transpose(pb[:, c4 * 128:(c4 + 1) * 128],
                                       ch.bf(AQB_O + qt * 1024 + c * 128, 128),
                                       ch.identb),
                       reads=(tuple(f"{tag}qb{qt}_{h}" for h in range(16))
                              + ("identb",)) if c4 in (0, 3) else (),
                       writes=(pbk,) if c4 in (0, 3) else (), inc=(c4 == 3))
            P.emit("vector",
                   lambda E, g=g, pb=pb, par=par:
                       E.tensor_copy(ch.bf(AQT2_O + par * 1024 + g * 512, 512),
                                     pb[:, 0:512]),
                   reads=(pbk,), writes=(qttk + str(g),))
        oattn_o = AOA_O + par * 1024
        for h in range(16):
            sb = ch.psf[h % 4]
            sbk = f"psf{h % 4}"
            kcol = (h // 2) * 2240 + qt * 128
            lh1 = ch.bf(AKT_O + (h // 2) * 2240, 2240)[64 * (h % 2):64 * (h % 2) + 64,
                                                       qt * 128:qt * 128 + 128]
            lh2 = ch.bf(AKT_O + (h // 2) * 2240, 2240)[64 * (h % 2):64 * (h % 2) + 64,
                                                       qt * 128 + 128:qt * 128 + 192]
            rh = ch.bf(AQT2_O + par * 1024 + (h // 2) * 128, 128)[64 * (h % 2):64 * (h % 2) + 64, :]
            kt_keys = tuple(f"{tag}kT{t_}_{g_}" for t_ in range(16) for g_ in range(2))
            P.emit("tensor",
                   lambda E, sb=sb, lh1=lh1, rh=rh:
                       E.matmul(sb[:, 0:128], lh1, rh, start=True, stop=True),
                   reads=(qttk + "0", qttk + "1") + kt_keys,
                   writes=(sbk,), inc=False)
            P.emit("tensor",
                   lambda E, sb=sb, lh2=lh2, rh=rh:
                       E.matmul(sb[0:64, 128:256], lh2, rh, start=True, stop=True),
                   reads=(qttk + "0", qttk + "1") + kt_keys, writes=(sbk,))
            ab = ch.bf(ATN_O_A + (h % 4) * 256, 256)
            abk = f"{tag}ab{h % 4}"
            P.emit("scalar",
                   lambda E, sb=sb, ab=ab:
                       E.activation(ab[:, 0:128], sb[:, 0:128], AF.Exp),
                   reads=(sbk,), writes=(abk,), inc=False)
            P.emit("scalar",
                   lambda E, sb=sb, ab=ab:
                       E.activation(ab[0:64, 128:256], sb[0:64, 128:256], AF.Exp),
                   reads=(sbk,), writes=(abk,))
            P.emit("vector",
                   lambda E, ab=ab, cls=cls:
                       E.tensor_tensor(ab[:, 0:128], ab[:, 0:128],
                                       maskb[:, cls * 256:cls * 256 + 128],
                                       op=ALU.mult),
                   reads=(abk, tag + "maskb"), writes=(abk,), inc=False)
            P.emit("vector",
                   lambda E, ab=ab, cls=cls:
                       E.tensor_tensor(ab[0:64, 128:256], ab[0:64, 128:256],
                                       maskb[0:64, cls * 256 + 128:cls * 256 + 256],
                                       op=ALU.mult),
                   reads=(abk, tag + "maskb"), writes=(abk,))
            vb2 = ch.psf[4 + h % 2]
            vbk = f"psf{4 + h % 2}"
            P.emit("tensor",
                   lambda E, vb2=vb2, ab=ab, qt=qt, h=h, vsv=vsv:
                       E.matmul(vb2[:, 0:65], ab[:, 0:128], vsv[:, qt, h, :],
                                start=True, stop=False),
                   reads=(abk,) + tuple(f"{tag}vs{t_}{s_}" for t_ in range(16)
                                        for s_ in "ab"),
                   writes=(vbk,), inc=False)
            P.emit("tensor",
                   lambda E, vb2=vb2, ab=ab, qt=qt, h=h, vsv=vsv:
                       E.matmul(vb2[:, 0:65], ab[0:64, 128:256],
                                vsv[0:64, qt + 1, h, :], start=False, stop=True),
                   reads=(abk,), writes=(vbk,))
            P.emit("vector",
                   lambda E, vb2=vb2: E.reciprocal(ch.ff(ARZ_O, 1),
                                                   vb2[:, 64:65]),
                   reads=(vbk,), writes=(tag + "rz",))
            P.emit("vector",
                   lambda E, vb2=vb2, h=h, oattn_o=oattn_o:
                       E.tensor_scalar(ch.bf(oattn_o + h * 64, 64),
                                       vb2[:, 0:64], ch.ff(ARZ_O, 1), None,
                                       op0=ALU.mult),
                   reads=(vbk, tag + "rz"), writes=(f"{tag}oa{par}_{h}",))
        oa_keys = tuple(f"{tag}oa{par}_{h}" for h in range(16))
        for g in range(2):
            pb, pbk = ch.psb[g], f"psb{g}"
            for c4 in range(4):
                c = g * 4 + c4
                P.emit("tensor",
                       lambda E, c=c, c4=c4, pb=pb, oattn_o=oattn_o:
                           E.transpose(pb[:, c4 * 128:(c4 + 1) * 128],
                                       ch.bf(oattn_o + c * 128, 128), ch.identb),
                       reads=(oa_keys + ("identb",)) if c4 in (0, 3) else (),
                       writes=(pbk,) if c4 in (0, 3) else (), inc=(c4 == 3))
            otv = ch.bf(AOT_O, 16384).rearrange("p (c t) -> p c t", t=2048)
            P.emit("vector",
                   lambda E, g=g, pb=pb, qt=qt, otv=otv:
                       E.tensor_copy(otv[:, g * 4:(g + 1) * 4,
                                         qt * 128:(qt + 1) * 128],
                                     pb.rearrange("p (c t) -> p c t", t=128)[:, 0:4, :]),
                   reads=(pbk,) + (wq_keys if qt == 0 else ()),
                   writes=(f"{tag}oT{qt}_{g}",))
    wov = a["a_woT"].rearrange("(c p) m -> p c m", p=128)
    wo = ch.bf(AQB_O, 8192)
    wok = f"{tag}wo"
    qb_keys_all = tuple(f"{tag}qb{t_}_{h_}" for t_ in range(16) for h_ in range(16))
    P.emit("gpsimd", lambda E, wo=wo: E.dma_start(wo, wov[:, :, :]),
           writes=(wok,) + qb_keys_all, dma=True)
    P.barrier()

    oT_keys = tuple(f"{tag}oT{q_}_{g_}" for q_ in range(16) for g_ in range(2))
    for grp in range(4):
        for j in range(4):
            qt = grp * 4 + j
            par = qt % 2
            for eta in range(2):
                bank = ch.psf[par * 2 + eta]
                bk = f"psf{par * 2 + eta}"
                pieces = [(ch.bf(AOT_O + c * 2048 + qt * 128, 128),
                           wo[:, c * 1024 + eta * 512:c * 1024 + eta * 512 + 512])
                          for c in range(8)]
                P.mm_group(bank[:, 0:512], pieces, (wok,) + oT_keys, bk)
                P.emit("vector",
                       lambda E, bank=bank, eta=eta, par=par:
                           E.tensor_tensor(ch.ff(AXN_O + par * 1024 + eta * 512, 512),
                                           bank[:, 0:512],
                                           boF[:, eta * 512:eta * 512 + 512],
                                           op=ALU.add),
                       reads=(bk, tag + "boF"), writes=(f"{tag}on{par}",))
            onat = ch.ff(AXN_O + par * 1024, 1024)
            onk = f"{tag}on{par}"
            sc = ch.ff(ASC_O + par * 1040, 1040)
            sck = f"{tag}csc{par}"
            mean = ch.ff(AMN_O + 4 + par, 1)
            rstd = ch.ff(AMN_O + 6 + par, 1)
            skey = f"{tag}cst{par}"
            ln_stats(P, onat, (onk,), mean, rstd, skey, sc, sck, eps)
            lnout = ch.ff(AKF_O, 1024)
            lnk = f"{tag}lno"
            P.emit("vector",
                   lambda E, onat=onat, mean=mean, rstd=rstd, lnout=lnout:
                       E.tensor_scalar(lnout, onat, mean, rstd,
                                       op0=ALU.subtract, op1=ALU.mult),
                   reads=(onk, skey + "_m", skey + "_r"), writes=(lnk,))
            xbuf = ch.ff(AQF_O, 1024)
            xk = f"{tag}cxr"
            P.emit("sync",
                   lambda E, qt=qt, xbuf=xbuf: E.dma_start(xbuf, x_in_t[qt]),
                   reads=(f"xres{qt}",), writes=(xk, f"{tag}qf0", f"{tag}qf1"),
                   dma=True)
            P.emit("vector",
                   lambda E, xbuf=xbuf, lnout=lnout:
                       E.tensor_add(xbuf, xbuf, lnout),
                   reads=(xk, lnk), writes=(xk,))
            P.emit("sync",
                   lambda E, qt=qt, xbuf=xbuf: E.dma_start(x_out_t[qt], xbuf),
                   reads=(xk,), writes=(f"xres{qt}",), dma=True)
    P.barrier()


def build_program_a():
    nc = bass.Bass("TRN2", target_bir_lowering=False, debug=False)
    x_in = nc.dram_tensor("x_in", [T, D], F32, kind="ExternalInput").ap()
    x_out = nc.dram_tensor("x_out", [T, D], F32, kind="ExternalOutput").ap()
    a = declare_dram_a(nc)
    a["c_ident"] = nc.dram_tensor("c_ident", [128, 128], F32,
                                  kind="ExternalInput").ap()
    P = PB(nc)
    ch = Chip(nc)
    ch.load_consts(P, a)
    build_stage_a(P, ch, a, x_in, x_out)
    P.barrier()
    P.build()
    return nc


def run_stage_a(x, inputs, cores=1):
    nc = build_program_a()
    shared = host_prep_a(inputs)
    shared.update(_host_consts())
    in_maps = []
    for c in range(cores):
        m = dict(shared)
        m["x_in"] = np.ascontiguousarray(x[c])
        in_maps.append(m)
    res = run_bass_kernel_spmd(nc, in_maps, core_ids=list(range(cores)))
    return np.stack([r["x_out"] for r in res.results])


BXT_O, BH_O, BW_O, BXB_O = 0, 8192, 40960, 73728
BSC_O, BXN_O, BXF_O, BRW_O = 0, 2080, 4128, 5152
BLG_O, BCB_O, BNR_O, BST_O = 5216, 5280, 5408, 5440
BMN_O, BIO_O, BKMI_O = 7490, 7500, 7508


def host_prep_b(inp):
    m = {}
    m["b_rwT"] = np.ascontiguousarray(np.asarray(inp["L0_ffn_router_w"]).T).astype(np.float32)
    m["b_rbrow"] = np.asarray(inp["L0_ffn_router_b"], np.float32).reshape(1, 8).copy()
    wup = np.asarray(inp["L0_ffn_w_up"])
    wdn = np.asarray(inp["L0_ffn_w_down"])
    swup = np.asarray(inp["L0_ffn_shared_w_up"])
    swdn = np.asarray(inp["L0_ffn_shared_w_down"])
    m["b_wupT"] = np.ascontiguousarray(
        np.concatenate([wup, swup], 0).transpose(0, 2, 1)).astype(np.float32)
    m["b_wdnT"] = np.ascontiguousarray(
        np.concatenate([wdn, swdn], 0).transpose(0, 2, 1)).astype(np.float32)
    m["c_iota8"] = np.broadcast_to(np.arange(8, dtype=np.float32), (128, 8)).copy()
    m["c_kmi8"] = np.broadcast_to(1e4 - np.arange(8, dtype=np.float32), (128, 8)).copy()
    pfx = np.zeros((128, 256), dtype=np.float32)
    pfx[:, 0:128] = 1.0
    pfx[:, 128:256] = np.tril(np.ones((128, 128), dtype=np.float32), -1)
    m["c_pfx"] = pfx
    m["c_iotacol"] = np.arange(128, dtype=np.float32)[:, None].copy()
    m["c_huge"] = np.full((4096, 2), 1e9, dtype=np.float32)
    return m


def declare_dram_b(nc):
    a = {}
    a["b_rwT"] = nc.dram_tensor("b_rwT", [1024, 8], F32, kind="ExternalInput").ap()
    a["b_rbrow"] = nc.dram_tensor("b_rbrow", [1, 8], F32, kind="ExternalInput").ap()
    a["b_wupT"] = nc.dram_tensor("b_wupT", [9, 1024, 2048], F32, kind="ExternalInput").ap()
    a["b_wdnT"] = nc.dram_tensor("b_wdnT", [9, 2048, 1024], F32, kind="ExternalInput").ap()
    a["c_iota8"] = nc.dram_tensor("c_iota8", [128, 8], F32, kind="ExternalInput").ap()
    a["c_kmi8"] = nc.dram_tensor("c_kmi8", [128, 8], F32, kind="ExternalInput").ap()
    a["c_pfx"] = nc.dram_tensor("c_pfx", [128, 256], F32, kind="ExternalInput").ap()
    a["c_iotacol"] = nc.dram_tensor("c_iotacol", [128, 1], F32, kind="ExternalInput").ap()
    a["c_huge"] = nc.dram_tensor("c_huge", [4096, 2], F32, kind="ExternalInput").ap()
    return a


def build_stage_b(P, ch, a, x_in, x_out, moe_scratch, tag="B",
                  tail_preload=None):
    nc = P.nc
    x_in_t = x_in.rearrange("(n p) d -> n p d", p=128)
    x_out_t = x_out.rearrange("(n p) d -> n p d", p=128)
    moe_t = moe_scratch.rearrange("(n p) d -> n p d", p=128)
    eps = ch.eps_ap()
    rwv = a["b_rwT"].rearrange("(c p) e -> p c e", p=128)
    wupv = a["b_wupT"].rearrange("e (c p) f -> p e c f", p=128)
    wdnv = a["b_wdnT"].rearrange("e (c p) f -> p e c f", p=128)

    P.emit("sync", lambda E: E.dma_start(
               ch.ff(BRW_O, 64).rearrange("p (c e) -> p c e", e=8), rwv),
           writes=(tag + "rw",), dma=True)
    P.emit("sync", lambda E: E.dma_start(ch.ff(BIO_O, 8), a["c_iota8"]),
           writes=(tag + "iota",), dma=True)
    P.emit("sync", lambda E: E.dma_start(ch.ff(BKMI_O, 8), a["c_kmi8"]),
           writes=(tag + "kmi",), dma=True)
    P.emit("sync", lambda E: E.dma_start(ch.rbrow, a["b_rbrow"]),
           writes=(tag + "rb",), dma=True)

    wup = ch.bf(BW_O, 16384)
    wdn = ch.bf(BW_O + 16384, 16384)
    wupk, wdnk = f"{tag}wup", f"{tag}wdn"

    def emit_wup_b(e):
        P.emit("gpsimd",
               lambda E, e=e:
                   E.dma_start(wup.rearrange("p (c f) -> p c f", f=2048),
                               wupv[:, e, :, :]),
               writes=(wupk,), dma=True)

    def emit_wdn_b(e):
        P.emit("gpsimd",
               lambda E, e=e:
                   E.dma_start(wdn.rearrange("p (c f) -> p c f", f=1024),
                               wdnv[:, e, :, :]),
               writes=(wdnk,), dma=True)

    for th in range(2):
        emit_wup_b(0)
        emit_wdn_b(0)
        for j in range(8):
            tt = th * 8 + j
            par = j % 2
            xbuf = ch.ff(BXN_O + par * 1024, 1024)
            xk = f"{tag}x{par}"
            P.emit("sync", lambda E, tt=tt, xbuf=xbuf: E.dma_start(xbuf, x_in_t[tt]),
                   reads=(f"xres{tt}",), writes=(xk,), dma=True)
            sc = ch.ff(BSC_O + par * 1040, 1040)
            sck = f"{tag}sc{par}"
            mean = ch.ff(BMN_O + par, 1)
            rstd = ch.ff(BMN_O + 2 + par, 1)
            skey = f"{tag}st{par}"
            ln_stats(P, xbuf, (xk,), mean, rstd, skey, sc, sck, eps)
            xhb = ch.bf(BXB_O + par * 1024, 1024)
            xhbk = f"{tag}xhb{par}"
            P.emit("vector",
                   lambda E, xhb=xhb, xbuf=xbuf, mean=mean, rstd=rstd:
                       E.tensor_scalar(xhb, xbuf, mean, rstd,
                                       op0=ALU.subtract, op1=ALU.mult),
                   reads=(xk, skey + "_m", skey + "_r"), writes=(xhbk,))
            xhf = ch.ff(BXF_O, 1024)
            xhfk = f"{tag}xhf"
            P.emit("vector",
                   lambda E, xhf=xhf, xbuf=xbuf, mean=mean, rstd=rstd:
                       E.tensor_scalar(xhf, xbuf, mean, rstd,
                                       op0=ALU.subtract, op1=ALU.mult),
                   reads=(xk, skey + "_m", skey + "_r"), writes=(xhfk,))
            for g in range(2):
                pb, pbk = ch.psb[g], f"psb{g}"
                for c4 in range(4):
                    c = g * 4 + c4
                    P.emit("tensor",
                           lambda E, c=c, c4=c4, pb=pb, par=par:
                               E.transpose(pb[:, c4 * 128:(c4 + 1) * 128],
                                           ch.bf(BXB_O + par * 1024 + c * 128, 128),
                                           ch.identb),
                           reads=(xhbk, "identb") if c4 in (0, 3) else (),
                           writes=(pbk,) if c4 in (0, 3) else (), inc=(c4 == 3))
                xtv = ch.bf(BXT_O, 8192).rearrange("p (c t) -> p c t", t=1024)
                P.emit("vector",
                       lambda E, g=g, j=j, pb=pb, xtv=xtv:
                           E.tensor_copy(xtv[:, g * 4:(g + 1) * 4,
                                             j * 128:(j + 1) * 128],
                                         pb.rearrange("p (c t) -> p c t", t=128)[:, 0:4, :]),
                       reads=(pbk,), writes=(f"{tag}xT_{j}_{g}",))
            xtf = ch.outT[:, 0:1024]
            for g in range(2):
                bank, bk = ch.psf[4 + g], f"psf{4 + g}"
                for c4 in range(4):
                    c = g * 4 + c4
                    P.emit("tensor",
                           lambda E, c=c, c4=c4, bank=bank, xhf=xhf:
                               E.transpose(bank[:, c4 * 128:(c4 + 1) * 128],
                                           xhf[:, c * 128:(c + 1) * 128],
                                           ch.identf),
                           reads=(xhfk, "identf") if c4 in (0, 3) else (),
                           writes=(bk,) if c4 in (0, 3) else (), inc=(c4 == 3))
                P.emit("vector",
                       lambda E, g=g, bank=bank, xtf=xtf:
                           E.tensor_copy(xtf[:, g * 512:(g + 1) * 512],
                                         bank[:, 0:512]),
                       reads=(bk,), writes=(f"{tag}xtf{g}",))
            rb_ = ch.psf[0]
            rbk = "psf0"
            pieces = [(xtf[:, c * 128:(c + 1) * 128], ch.ff(BRW_O + c * 8, 8))
                      for c in range(8)]
            n_p = len(pieces)
            for i, (lh, rh) in enumerate(pieces):
                P.emit("tensor",
                       lambda E, lh=lh, rh=rh, i=i:
                           E.matmul(rb_[:, 0:8], lh, rh, start=(i == 0), stop=False),
                       reads=((f"{tag}xtf0", f"{tag}xtf1", tag + "rw", tag + "rb")
                              if i == 0 else ()),
                       writes=(rbk,) if i == 0 else (), inc=False)
            P.emit("tensor",
                   lambda E: E.matmul(rb_[:, 0:8], ch.onesf[0:1, :],
                                      ch.rbrow[0:1, :], start=False, stop=True),
                   reads=(f"{tag}xtf0", f"{tag}xtf1", tag + "rw", tag + "rb"),
                   writes=(rbk,))
            eb = ch.ff(BLG_O, 8)
            P.emit("scalar", lambda E, eb=eb, rb_=rb_: E.activation(eb, rb_[:, 0:8], AF.Exp),
                   reads=(rbk,), writes=(tag + "eb",))
            c_m1 = ch.ff(BNR_O, 1); c_i1 = ch.ff(BNR_O + 1, 1)
            c_m2 = ch.ff(BNR_O + 2, 1); c_rd = ch.ff(BNR_O + 3, 1)
            w8a = ch.ff(BLG_O + 8, 8); w8b = ch.ff(BLG_O + 16, 8)
            cmb = ch.ff(BCB_O + tt * 8, 8)
            def ve(fn, rd, wr):
                P.emit("vector", fn, reads=rd, writes=wr)
            ve(lambda E, eb=eb, c_m1=c_m1: E.reduce_max(c_m1, eb, axis=AX.X),
               (tag + "eb",), (tag + "m1",))
            ve(lambda E, w8a=w8a, eb=eb, c_m1=c_m1:
               E.tensor_scalar(w8a, eb, c_m1, None, op0=ALU.is_equal),
               (tag + "eb", tag + "m1"), (tag + "w8a",))
            ve(lambda E, w8a=w8a: E.tensor_mul(w8a, w8a, ch.ff(BKMI_O, 8)),
               (tag + "w8a", tag + "kmi"), (tag + "w8a",))
            ve(lambda E, w8a=w8a: E.tensor_scalar(w8a, w8a, -1e4, None, op0=ALU.add),
               (tag + "w8a",), (tag + "w8a",))
            ve(lambda E, c_i1=c_i1, w8a=w8a: E.reduce_max(c_i1, w8a, axis=AX.X),
               (tag + "w8a",), (tag + "i1",))
            ve(lambda E, c_i1=c_i1: E.tensor_scalar(c_i1, c_i1, -1.0, None, op0=ALU.mult),
               (tag + "i1",), (tag + "i1",))
            ve(lambda E, w8a=w8a, c_i1=c_i1:
               E.tensor_scalar(w8a, ch.ff(BIO_O, 8), c_i1, None, op0=ALU.is_equal),
               (tag + "iota", tag + "i1"), (tag + "w8a",))
            ve(lambda E, w8b=w8b, eb=eb: E.tensor_scalar(w8b, eb, 1e4, None, op0=ALU.add),
               (tag + "eb",), (tag + "w8b",))
            ve(lambda E, w8b=w8b, w8a=w8a: E.tensor_mul(w8b, w8a, w8b),
               (tag + "w8a", tag + "w8b"), (tag + "w8b",))
            ve(lambda E, w8b=w8b, eb=eb: E.tensor_sub(w8b, eb, w8b),
               (tag + "eb", tag + "w8b"), (tag + "w8b",))
            ve(lambda E, c_m2=c_m2, w8b=w8b: E.reduce_max(c_m2, w8b, axis=AX.X),
               (tag + "w8b",), (tag + "m2",))
            ve(lambda E, c_rd=c_rd, c_m1=c_m1, c_m2=c_m2: E.tensor_add(c_rd, c_m1, c_m2),
               (tag + "m1", tag + "m2"), (tag + "rd",))
            ve(lambda E, c_rd=c_rd: E.reciprocal(c_rd, c_rd), (tag + "rd",), (tag + "rd",))
            ve(lambda E, w8b=w8b, c_m2=c_m2:
               E.tensor_scalar(w8b, w8b, c_m2, None, op0=ALU.is_equal),
               (tag + "w8b", tag + "m2"), (tag + "w8b",))
            ve(lambda E, w8a=w8a, c_m1=c_m1: E.tensor_scalar(w8a, w8a, c_m1, None, op0=ALU.mult),
               (tag + "w8a", tag + "m1"), (tag + "w8a",))
            ve(lambda E, w8b=w8b, c_m2=c_m2: E.tensor_scalar(w8b, w8b, c_m2, None, op0=ALU.mult),
               (tag + "w8b", tag + "m2"), (tag + "w8b",))
            ve(lambda E, w8a=w8a, w8b=w8b: E.tensor_add(w8a, w8a, w8b),
               (tag + "w8a", tag + "w8b"), (tag + "w8a",))
            ve(lambda E, cmb=cmb, w8a=w8a, c_rd=c_rd:
               E.tensor_scalar(cmb, w8a, c_rd, None, op0=ALU.mult),
               (tag + "w8a", tag + "rd"), (f"{tag}cb{tt}",))
        xT_keys = tuple(f"{tag}xT_{j}_{g}" for j in range(8) for g in range(2))

        for e in range(9):
            hb_o = BH_O + (e % 2) * 16384
            hpfx = f"{tag}h{e % 2}_"
            if e > 0:
                emit_wdn_b(e)
            for ft in range(16):
                for eta in range(2):
                    bank = ch.psf[(ft % 2) * 2 + eta]
                    bk = f"psf{(ft % 2) * 2 + eta}"
                    pieces = [(wup[:, c * 2048 + ft * 128:c * 2048 + ft * 128 + 128],
                               ch.bf(BXT_O + c * 1024 + eta * 512, 512))
                              for c in range(8)]
                    P.mm_group(bank[:, 0:512], pieces, (wupk,) + xT_keys, bk)
                    P.emit("scalar",
                           lambda E, bank=bank, ft=ft, eta=eta, hb_o=hb_o:
                               E.activation(ch.bf(hb_o + ft * 1024 + eta * 512, 512),
                                            bank[:, 0:512], AF.Silu),
                           reads=(bk,), writes=(hpfx + f"{ft}_{eta}",))
            h_keys = tuple(hpfx + f"{f_}_{e_}" for f_ in range(16) for e_ in range(2))
            if e < 8:
                emit_wup_b(e + 1)
            for tt in range(8):
                gtt = th * 8 + tt
                stg = ch.ff(BST_O + (tt % 2) * 1024, 1024)
                stgk = f"{tag}stg{tt % 2}"
                for eta in range(2):
                    bank = ch.psf[4 + eta]
                    bk = f"psf{4 + eta}"
                    pieces = [(ch.bf(hb_o + c * 1024 + tt * 128, 128),
                               wdn[:, c * 1024 + eta * 512:c * 1024 + eta * 512 + 512])
                              for c in range(16)]
                    P.mm_group(bank[:, 0:512], pieces, (wdnk,) + h_keys, bk)
                    if e < 8:
                        P.emit("vector",
                               lambda E, bank=bank, eta=eta, stg=stg, gtt=gtt, e=e:
                                   E.tensor_scalar(stg[:, eta * 512:eta * 512 + 512],
                                                   bank[:, 0:512],
                                                   ch.ff(BCB_O + gtt * 8 + e, 1),
                                                   None, op0=ALU.mult),
                               reads=(bk, f"{tag}cb{gtt}"), writes=(stgk,))
                    else:
                        P.emit("vector",
                               lambda E, bank=bank, eta=eta, stg=stg:
                                   E.tensor_copy(stg[:, eta * 512:eta * 512 + 512],
                                                 bank[:, 0:512]),
                               reads=(bk,), writes=(stgk,))
                P.emit("gpsimd" if e > 0 else "gpsimd",
                       lambda E, stg=stg, gtt=gtt, e=e:
                           E.dma_start(moe_t[gtt], stg,
                                       accum_op=(ALU.bypass if e == 0 else ALU.add)),
                       reads=(stgk,), writes=(f"{tag}moe{gtt}",), dma=True)
        for j in range(8):
            tt = th * 8 + j
            par = j % 2
            obuf = ch.ff(BXN_O + par * 1024, 1024)
            obk = f"{tag}ob{par}"
            P.emit("sync",
                   lambda E, tt=tt, obuf=obuf: E.dma_start(obuf, moe_t[tt]),
                   reads=(f"{tag}moe{tt}",), writes=(obk,), dma=True)
            sc = ch.ff(BSC_O + par * 1040, 1040)
            sck = f"{tag}fsc{par}"
            mean = ch.ff(BMN_O + 4 + par, 1)
            rstd = ch.ff(BMN_O + 6 + par, 1)
            skey = f"{tag}fst{par}"
            ln_stats(P, obuf, (obk,), mean, rstd, skey, sc, sck, eps)
            lnout = ch.ff(BXF_O, 1024)
            lnk = f"{tag}flno"
            P.emit("vector",
                   lambda E, obuf=obuf, mean=mean, rstd=rstd, lnout=lnout:
                       E.tensor_scalar(lnout, obuf, mean, rstd,
                                       op0=ALU.subtract, op1=ALU.mult),
                   reads=(obk, skey + "_m", skey + "_r"),
                   writes=(lnk, f"{tag}xhf"),)
            xbuf = ch.ff(BST_O, 1024)
            xk2 = f"{tag}fxr"
            P.emit("sync",
                   lambda E, tt=tt, xbuf=xbuf: E.dma_start(xbuf, x_in_t[tt]),
                   reads=(f"xres{tt}",), writes=(xk2, f"{tag}stg0"), dma=True)
            P.emit("vector",
                   lambda E, xbuf=xbuf, lnout=lnout: E.tensor_add(xbuf, xbuf, lnout),
                   reads=(xk2, lnk), writes=(xk2,))
            P.emit("sync",
                   lambda E, tt=tt, xbuf=xbuf: E.dma_start(x_out_t[tt], xbuf),
                   reads=(xk2,), writes=(f"xres{tt}",), dma=True)
        if th == 1 and tail_preload is not None:
            tail_preload()
        P.barrier()


def build_program_b():
    nc = bass.Bass("TRN2", target_bir_lowering=False, debug=False)
    x_in = nc.dram_tensor("x_in", [T, D], F32, kind="ExternalInput").ap()
    x_out = nc.dram_tensor("x_out", [T, D], F32, kind="ExternalOutput").ap()
    moe = nc.dram_tensor("b_moe", [T, D], F32, kind="Internal").ap()
    a = declare_dram_b(nc)
    a["c_ident"] = nc.dram_tensor("c_ident", [128, 128], F32,
                                  kind="ExternalInput").ap()
    P = PB(nc)
    ch = Chip(nc)
    ch.load_consts(P, a)
    build_stage_b(P, ch, a, x_in, x_out, moe)
    P.barrier()
    P.build()
    return nc


def run_stage_b(x, inputs, cores=1):
    nc = build_program_b()
    shared = host_prep_b(inputs)
    shared.update(_host_consts())
    in_maps = []
    for c in range(cores):
        m = dict(shared)
        m["x_in"] = np.ascontiguousarray(x[c])
        in_maps.append(m)
    res = run_bass_kernel_spmd(nc, in_maps, core_ids=list(range(cores)))
    return np.stack([r["x_out"] for r in res.results])


CU_O, CV_O, CVF_O, CDG_O, CG_O, CG1_O, CW_O = \
    0, 16384, 32768, 67584, 71936, 72976, 77072
CSC_O, CXN_O, CS1_O, CS2_O, CMN_O, CB12_O = 0, 2080, 4128, 5152, 6176, 6200
NF = 2176
NFT = NF // 128


def host_prep_c(inp):
    m = {}
    wu = np.asarray(inp["L1_attn_Wu"], np.float32)
    wv = np.asarray(inp["L1_attn_Wv"], np.float32)
    m["c_uvw"] = np.ascontiguousarray(np.concatenate([wu, wv], axis=1))
    m["c_wo2"] = np.ascontiguousarray(np.asarray(inp["L1_attn_Wo"], np.float32))
    bu = np.asarray(inp["L1_attn_bu"], np.float32)
    bv = np.asarray(inp["L1_attn_bv"], np.float32)
    bo = np.asarray(inp["L1_attn_bo"], np.float32)
    m["c_buv"] = np.ascontiguousarray(np.broadcast_to(
        np.concatenate([bu, bv])[None, :], (128, 2048)))
    m["c_bo2"] = np.ascontiguousarray(np.broadcast_to(bo[None, :], (128, 1024)))
    L, FO = 2048, 64
    pos = np.arange(L, dtype=np.float64)[:, None]
    div = np.exp(np.arange(0, FO, 2, dtype=np.float64) * -(np.log(10000.0) / FO))
    pe = np.zeros((L, FO))
    pe[:, 0::2] = np.sin(pos * div)
    pe[:, 1::2] = np.cos(pos * div)
    m["c_peT"] = np.ascontiguousarray(pe.T).astype(np.float32)
    m["c_w12"] = np.ascontiguousarray(np.concatenate(
        [np.asarray(inp["L1_attn_W1"], np.float32),
         np.asarray(inp["L1_attn_W2"], np.float32)], axis=1))
    m["c_b12"] = np.ascontiguousarray(np.stack(
        [np.asarray(inp["L1_attn_b1"], np.float32),
         np.asarray(inp["L1_attn_b2"], np.float32)], axis=1))
    m["c_w3p"] = np.ascontiguousarray(np.concatenate(
        [np.asarray(inp["L1_attn_W3"], np.float32),
         np.asarray(inp["L1_attn_b3"], np.float32)[None, :]], axis=0))
    n = 4096
    t = np.arange(L, dtype=np.float64)
    f = np.arange(2049, dtype=np.float64)
    ang = 2 * np.pi * np.outer(t, f) / n
    dftt = np.zeros((L, 2 * NF))
    dftt[:, 0:2049] = np.cos(ang)
    dftt[:, NF:NF + 2049] = -np.sin(ang)
    m["c_dftt"] = dftt.astype(np.float32)
    decay = np.exp(-0.01 * t)[:, None]
    m["c_ddt"] = (dftt * decay).astype(np.float32)
    wf = np.full(2049, 2.0); wf[0] = 1.0; wf[2048] = 1.0
    ict = np.zeros((2 * NF, L))
    ict[0:2049, :] = (wf[:, None] / n) * np.cos(ang.T)
    ict[NF:NF + 2049, :] = -(wf[:, None] / n) * np.sin(ang.T)
    m["c_ict"] = ict.astype(np.float32)
    return m


def declare_dram_c(nc):
    a = {}
    a["c_uvw"] = nc.dram_tensor("c_uvw", [1024, 2048], F32, kind="ExternalInput").ap()
    a["c_wo2"] = nc.dram_tensor("c_wo2", [1024, 1024], F32, kind="ExternalInput").ap()
    a["c_buv"] = nc.dram_tensor("c_buv", [128, 2048], F32, kind="ExternalInput").ap()
    a["c_bo2"] = nc.dram_tensor("c_bo2", [128, 1024], F32, kind="ExternalInput").ap()
    a["c_peT"] = nc.dram_tensor("c_peT", [64, 2048], F32, kind="ExternalInput").ap()
    a["c_w12"] = nc.dram_tensor("c_w12", [64, 128], F32, kind="ExternalInput").ap()
    a["c_b12"] = nc.dram_tensor("c_b12", [64, 2], F32, kind="ExternalInput").ap()
    a["c_w3p"] = nc.dram_tensor("c_w3p", [65, 1024], F32, kind="ExternalInput").ap()
    a["c_dftt"] = nc.dram_tensor("c_dftt", [2048, 2 * NF], F32, kind="ExternalInput").ap()
    a["c_ddt"] = nc.dram_tensor("c_ddt", [2048, 2 * NF], F32, kind="ExternalInput").ap()
    a["c_ict"] = nc.dram_tensor("c_ict", [2 * NF, 2048], F32, kind="ExternalInput").ap()
    return a


def build_stage_c(P, ch, a, x_in, x_out, tag="C", skip_uvw=False,
                  tail_preload=None):
    x_in_t = x_in.rearrange("(n p) d -> n p d", p=128)
    x_out_t = x_out.rearrange("(n p) d -> n p d", p=128)
    eps = ch.eps_ap()
    buvF = ch.outT[:, 0:2048]
    boF = ch.outT[:, 2048:3072]
    P.emit("sync", lambda E: E.dma_start(buvF, a["c_buv"]),
           writes=(tag + "buv",), dma=True)
    P.emit("sync", lambda E: E.dma_start(boF, a["c_bo2"]),
           writes=(tag + "bo",), dma=True)
    uvv = a["c_uvw"].rearrange("(c p) o -> p c o", p=128)
    if not skip_uvw:
        for c in range(8):
            P.emit("gpsimd",
                   lambda E, c=c: E.dma_start(ch.bf(CVF_O + c * 2048, 2048),
                                              uvv[:, c, :]),
                   writes=(f"{tag}uvw{c}",), dma=True)
    uvw_keys = tuple(f"{tag}uvw{c}" for c in range(8))

    for tt in range(16):
        par = tt % 2
        xbuf = ch.ff(CXN_O + par * 1024, 1024)
        xk = f"{tag}x{par}"
        P.emit("sync", lambda E, tt=tt, xbuf=xbuf: E.dma_start(xbuf, x_in_t[tt]),
               reads=(f"xres{tt}",), writes=(xk,), dma=True)
        sc = ch.ff(CSC_O + par * 1040, 1040)
        sck = f"{tag}sc{par}"
        mean = ch.ff(CMN_O + par, 1)
        rstd = ch.ff(CMN_O + 2 + par, 1)
        skey = f"{tag}st{par}"
        ln_stats(P, xbuf, (xk,), mean, rstd, skey, sc, sck, eps)
        xhb = ch.bf(CG1_O + par * 1024, 1024)
        xhbk = f"{tag}xhb{par}"
        P.emit("vector",
               lambda E, xhb=xhb, xbuf=xbuf, mean=mean, rstd=rstd:
                   E.tensor_scalar(xhb, xbuf, mean, rstd,
                                   op0=ALU.subtract, op1=ALU.mult),
               reads=(xk, skey + "_m", skey + "_r"), writes=(xhbk,))
        for g in range(2):
            pb, pbk = ch.psb[g], f"psb{g}"
            for c4 in range(4):
                c = g * 4 + c4
                P.emit("tensor",
                       lambda E, c=c, c4=c4, pb=pb, par=par:
                           E.transpose(pb[:, c4 * 128:(c4 + 1) * 128],
                                       ch.bf(CG1_O + par * 1024 + c * 128, 128),
                                       ch.identb),
                       reads=(xhbk, "identb") if c4 in (0, 3) else (),
                       writes=(pbk,) if c4 in (0, 3) else (), inc=(c4 == 3))
            P.emit("vector",
                   lambda E, g=g, pb=pb:
                       E.tensor_copy(ch.bf(CW_O + g * 512, 512), pb[:, 0:512]),
                   reads=(pbk,), writes=(f"{tag}xtt{g}",))
        xtt_keys = (f"{tag}xtt0", f"{tag}xtt1")
        for q in range(4):
            bank, bk = ch.psf[q], f"psf{q}"
            pieces = [(ch.bf(CW_O + c * 128, 128),
                       ch.bf(CVF_O + c * 2048 + q * 512, 512))
                      for c in range(8)]
            P.mm_group(bank[:, 0:512], pieces, xtt_keys + uvw_keys, bk)
            dst_o = (CU_O if q < 2 else CV_O) + tt * 1024 + (q % 2) * 512
            P.emit("vector",
                   lambda E, bank=bank, q=q, dst_o=dst_o:
                       E.tensor_tensor(ch.bf(dst_o, 512), bank[:, 0:512],
                                       buvF[:, q * 512:(q + 1) * 512], op=ALU.add),
                   reads=(bk, tag + "buv"), writes=(f"{tag}uv{tt}_{q}",))
    u_keys = tuple(f"{tag}uv{t_}_{q_}" for t_ in range(16) for q_ in (0, 1))
    v_keys = tuple(f"{tag}uv{t_}_{q_}" for t_ in range(16) for q_ in (2, 3))
    P.barrier()

    peT = ch.bf(CG1_O, 2048)
    P.emit("gpsimd", lambda E: E.dma_start(peT[0:64, :], a["c_peT"]),
           writes=(tag + "peT",), dma=True)
    w12 = ch.bf(CW_O + 4096, 128)
    P.emit("gpsimd", lambda E: E.dma_start(w12[0:64, :], a["c_w12"]),
           writes=(tag + "w12",), dma=True)
    b12 = ch.ff(CB12_O, 2)
    P.emit("sync", lambda E: E.dma_start(b12[0:64, :], a["c_b12"]),
           writes=(tag + "b12",), dma=True)
    g1T = ch.bf(CG1_O + 2048, 2048)
    for q in range(4):
        bank, bk = ch.psf[4 + q % 2], f"psf{4 + q % 2}"
        P.emit("tensor",
               lambda E, q=q, bank=bank:
                   E.matmul(bank[0:64, 0:512], w12[0:64, 0:64],
                            peT[0:64, q * 512:(q + 1) * 512],
                            start=True, stop=True),
               reads=(tag + "peT", tag + "w12"), writes=(bk,))
        P.emit("scalar",
               lambda E, q=q, bank=bank, g1T=g1T:
                   E.activation(g1T[0:64, q * 512:(q + 1) * 512],
                                bank[0:64, 0:512], AF.Gelu_apprx_tanh,
                                bias=b12[0:64, 0:1]),
               reads=(f"psf{4 + q % 2}", tag + "b12"), writes=(f"{tag}g1_{q}",))
    g2T = ch.bf(CG1_O, 2048)
    for q in range(4):
        bank, bk = ch.psf[4 + q % 2], f"psf{4 + q % 2}"
        P.emit("tensor",
               lambda E, q=q, bank=bank, g1T=g1T:
                   E.matmul(bank[0:64, 0:512], w12[0:64, 64:128],
                            g1T[0:64, q * 512:(q + 1) * 512],
                            start=True, stop=True),
               reads=tuple(f"{tag}g1_{qq}" for qq in range(4)) + (tag + "w12",)
                     + ((tag + "peT",) if q == 0 else ()),
               writes=(bk,))
        P.emit("scalar",
               lambda E, q=q, bank=bank, g2T=g2T:
                   E.activation(g2T[0:64, q * 512:(q + 1) * 512],
                                bank[0:64, 0:512], AF.Gelu_apprx_tanh,
                                bias=b12[0:64, 1:2]),
               reads=(f"psf{4 + q % 2}", tag + "b12"),
               writes=(f"{tag}g2_{q}",))
    gp = ch.bf(CG_O, 1040)
    gpv = gp.rearrange("p (t c) -> p t c", c=65)
    P.emit("vector", lambda E, gp=gp: E.memset(gp, 1.0), writes=(tag + "gp1",))
    g2_keys = tuple(f"{tag}g2_{q}" for q in range(4))
    for tt in range(16):
        pb, pbk = ch.psb[tt % 2], f"psb{tt % 2}"
        P.emit("tensor",
               lambda E, tt=tt, pb=pb, g2T=g2T:
                   E.transpose(pb[:, 0:64], g2T[0:64, tt * 128:(tt + 1) * 128],
                               ch.identb[0:64, 0:64]),
               reads=g2_keys + ("identb",), writes=(pbk,))
        P.emit("vector",
               lambda E, tt=tt, pb=pb, gpv=gpv:
                   E.tensor_copy(gpv[:, tt, 0:64], pb[:, 0:64]),
               reads=(pbk, tag + "gp1"), writes=(f"{tag}gp{tt}",))
    gp_keys = tuple(f"{tag}gp{t_}" for t_ in range(16)) + (tag + "gp1",)

    ddv = a["c_ddt"].rearrange("(c p) f -> p c f", p=128)
    dgt = ch.bf(CDG_O, 4352)
    for fs in range(34):
        dbuf = ch.bf(CW_O + (fs % 2) * 2048, 2048)
        dbk = f"{tag}db{fs % 2}"
        P.emit("gpsimd",
               lambda E, fs=fs, dbuf=dbuf:
                   E.dma_start(dbuf.rearrange("p (c f) -> p c f", f=128),
                               ddv[:, :, fs * 128:(fs + 1) * 128]),
               writes=(dbk,), dma=True)
        bank, bk = ch.psf[fs % 2], f"psf{fs % 2}"
        pieces = [(ch.bf(CG_O + c * 65, 65),
                   dbuf[:, c * 128:(c + 1) * 128]) for c in range(16)]
        P.mm_group(bank[0:65, 0:128], pieces, gp_keys + (dbk,), bk)
        P.emit("vector",
               lambda E, fs=fs, bank=bank, dgt=dgt:
                   E.tensor_copy(dgt[0:65, fs * 128:(fs + 1) * 128],
                                 bank[0:65, 0:128]),
               reads=(bk,), writes=(f"{tag}dgt{fs}",))
    dgt_keys = tuple(f"{tag}dgt{f_}" for f_ in range(34))
    P.barrier()

    w3b = ch.bf(CG1_O, 1024)
    P.emit("gpsimd", lambda E, w3b=w3b: E.dma_start(w3b[0:65, :], a["c_w3p"]),
           writes=(tag + "w3b",), dma=True)
    dfv = a["c_dftt"].rearrange("(c p) f -> p c f", p=128)
    dgt = ch.bf(CDG_O, 4352)
    for fp in range(17):
        for half, ft in enumerate((fp, 17 + fp)):
            dbuf = ch.bf(CW_O + (ft % 2) * 2048, 2048)
            dbk = f"{tag}db{ft % 2}"
            P.emit("gpsimd",
                   lambda E, ft=ft, dbuf=dbuf:
                       E.dma_start(dbuf.rearrange("p (c f) -> p c f", f=128),
                                   dfv[:, :, ft * 128:(ft + 1) * 128]),
                   writes=(dbk,) + (uvw_keys if fp == 0 else ()), dma=True)
            for eta in range(2):
                bank = ch.psf[2 * half + eta]
                bk = f"psf{2 * half + eta}"
                pieces = [(dbuf[:, c * 128:(c + 1) * 128],
                           ch.bf(CV_O + c * 1024 + eta * 512, 512))
                          for c in range(16)]
                P.mm_group(bank[:, 0:512], pieces, (dbk,) + v_keys, bk)
                P.emit("vector",
                       lambda E, ft=ft, eta=eta, bank=bank:
                           E.tensor_copy(ch.bf(CVF_O + ft * 1024 + eta * 512, 512),
                                         bank[:, 0:512]),
                       reads=(bk,), writes=(f"{tag}vf{ft}_{eta}",))
        for eta in range(2):
            hr, hrk = ch.psf[4], "psf4"
            hi, hik = ch.psf[5], "psf5"
            P.emit("tensor",
                   lambda E, hr=hr, fp=fp, eta=eta, w3b=w3b, dgt=dgt:
                       E.matmul(hr[:, 0:512], dgt[0:65, fp * 128:fp * 128 + 128],
                                w3b[0:65, eta * 512:(eta + 1) * 512],
                                start=True, stop=True),
                   reads=dgt_keys + (tag + "w3b",), writes=(hrk,))
            P.emit("tensor",
                   lambda E, hi=hi, fp=fp, eta=eta, w3b=w3b, dgt=dgt:
                       E.matmul(hi[:, 0:512],
                                dgt[0:65, NF + fp * 128:NF + fp * 128 + 128],
                                w3b[0:65, eta * 512:(eta + 1) * 512],
                                start=True, stop=True),
                   reads=dgt_keys + (tag + "w3b",), writes=(hik,))
            vrs = ch.bf(CVF_O + fp * 1024 + eta * 512, 512)
            vis = ch.bf(CVF_O + (17 + fp) * 1024 + eta * 512, 512)
            vrk = f"{tag}vf{fp}_{eta}"
            vik = f"{tag}vf{17 + fp}_{eta}"
            s1 = ch.ff(CS1_O + eta * 512, 512)
            s2 = ch.ff(CS2_O + eta * 512, 512)
            s1k, s2k = f"{tag}s1{eta}", f"{tag}s2{eta}"
            P.emit("vector", lambda E, s1=s1, vrs=vrs, hr=hr:
                   E.tensor_tensor(s1, vrs, hr[:, 0:512], op=ALU.mult),
                   reads=(vrk, hrk), writes=(s1k,))
            P.emit("vector", lambda E, s2=s2, vrs=vrs, hi=hi:
                   E.tensor_tensor(s2, vrs, hi[:, 0:512], op=ALU.mult),
                   reads=(vrk, hik), writes=(s2k,))
            P.emit("vector", lambda E, vrs=vrs, vis=vis, hi=hi:
                   E.tensor_tensor(vrs, vis, hi[:, 0:512], op=ALU.mult),
                   reads=(vik, hik, s1k, s2k), writes=(vrk,))
            P.emit("vector", lambda E, vrs=vrs, s1=s1:
                   E.tensor_sub(vrs, s1, vrs),
                   reads=(vrk, s1k), writes=(vrk,))
            P.emit("vector", lambda E, s1=s1, vis=vis, hr=hr:
                   E.tensor_tensor(s1, vis, hr[:, 0:512], op=ALU.mult),
                   reads=(vik, hrk, s1k), writes=(s1k,))
            P.emit("vector", lambda E, vis=vis, s1=s1, s2=s2:
                   E.tensor_add(vis, s2, s1),
                   reads=(s1k, s2k, vik), writes=(vik,))
    P.barrier()

    icv = a["c_ict"].rearrange("(c p) t -> p c t", p=128)
    ict1 = ch.bf(CW_O, 2176)
    ict2 = ch.bf(CW_O + 2176, 2176)
    vf_keys = tuple(f"{tag}vf{f_}_{e_}" for f_ in range(34) for e_ in range(2))
    for ts in range(16):
        for half, (buf, bufk, c0) in enumerate(((ict1, f"{tag}ic1", 0),
                                                (ict2, f"{tag}ic2", 17))):
            P.emit("gpsimd",
                   lambda E, ts=ts, buf=buf, c0=c0:
                       E.dma_start(buf.rearrange("p (c t) -> p c t", t=128),
                                   icv[:, c0:c0 + 17, ts * 128:(ts + 1) * 128]),
                   writes=(bufk,), dma=True)
        for eta in range(2):
            bank = ch.psf[4 + eta]
            bk = f"psf{4 + eta}"
            p1 = [(ict1[:, c * 128:(c + 1) * 128],
                   ch.bf(CVF_O + c * 1024 + eta * 512, 512)) for c in range(17)]
            p2 = [(ict2[:, c * 128:(c + 1) * 128],
                   ch.bf(CVF_O + (17 + c) * 1024 + eta * 512, 512))
                  for c in range(17)]
            for i, (lh, rh) in enumerate(p1):
                P.emit("tensor",
                       lambda E, lh=lh, rh=rh, i=i, bank=bank:
                           E.matmul(bank[:, 0:512], lh, rh, start=(i == 0),
                                    stop=False),
                       reads=((f"{tag}ic1",) + vf_keys) if i in (0, 16) else (),
                       writes=(bk,) if i == 0 else (), inc=(i == 16))
            for i, (lh, rh) in enumerate(p2):
                P.emit("tensor",
                       lambda E, lh=lh, rh=rh, i=i, bank=bank:
                           E.matmul(bank[:, 0:512], lh, rh, start=False,
                                    stop=(i == 16)),
                       reads=((f"{tag}ic2",) + vf_keys) if i in (0, 16) else (),
                       writes=(bk,) if i in (0, 16) else (), inc=(i == 16))
            P.emit("vector",
                   lambda E, bank=bank, ts=ts, eta=eta:
                       E.tensor_tensor(ch.bf(CG1_O + 2048 + (ts % 2) * 1024
                                             + eta * 512, 512),
                                       bank[:, 0:512],
                                       ch.bf(CU_O + ts * 1024 + eta * 512, 512),
                                       op=ALU.mult),
                   reads=(bk,) + u_keys, writes=(f"{tag}yb{ts % 2}_{eta}",))
        ybk = (f"{tag}yb{ts % 2}_0", f"{tag}yb{ts % 2}_1")
        for g in range(2):
            pb, pbk = ch.psb[g], f"psb{g}"
            for c4 in range(4):
                c = g * 4 + c4
                P.emit("tensor",
                       lambda E, c=c, c4=c4, pb=pb, ts=ts:
                           E.transpose(pb[:, c4 * 128:(c4 + 1) * 128],
                                       ch.bf(CG1_O + 2048 + (ts % 2) * 1024
                                             + c * 128, 128), ch.identb),
                       reads=(ybk + ("identb",)) if c4 in (0, 3) else (),
                       writes=(pbk,) if c4 in (0, 3) else (), inc=(c4 == 3))
            ytv = ch.bf(CV_O, 16384).rearrange("p (c t) -> p c t", t=2048)
            P.emit("vector",
                   lambda E, g=g, pb=pb, ts=ts, ytv=ytv:
                       E.tensor_copy(ytv[:, g * 4:(g + 1) * 4,
                                         ts * 128:(ts + 1) * 128],
                                     pb.rearrange("p (c t) -> p c t", t=128)[:, 0:4, :]),
                   reads=(pbk,) + (v_keys if ts == 0 else ()),
                   writes=(f"{tag}yT{ts}_{g}",))
    wov = a["c_wo2"].rearrange("(c p) m -> p c m", p=128)
    wo = ch.bf(CU_O, 8192)
    wok = f"{tag}wo"
    P.emit("gpsimd", lambda E, wo=wo: E.dma_start(wo, wov[:, :, :]),
           writes=(wok,) + u_keys, dma=True)
    P.barrier()

    yT_keys = tuple(f"{tag}yT{t_}_{g_}" for t_ in range(16) for g_ in range(2))
    for qt in range(16):
        par = qt % 2
        for eta in range(2):
            bank = ch.psf[par * 2 + eta]
            bk = f"psf{par * 2 + eta}"
            pieces = [(ch.bf(CV_O + c * 2048 + qt * 128, 128),
                       wo[:, c * 1024 + eta * 512:c * 1024 + eta * 512 + 512])
                      for c in range(8)]
            P.mm_group(bank[:, 0:512], pieces, (wok,) + yT_keys, bk)
            P.emit("vector",
                   lambda E, bank=bank, eta=eta, par=par:
                       E.tensor_tensor(ch.ff(CXN_O + par * 1024 + eta * 512, 512),
                                       bank[:, 0:512],
                                       boF[:, eta * 512:eta * 512 + 512],
                                       op=ALU.add),
                   reads=(bk, tag + "bo"), writes=(f"{tag}on{par}",))
        onat = ch.ff(CXN_O + par * 1024, 1024)
        onk = f"{tag}on{par}"
        sc = ch.ff(CSC_O + par * 1040, 1040)
        sck = f"{tag}fsc{par}"
        mean = ch.ff(CMN_O + 4 + par, 1)
        rstd = ch.ff(CMN_O + 6 + par, 1)
        skey = f"{tag}fst{par}"
        ln_stats(P, onat, (onk,), mean, rstd, skey, sc, sck, eps)
        lnout = ch.ff(CS1_O, 1024)
        lnk = f"{tag}flno"
        P.emit("vector",
               lambda E, onat=onat, mean=mean, rstd=rstd, lnout=lnout:
                   E.tensor_scalar(lnout, onat, mean, rstd,
                                   op0=ALU.subtract, op1=ALU.mult),
               reads=(onk, skey + "_m", skey + "_r"),
               writes=(lnk, f"{tag}s10", f"{tag}s11"))
        xbuf = ch.ff(CS2_O, 1024)
        xk = f"{tag}fxr"
        P.emit("sync",
               lambda E, qt=qt, xbuf=xbuf: E.dma_start(xbuf, x_in_t[qt]),
               reads=(f"xres{qt}",), writes=(xk, f"{tag}s20", f"{tag}s21"),
               dma=True)
        P.emit("vector",
               lambda E, xbuf=xbuf, lnout=lnout: E.tensor_add(xbuf, xbuf, lnout),
               reads=(xk, lnk), writes=(xk,))
        P.emit("sync",
               lambda E, qt=qt, xbuf=xbuf: E.dma_start(x_out_t[qt], xbuf),
               reads=(xk,), writes=(f"xres{qt}",), dma=True)
    if tail_preload is not None:
        tail_preload()
    P.barrier()


def build_program_c():
    nc = bass.Bass("TRN2", target_bir_lowering=False, debug=False)
    x_in = nc.dram_tensor("x_in", [T, D], F32, kind="ExternalInput").ap()
    x_out = nc.dram_tensor("x_out", [T, D], F32, kind="ExternalOutput").ap()
    a = declare_dram_c(nc)
    a["c_ident"] = nc.dram_tensor("c_ident", [128, 128], F32,
                                  kind="ExternalInput").ap()
    P = PB(nc)
    ch = Chip(nc)
    ch.load_consts(P, a)
    build_stage_c(P, ch, a, x_in, x_out)
    P.barrier()
    P.build()
    return nc


def run_stage_c(x, inputs, cores=1):
    nc = build_program_c()
    shared = host_prep_c(inputs)
    shared.update(_host_consts())
    in_maps = []
    for c in range(cores):
        m = dict(shared)
        m["x_in"] = np.ascontiguousarray(x[c])
        in_maps.append(m)
    res = run_bass_kernel_spmd(nc, in_maps, core_ids=list(range(cores)))
    return np.stack([r["x_out"] for r in res.results])


from concourse.bass_core_rust import IndirectOffsetOnAxis

BS_XT, BS_XET, BS_H, BS_W, BS_XB, BS_XE = 0, 8192, 12288, 28672, 61440, 63488
BS_CNT, BS_GS, BS_C12, BS_IOJ, BS_SLT, BS_PFV, BS_TOK, BS_IOC, BS_EQ1, BS_EQ2 = \
    7516, 7580, 7582, 7584, 7585, 7593, 7601, 7609, 7610, 7618
BS_TOKG = 7626
BS_GS2 = 7631
BS_GSA = 7634


def build_stage_b2(P, ch, a, x_in, x_out, moe_scratch, xh_scratch,
                   toke_scratch, eo_scratch, tag="B"):
    nc = P.nc
    x_in_t = x_in.rearrange("(n p) d -> n p d", p=128)
    x_out_t = x_out.rearrange("(n p) d -> n p d", p=128)
    moe_t = moe_scratch.rearrange("(n p) d -> n p d", p=128)
    eps = ch.eps_ap()
    rwv = a["b_rwT"].rearrange("(c p) e -> p c e", p=128)
    wupv = a["b_wupT"].rearrange("e (c p) f -> p e c f", p=128)
    wdnv = a["b_wdnT"].rearrange("e (c p) f -> p e c f", p=128)
    btv = toke_scratch.rearrange("(c p) o -> p c o", p=128)
    pfxc = ch.outT[:, 1024:1280]

    P.emit("sync", lambda E: E.dma_start(
               ch.ff(BRW_O, 64).rearrange("p (c e) -> p c e", e=8), rwv),
           writes=(tag + "rw",), dma=True)
    P.emit("sync", lambda E: E.dma_start(ch.ff(BIO_O, 8), a["c_iota8"]),
           writes=(tag + "iota",), dma=True)
    P.emit("sync", lambda E: E.dma_start(ch.ff(BKMI_O, 8), a["c_kmi8"]),
           writes=(tag + "kmi",), dma=True)
    P.emit("sync", lambda E: E.dma_start(ch.rbrow, a["b_rbrow"]),
           writes=(tag + "rb",), dma=True)
    P.emit("sync", lambda E: E.dma_start(pfxc, a["c_pfx"]),
           writes=(tag + "pfxc",), dma=True)
    P.emit("sync", lambda E: E.dma_start(ch.ff(BS_IOC, 1), a["c_iotacol"]),
           writes=(tag + "ioc",), dma=True)

    for th in range(2):
        P.emit("gpsimd", lambda E: E.dma_start(toke_scratch, a["c_huge"]),
               writes=(tag + "tokeinit",) + tuple(
                   f"{tag}tk{e_}_{j_}" for e_ in range(8) for j_ in range(8)),
               dma=True)
        for j in range(8):
            tt = th * 8 + j
            par = j % 2
            xbuf = ch.ff(BXN_O + par * 1024, 1024)
            xk = f"{tag}x{par}"
            P.emit("sync", lambda E, tt=tt, xbuf=xbuf: E.dma_start(xbuf, x_in_t[tt]),
                   reads=(f"xres{tt}",), writes=(xk,), dma=True)
            sc = ch.ff(BSC_O + par * 1040, 1040)
            sck = f"{tag}sc{par}"
            mean = ch.ff(BMN_O + par, 1)
            rstd = ch.ff(BMN_O + 2 + par, 1)
            skey = f"{tag}st{par}"
            ln_stats(P, xbuf, (xk,), mean, rstd, skey, sc, sck, eps)
            xhb = ch.bf(BS_XB + par * 1024, 1024)
            xhbk = f"{tag}xhb{par}"
            P.emit("vector",
                   lambda E, xhb=xhb, xbuf=xbuf, mean=mean, rstd=rstd:
                       E.tensor_scalar(xhb, xbuf, mean, rstd,
                                       op0=ALU.subtract, op1=ALU.mult),
                   reads=(xk, skey + "_m", skey + "_r"), writes=(xhbk,))
            xhf = ch.ff(BXF_O, 1024)
            xhfk = f"{tag}xhf"
            P.emit("vector",
                   lambda E, xhf=xhf, xbuf=xbuf, mean=mean, rstd=rstd:
                       E.tensor_scalar(xhf, xbuf, mean, rstd,
                                       op0=ALU.subtract, op1=ALU.mult),
                   reads=(xk, skey + "_m", skey + "_r"), writes=(xhfk,))
            xhv = xh_scratch.rearrange("(n p) d -> n p d", p=128)
            P.emit("sync",
                   lambda E, j=j, xhb=xhb, xhv=xhv:
                       E.dma_start(xhv[j], xhb.bitcast(F32)),
                   reads=(xhbk,), writes=(f"{tag}xh{j}",), dma=True)
            for g in range(2):
                pb, pbk = ch.psb[g], f"psb{g}"
                for c4 in range(4):
                    c = g * 4 + c4
                    P.emit("tensor",
                           lambda E, c=c, c4=c4, pb=pb, par=par:
                               E.transpose(pb[:, c4 * 128:(c4 + 1) * 128],
                                           ch.bf(BS_XB + par * 1024 + c * 128, 128),
                                           ch.identb),
                           reads=(xhbk, "identb") if c4 in (0, 3) else (),
                           writes=(pbk,) if c4 in (0, 3) else (), inc=(c4 == 3))
                xtv = ch.bf(BS_XT, 8192).rearrange("p (c t) -> p c t", t=1024)
                P.emit("vector",
                       lambda E, g=g, j=j, pb=pb, xtv=xtv:
                           E.tensor_copy(xtv[:, g * 4:(g + 1) * 4,
                                             j * 128:(j + 1) * 128],
                                         pb.rearrange("p (c t) -> p c t", t=128)[:, 0:4, :]),
                       reads=(pbk,), writes=(f"{tag}xT_{j}_{g}",))
            xtf = ch.outT[:, 0:1024]
            for g in range(2):
                bank, bk = ch.psf[4 + g], f"psf{4 + g}"
                for c4 in range(4):
                    c = g * 4 + c4
                    P.emit("tensor",
                           lambda E, c=c, c4=c4, bank=bank, xhf=xhf:
                               E.transpose(bank[:, c4 * 128:(c4 + 1) * 128],
                                           xhf[:, c * 128:(c + 1) * 128],
                                           ch.identf),
                           reads=(xhfk, "identf") if c4 in (0, 3) else (),
                           writes=(bk,) if c4 in (0, 3) else (), inc=(c4 == 3))
                P.emit("vector",
                       lambda E, g=g, bank=bank, xtf=xtf:
                           E.tensor_copy(xtf[:, g * 512:(g + 1) * 512],
                                         bank[:, 0:512]),
                       reads=(bk,), writes=(f"{tag}xtf{g}",))
            rb_ = ch.psf[0]
            rbk = "psf0"
            pieces = [(xtf[:, c * 128:(c + 1) * 128], ch.ff(BRW_O + c * 8, 8))
                      for c in range(8)]
            for i, (lh, rh) in enumerate(pieces):
                P.emit("tensor",
                       lambda E, lh=lh, rh=rh, i=i, rb_=rb_:
                           E.matmul(rb_[:, 0:8], lh, rh, start=(i == 0), stop=False),
                       reads=((f"{tag}xtf0", f"{tag}xtf1", tag + "rw", tag + "rb")
                              if i == 0 else ()),
                       writes=(rbk,) if i == 0 else (), inc=False)
            P.emit("tensor",
                   lambda E, rb_=rb_: E.matmul(rb_[:, 0:8], ch.onesf[0:1, :],
                                               ch.rbrow[0:1, :], start=False,
                                               stop=True),
                   reads=(f"{tag}xtf0", f"{tag}xtf1", tag + "rw", tag + "rb"),
                   writes=(rbk,))
            eb = ch.ff(BLG_O, 8)
            P.emit("scalar", lambda E, eb=eb, rb_=rb_:
                   E.activation(eb, rb_[:, 0:8], AF.Exp),
                   reads=(rbk,), writes=(tag + "eb",))
            c_m1 = ch.ff(BNR_O, 1); c_i1 = ch.ff(BNR_O + 1, 1)
            c_m2 = ch.ff(BNR_O + 2, 1); c_rd = ch.ff(BNR_O + 3, 1)
            w8a = ch.ff(BLG_O + 8, 8); w8b = ch.ff(BLG_O + 16, 8)
            eq1b = ch.ff(BS_EQ1, 8); eq2b = ch.ff(BS_EQ2, 8)
            cmb = ch.ff(BCB_O + tt * 8, 8)
            def ve(fn, rd, wr):
                P.emit("vector", fn, reads=rd, writes=wr)
            ve(lambda E, eb=eb, c_m1=c_m1: E.reduce_max(c_m1, eb, axis=AX.X),
               (tag + "eb",), (tag + "m1",))
            ve(lambda E, w8a=w8a, eb=eb, c_m1=c_m1:
               E.tensor_scalar(w8a, eb, c_m1, None, op0=ALU.is_equal),
               (tag + "eb", tag + "m1"), (tag + "w8a",))
            ve(lambda E, w8a=w8a: E.tensor_mul(w8a, w8a, ch.ff(BKMI_O, 8)),
               (tag + "w8a", tag + "kmi"), (tag + "w8a",))
            ve(lambda E, w8a=w8a: E.tensor_scalar(w8a, w8a, -1e4, None, op0=ALU.add),
               (tag + "w8a",), (tag + "w8a",))
            ve(lambda E, c_i1=c_i1, w8a=w8a: E.reduce_max(c_i1, w8a, axis=AX.X),
               (tag + "w8a",), (tag + "i1",))
            ve(lambda E, c_i1=c_i1: E.tensor_scalar(c_i1, c_i1, -1.0, None, op0=ALU.mult),
               (tag + "i1",), (tag + "i1",))
            ve(lambda E, eq1b=eq1b, c_i1=c_i1:
               E.tensor_scalar(eq1b, ch.ff(BIO_O, 8), c_i1, None, op0=ALU.is_equal),
               (tag + "iota", tag + "i1"), (tag + "eq1b",))
            ve(lambda E, w8b=w8b, eb=eb: E.tensor_scalar(w8b, eb, 1e4, None, op0=ALU.add),
               (tag + "eb",), (tag + "w8b",))
            ve(lambda E, w8b=w8b, eq1b=eq1b: E.tensor_mul(w8b, eq1b, w8b),
               (tag + "eq1b", tag + "w8b"), (tag + "w8b",))
            ve(lambda E, w8b=w8b, eb=eb: E.tensor_sub(w8b, eb, w8b),
               (tag + "eb", tag + "w8b"), (tag + "w8b",))
            ve(lambda E, c_m2=c_m2, w8b=w8b: E.reduce_max(c_m2, w8b, axis=AX.X),
               (tag + "w8b",), (tag + "m2",))
            ve(lambda E, c_rd=c_rd, c_m1=c_m1, c_m2=c_m2: E.tensor_add(c_rd, c_m1, c_m2),
               (tag + "m1", tag + "m2"), (tag + "rd",))
            ve(lambda E, c_rd=c_rd: E.reciprocal(c_rd, c_rd), (tag + "rd",), (tag + "rd",))
            ve(lambda E, eq2b=eq2b, w8b=w8b, c_m2=c_m2:
               E.tensor_scalar(eq2b, w8b, c_m2, None, op0=ALU.is_equal),
               (tag + "w8b", tag + "m2"), (tag + "eq2b",))
            ve(lambda E, w8a=w8a, eq1b=eq1b, c_m1=c_m1:
               E.tensor_scalar(w8a, eq1b, c_m1, None, op0=ALU.mult),
               (tag + "eq1b", tag + "m1"), (tag + "w8a",))
            ve(lambda E, w8b=w8b, eq2b=eq2b, c_m2=c_m2:
               E.tensor_scalar(w8b, eq2b, c_m2, None, op0=ALU.mult),
               (tag + "eq2b", tag + "m2"), (tag + "w8b",))
            ve(lambda E, w8a=w8a, w8b=w8b: E.tensor_add(w8a, w8a, w8b),
               (tag + "w8a", tag + "w8b"), (tag + "w8a",))
            ve(lambda E, cmb=cmb, w8a=w8a, c_rd=c_rd:
               E.tensor_scalar(cmb, w8a, c_rd, None, op0=ALU.mult),
               (tag + "w8a", tag + "rd"), (f"{tag}cb{tt}",))
            cntj = ch.ff(BS_CNT + j * 8, 8)
            ve(lambda E, cntj=cntj, eq1b=eq1b, eq2b=eq2b:
               E.tensor_add(cntj, eq1b, eq2b),
               (tag + "eq1b", tag + "eq2b"), (f"{tag}cnt{j}",))
            pbk2 = "psf1"
            pb2 = ch.psf[1]
            pieces2 = [(pfxc[:, 0:128], ch.ff(BS_CNT + c * 8, 8)) for c in range(j)] \
                + [(pfxc[:, 128:256], cntj)]
            n2 = len(pieces2)
            for i, (lh, rh) in enumerate(pieces2):
                P.emit("tensor",
                       lambda E, lh=lh, rh=rh, i=i, n2=n2, pb2=pb2:
                           E.matmul(pb2[:, 0:8], lh, rh, start=(i == 0),
                                    stop=(i == n2 - 1)),
                       reads=(tuple(f"{tag}cnt{c_}" for c_ in range(j + 1))
                              + (tag + "pfxc",)) if i in (0, n2 - 1) else (),
                       writes=(pbk2,) if i in (0, n2 - 1) else (),
                       inc=(i == n2 - 1))
            pfv = ch.ff(BS_PFV, 8)
            ve(lambda E, pfv=pfv, pb2=pb2: E.tensor_copy(pfv, pb2[:, 0:8]),
               (pbk2,), (tag + "pfv",))
            slt = ch.ff(BS_SLT, 8)
            ve(lambda E, slt=slt, cntj=cntj, pfv=pfv:
               E.scalar_tensor_tensor(slt, cntj, -1e9, pfv, op0=ALU.mult, op1=ALU.add),
               (f"{tag}cnt{j}", tag + "pfv"), (tag + "slt",))
            ve(lambda E, slt=slt: E.tensor_scalar(slt, slt, 1e9, None, op0=ALU.add),
               (tag + "slt",), (tag + "slt",))
            ve(lambda E, w8b=w8b, slt=slt:
               E.tensor_scalar(w8b, slt, 511.5, None, op0=ALU.is_gt),
               (tag + "slt",), (tag + "w8b",))
            ve(lambda E, slt=slt, w8b=w8b:
               E.scalar_tensor_tensor(slt, w8b, 1e9, slt, op0=ALU.mult, op1=ALU.add),
               (tag + "slt", tag + "w8b"), (tag + "slt",))
            ve(lambda E, slt=slt:
               E.scalar_tensor_tensor(slt, ch.ff(BIO_O, 8), 512.0, slt,
                                      op0=ALU.mult, op1=ALU.add),
               (tag + "slt", tag + "iota"), (tag + "slt",))
            gs1 = ch.ff(BS_GSA + 2 * j, 1); gs2 = ch.ff(BS_GSA + 2 * j + 1, 1)
            pair1 = ch.ff(BS_GS, 2); pair2 = ch.ff(BS_C12, 2)
            c1 = ch.ff(BS_GS + 1, 1); c2 = ch.ff(BS_C12 + 1, 1)
            ve(lambda E, w8a=w8a, slt=slt, eq1b=eq1b: E.tensor_mul(w8a, eq1b, slt),
               (tag + "eq1b", tag + "slt"), (tag + "w8a",))
            ve(lambda E, gs1=gs1, w8a=w8a: E.reduce_sum(gs1, w8a, axis=AX.X),
               (tag + "w8a",), (tag + "gs1",))
            ve(lambda E, w8a=w8a, slt=slt, eq2b=eq2b: E.tensor_mul(w8a, eq2b, slt),
               (tag + "eq2b", tag + "slt"), (tag + "w8a",))
            ve(lambda E, gs2=gs2, w8a=w8a: E.reduce_sum(gs2, w8a, axis=AX.X),
               (tag + "w8a",), (tag + "gs2",))
            ve(lambda E, w8a=w8a, cmb=cmb, eq1b=eq1b: E.tensor_mul(w8a, eq1b, cmb),
               (tag + "eq1b", f"{tag}cb{tt}"), (tag + "w8a",))
            ve(lambda E, c1=c1, w8a=w8a: E.reduce_sum(c1, w8a, axis=AX.X),
               (tag + "w8a",), (tag + "c1",))
            ve(lambda E, w8a=w8a, cmb=cmb, eq2b=eq2b: E.tensor_mul(w8a, eq2b, cmb),
               (tag + "eq2b", f"{tag}cb{tt}"), (tag + "w8a",))
            ve(lambda E, c2=c2, w8a=w8a: E.reduce_sum(c2, w8a, axis=AX.X),
               (tag + "w8a",), (tag + "c2",))
            ioj = ch.ff(BS_IOJ, 1)
            ve(lambda E, ioj=ioj, j=j:
               E.tensor_scalar(ioj, ch.ff(BS_IOC, 1), float(j * 128), None,
                               op0=ALU.add),
               (tag + "ioc",), (tag + "ioj",))
            ve(lambda E, ioj=ioj, pair1=pair1: E.tensor_copy(pair1[:, 0:1], ioj),
               (tag + "ioj",), (tag + "p1a",))
            ve(lambda E, ioj=ioj, pair2=pair2: E.tensor_copy(pair2[:, 0:1], ioj),
               (tag + "ioj",), (tag + "p2a",))
            ve(lambda E, gs1=gs1: E.tensor_copy(ch.idxi[:, 0:1], gs1),
               (tag + "gs1",), (tag + "ix1",))
            ve(lambda E, gs2=gs2: E.tensor_copy(ch.idxi[:, 1:2], gs2),
               (tag + "gs2",), (tag + "ix2",))
            tkkeys = tuple(f"{tag}tk{e_}_{j}" for e_ in range(8))
            P.emit("gpsimd",
                   lambda E, pair1=pair1:
                       E.indirect_dma_start(toke_scratch,
                                            IndirectOffsetOnAxis(ch.idxi[:, 0:1], 0),
                                            pair1, None, bounds_check=4095,
                                            oob_is_err=False),
                   reads=(tag + "ix1", tag + "p1a", tag + "c1", tag + "tokeinit"),
                   writes=tkkeys, dma=True)
            P.emit("gpsimd",
                   lambda E, pair2=pair2:
                       E.indirect_dma_start(toke_scratch,
                                            IndirectOffsetOnAxis(ch.idxi[:, 1:2], 0),
                                            pair2, None, bounds_check=4095,
                                            oob_is_err=False),
                   reads=(tag + "ix2", tag + "p2a", tag + "c2"),
                   writes=tkkeys, dma=True)
        xT_keys = tuple(f"{tag}xT_{j}_{g}" for j in range(8) for g in range(2))
        tk_all = tuple(f"{tag}tk{e_}_{j_}" for e_ in range(8) for j_ in range(8))
        ce_all = tuple(f"{tag}ce{j_}" for j_ in range(8))

        wup = ch.bf(BS_W, 16384)
        wdn = ch.bf(BS_W + 16384, 16384)
        wupk, wdnk = f"{tag}wslot0", f"{tag}wslot1"
        P.emit("gpsimd",
               lambda E, wup=wup:
                   E.dma_start(wup.rearrange("p (c f) -> p c f", f=2048),
                               wupv[:, 8, :, :]),
               writes=(wupk,), dma=True)
        P.emit("gpsimd",
               lambda E, wdn=wdn:
                   E.dma_start(wdn.rearrange("p (c f) -> p c f", f=1024),
                               wdnv[:, 8, :, :]),
               writes=(wdnk,), dma=True)
        hsh = ch.bf(BS_H, 16384)
        for ft in range(16):
            for eta in range(2):
                bank = ch.psf[(ft % 2) * 2 + eta]
                bk = f"psf{(ft % 2) * 2 + eta}"
                pieces = [(wup[:, c * 2048 + ft * 128:c * 2048 + ft * 128 + 128],
                           ch.bf(BS_XT + c * 1024 + eta * 512, 512))
                          for c in range(8)]
                P.mm_group(bank[:, 0:512], pieces, (wupk,) + xT_keys, bk)
                P.emit("scalar",
                       lambda E, bank=bank, ft=ft, eta=eta, hsh=hsh:
                           E.activation(hsh[:, ft * 1024 + eta * 512:
                                            ft * 1024 + eta * 512 + 512],
                                        bank[:, 0:512], AF.Silu),
                       reads=(bk,), writes=(f"{tag}hs{ft}_{eta}",))
        hs_keys = tuple(f"{tag}hs{f_}_{e_}" for f_ in range(16) for e_ in range(2))
        for tt in range(8):
            gtt = th * 8 + tt
            stg = ch.ff(BST_O + (tt % 2) * 1024, 1024)
            stgk = f"{tag}stg{tt % 2}"
            for eta in range(2):
                bank = ch.psf[4 + eta]
                bk = f"psf{4 + eta}"
                pieces = [(hsh[:, c * 1024 + tt * 128:c * 1024 + tt * 128 + 128],
                           wdn[:, c * 1024 + eta * 512:c * 1024 + eta * 512 + 512])
                          for c in range(16)]
                P.mm_group(bank[:, 0:512], pieces, (wdnk,) + hs_keys, bk)
                P.emit("vector",
                       lambda E, bank=bank, eta=eta, stg=stg:
                           E.tensor_copy(stg[:, eta * 512:eta * 512 + 512],
                                         bank[:, 0:512]),
                       reads=(bk,), writes=(stgk,))
            P.emit("gpsimd",
                   lambda E, stg=stg, gtt=gtt:
                       E.dma_start(moe_t[gtt], stg, accum_op=ALU.bypass),
                   reads=(stgk,), writes=(f"{tag}moe{gtt}",), dma=True)
        moe_keys_th = tuple(f"{tag}moe{th * 8 + t_}" for t_ in range(8))

        prev_scatter = ()
        for e in range(8):
            par_e = e % 2
            wup = ch.bf(BS_W + par_e * 16384, 16384)
            wdn = ch.bf(BS_W + (1 - par_e) * 16384, 16384)
            wupk = f"{tag}wslot{par_e}"
            wdnk = f"{tag}wslot{1 - par_e}"
            P.emit("gpsimd",
                   lambda E, e=e, wup=wup:
                       E.dma_start(wup.rearrange("p (c f) -> p c f", f=2048),
                                   wupv[:, e, :, :]),
                   writes=(wupk,), dma=True)
            P.emit("gpsimd",
                   lambda E, e=e, wdn=wdn:
                       E.dma_start(wdn.rearrange("p (c f) -> p c f", f=1024),
                                   wdnv[:, e, :, :]),
                   writes=(wdnk,), dma=True)
            tcf = ch.ff(BS_TOK, 8)
            tcv = tcf.rearrange("p (c o) -> p c o", o=2)
            P.emit("gpsimd",
                   lambda E, e=e, tcv=tcv:
                       E.dma_start(tcv, btv[:, 4 * e:4 * e + 4, :],
                                   allow_slow_non_contiguous=True),
                   reads=tk_all, writes=(f"{tag}tokf", f"{tag}cef"), dma=True)
            P.emit("vector",
                   lambda E, tcv=tcv: E.tensor_copy(
                       ch.idxi[:, 4:8].rearrange("p (c o) -> p c o", o=1),
                       tcv[:, :, 0:1]),
                   reads=(f"{tag}tokf",), writes=(f"{tag}toki",))

            for ct in range(4):
                xe = ch.bf(BS_XE + (ct % 2) * 1024, 1024)
                xek = f"{tag}xe{ct % 2}"
                P.emit("gpsimd",
                       lambda E, ct=ct, xe=xe:
                           E.indirect_dma_start(xe.bitcast(F32), None, xh_scratch,
                                                IndirectOffsetOnAxis(
                                                    ch.idxi[:, 4 + ct:5 + ct], 0),
                                                bounds_check=1023,
                                                oob_is_err=False),
                       reads=(f"{tag}toki",) + tuple(f"{tag}xh{j_}" for j_ in range(8)),
                       writes=(xek,), dma=True)
                for g in range(2):
                    pb, pbk = ch.psb[g], f"psb{g}"
                    for c4 in range(4):
                        c = g * 4 + c4
                        P.emit("tensor",
                               lambda E, c=c, c4=c4, pb=pb, xe=xe:
                                   E.transpose(pb[:, c4 * 128:(c4 + 1) * 128],
                                               xe[:, c * 128:(c + 1) * 128],
                                               ch.identb),
                               reads=(xek, "identb") if c4 in (0, 3) else (),
                               writes=(pbk,) if c4 in (0, 3) else (), inc=(c4 == 3))
                    xetv = ch.bf(BS_XET, 4096).rearrange("p (c t) -> p c t", t=512)
                    P.emit("vector",
                           lambda E, g=g, ct=ct, pb=pb, xetv=xetv:
                               E.tensor_copy(xetv[:, g * 4:(g + 1) * 4,
                                                 ct * 128:(ct + 1) * 128],
                                             pb.rearrange("p (c t) -> p c t",
                                                          t=128)[:, 0:4, :]),
                           reads=(pbk,), writes=(f"{tag}xet{ct}_{g}",))
            xet_keys = tuple(f"{tag}xet{c_}_{g_}" for c_ in range(4) for g_ in range(2))
            hsp = ch.bf(BS_H + par_e * 8192, 8192)
            for ft in range(16):
                bank = ch.psf[ft % 4]
                bk = f"psf{ft % 4}"
                pieces = [(wup[:, c * 2048 + ft * 128:c * 2048 + ft * 128 + 128],
                           ch.bf(BS_XET + c * 512, 512))
                          for c in range(8)]
                P.mm_group(bank[:, 0:512], pieces, (wupk,) + xet_keys, bk)
                hs_alias = (tuple(f"{tag}hs{8 * par_e + f2}_{e2}"
                                  for f2 in range(8) for e2 in range(2))
                            if e < 2 else ())
                P.emit("scalar",
                       lambda E, bank=bank, ft=ft, hsp=hsp:
                           E.activation(hsp[:, ft * 512:(ft + 1) * 512],
                                        bank[:, 0:512], AF.Silu),
                       reads=(bk,), writes=(f"{tag}he{par_e}_{ft}",) + hs_alias)
            he_keys = tuple(f"{tag}he{par_e}_{f_}" for f_ in range(16))
            scat_keys = []
            for ct in range(4):
                stg = ch.ff(BST_O + (ct % 2) * 1024, 1024)
                stgk = f"{tag}stg{ct % 2}"
                for eta in range(2):
                    bank = ch.psf[4 + eta]
                    bk = f"psf{4 + eta}"
                    pieces = [(hsp[:, c * 512 + ct * 128:c * 512 + ct * 128 + 128],
                               wdn[:, c * 1024 + eta * 512:c * 1024 + eta * 512 + 512])
                              for c in range(16)]
                    P.mm_group(bank[:, 0:512], pieces, (wdnk,) + he_keys, bk)
                    P.emit("vector",
                           lambda E, bank=bank, eta=eta, stg=stg, ct=ct, tcf=tcf:
                               E.tensor_scalar(stg[:, eta * 512:eta * 512 + 512],
                                               bank[:, 0:512],
                                               tcf[:, 2 * ct + 1:2 * ct + 2], None,
                                               op0=ALU.mult),
                           reads=(bk, f"{tag}cef"), writes=(stgk,))
                kk = f"{tag}eo{e}_{ct}"
                eov = eo_scratch.rearrange("(n p) d -> n p d", p=128)
                P.emit("gpsimd",
                       lambda E, stg=stg, ct=ct, e=e, eov=eov:
                           E.dma_start(eov[4 * e + ct], stg),
                       reads=(stgk,), writes=(kk,), dma=True)
                scat_keys.append(kk)
            prev_scatter = prev_scatter + tuple(scat_keys)

        eo_all = prev_scatter
        for j in range(8):
            tt = th * 8 + j
            par = j % 2
            obuf = ch.ff(BXN_O + par * 1024, 1024)
            obk = f"{tag}ob{par}"
            P.emit("sync",
                   lambda E, tt=tt, obuf=obuf: E.dma_start(obuf, moe_t[tt]),
                   reads=(f"{tag}moe{tt}",), writes=(obk,), dma=True)
            g1 = ch.outT[:, 1280:2304]
            g2 = ch.outT[:, 2304:3328]
            P.emit("vector", lambda E, g1=g1: E.memset(g1, 0.0),
                   writes=(tag + "g1",))
            P.emit("vector", lambda E, g2=g2: E.memset(g2, 0.0),
                   writes=(tag + "g2",))
            P.emit("vector",
                   lambda E, j=j: E.tensor_copy(ch.idxg[0], ch.ff(BS_GSA + 2 * j, 1)),
                   reads=(f"{tag}gs1",), writes=(tag + "gi1",))
            P.emit("vector",
                   lambda E, j=j: E.tensor_copy(ch.idxg[1], ch.ff(BS_GSA + 2 * j + 1, 1)),
                   reads=(f"{tag}gs2",), writes=(tag + "gi2",))
            P.emit("gpsimd",
                   lambda E, g1=g1:
                       E.indirect_dma_start(g1, None, eo_scratch,
                                            IndirectOffsetOnAxis(ch.idxg[0], 0),
                                            bounds_check=4095, oob_is_err=False),
                   reads=(tag + "gi1", tag + "g1") + eo_all,
                   writes=(tag + "g1",), dma=True)
            P.emit("gpsimd",
                   lambda E, g2=g2:
                       E.indirect_dma_start(g2, None, eo_scratch,
                                            IndirectOffsetOnAxis(ch.idxg[1], 0),
                                            bounds_check=4095, oob_is_err=False),
                   reads=(tag + "gi2", tag + "g2") + eo_all,
                   writes=(tag + "g2",), dma=True)
            P.emit("vector",
                   lambda E, obuf=obuf, g1=g1: E.tensor_add(obuf, obuf, g1),
                   reads=(obk, tag + "g1"), writes=(obk,))
            P.emit("vector",
                   lambda E, obuf=obuf, g2=g2: E.tensor_add(obuf, obuf, g2),
                   reads=(obk, tag + "g2"), writes=(obk,))
            sc = ch.ff(BSC_O + par * 1040, 1040)
            sck = f"{tag}fsc{par}"
            mean = ch.ff(BMN_O + 4 + par, 1)
            rstd = ch.ff(BMN_O + 6 + par, 1)
            skey = f"{tag}fst{par}"
            ln_stats(P, obuf, (obk,), mean, rstd, skey, sc, sck, eps)
            lnout = ch.ff(BXF_O, 1024)
            lnk = f"{tag}flno"
            P.emit("vector",
                   lambda E, obuf=obuf, mean=mean, rstd=rstd, lnout=lnout:
                       E.tensor_scalar(lnout, obuf, mean, rstd,
                                       op0=ALU.subtract, op1=ALU.mult),
                   reads=(obk, skey + "_m", skey + "_r"),
                   writes=(lnk, f"{tag}xhf"),)
            xbuf = ch.ff(BST_O, 1024)
            xk2 = f"{tag}fxr"
            P.emit("sync",
                   lambda E, tt=tt, xbuf=xbuf: E.dma_start(xbuf, x_in_t[tt]),
                   reads=(f"xres{tt}",), writes=(xk2, f"{tag}stg0"), dma=True)
            P.emit("vector",
                   lambda E, xbuf=xbuf, lnout=lnout: E.tensor_add(xbuf, xbuf, lnout),
                   reads=(xk2, lnk), writes=(xk2,))
            P.emit("sync",
                   lambda E, tt=tt, xbuf=xbuf: E.dma_start(x_out_t[tt], xbuf),
                   reads=(xk2,), writes=(f"xres{tt}",), dma=True)
        P.barrier()


def build_program_b2():
    nc = bass.Bass("TRN2", target_bir_lowering=False, debug=False)
    x_in = nc.dram_tensor("x_in", [T, D], F32, kind="ExternalInput").ap()
    x_out = nc.dram_tensor("x_out", [T, D], F32, kind="ExternalOutput").ap()
    moe = nc.dram_tensor("b_moe", [T, D], F32, kind="Internal").ap()
    xh = nc.dram_tensor("b_xh", [1024, 512], F32, kind="Internal").ap()
    toke = nc.dram_tensor("b_toke", [4096, 2], F32, kind="Internal").ap()
    eo = nc.dram_tensor("b_eo", [4096, 1024], F32, kind="Internal").ap()
    a = declare_dram_b(nc)
    a["c_ident"] = nc.dram_tensor("c_ident", [128, 128], F32,
                                  kind="ExternalInput").ap()
    P = PB(nc)
    ch = Chip(nc)
    ch.load_consts(P, a)
    build_stage_b2(P, ch, a, x_in, x_out, moe, xh, toke, eo)
    P.barrier()
    P.build()
    return nc


def run_stage_b2(x, inputs, cores=1):
    nc = build_program_b2()
    shared = host_prep_b(inputs)
    shared.update(_host_consts())
    in_maps = []
    for c in range(cores):
        m = dict(shared)
        m["x_in"] = np.ascontiguousarray(x[c])
        in_maps.append(m)
    res = run_bass_kernel_spmd(nc, in_maps, core_ids=list(range(cores)))
    return np.stack([r["x_out"] for r in res.results])


FUSED = True


def build_program_full():
    nc = bass.Bass("TRN2", target_bir_lowering=False, debug=False)
    x_in = nc.dram_tensor("x_in", [T, D], F32, kind="ExternalInput").ap()
    x_out = nc.dram_tensor("x_out", [T, D], F32, kind="ExternalOutput").ap()
    moe = nc.dram_tensor("b_moe", [T, D], F32, kind="Internal").ap()
    a = {}
    a.update(declare_dram_a(nc))
    a.update(declare_dram_b(nc))
    a.update(declare_dram_c(nc))
    a.update(declare_dram_d(nc))
    a["c_ident"] = nc.dram_tensor("c_ident", [128, 128], F32,
                                  kind="ExternalInput").ap()
    P = PB(nc)
    ch = Chip(nc)
    ch.load_consts(P, a)

    def preload_c():
        uvv = a["c_uvw"].rearrange("(c p) o -> p c o", p=128)
        bkeys = ("Bwup", "Bwdn") + tuple(f"Bh1_{f_}_{e_}" for f_ in range(16)
                                         for e_ in range(2))
        for c in range(8):
            P.emit("gpsimd",
                   lambda E, c=c, uvv=uvv:
                       E.dma_start(ch.bf(CVF_O + c * 2048, 2048), uvv[:, c, :]),
                   writes=(f"Cuvw{c}",) + (bkeys if c == 0 else ()), dma=True)

    def preload_d():
        for off, name in ((S1_O, "d_s1"), (S2_O, "d_s2"), (SH1_O, "d_sh1"),
                          (SH2_O, "d_sh2")):
            P.emit("gpsimd",
                   lambda E, off=off, name=name:
                       E.dma_start(ch.ff(off, 32), a[name]),
                   writes=("D" + name,), dma=True)
        P.emit("gpsimd", lambda E: E.dma_start(ch.ff(SW3_O, 8), a["d_sw3"]),
               writes=("Dd_sw3",), dma=True)

    build_stage_a(P, ch, a, x_in, x_out)
    build_stage_b(P, ch, a, x_out, x_out, moe, tail_preload=preload_c)
    build_stage_c(P, ch, a, x_out, x_out, skip_uvw=True, tail_preload=preload_d)
    build_stage_d(P, ch, a, x_out, x_out, skip_first=True)
    P.barrier()
    P.build()
    return nc


def kernel(**inputs):
    x = np.ascontiguousarray(np.asarray(inputs["L0_attn_x"], np.float32))
    consts = _host_consts()
    if FUSED:
        shared = {}
        for prep in (host_prep_a, host_prep_b, host_prep_c, host_prep_d):
            shared.update(prep(inputs))
        shared.update(consts)
        nc = build_program_full()
        in_maps = [dict(shared, x_in=np.ascontiguousarray(x[c]))
                   for c in range(N_CORES)]
        res = run_bass_kernel_spmd(nc, in_maps, core_ids=list(range(N_CORES)))
        return np.stack([r["x_out"] for r in res.results])
    cur = x
    for builder, prep in ((build_program_a, host_prep_a),
                          (build_program_b, host_prep_b),
                          (build_program_c, host_prep_c),
                          (build_program_d, host_prep_d)):
        nc = builder()
        shared = dict(prep(inputs))
        shared.update(consts)
        in_maps = [dict(shared, x_in=np.ascontiguousarray(cur[c]))
                   for c in range(N_CORES)]
        res = run_bass_kernel_spmd(nc, in_maps, core_ids=list(range(N_CORES)))
        cur = np.stack([r["x_out"] for r in res.results])
    return cur
